# version 04
# baseline (speedup 1.0000x reference)
import os
import numpy as np
import ml_dtypes
import concourse.bass as bass
import concourse.mybir as mybir
from concourse.bass_utils import run_bass_kernel_spmd

F32 = mybir.dt.float32
BF16 = mybir.dt.bfloat16
U8 = mybir.dt.uint8
AX = mybir.AxisListType.X
ALU = mybir.AluOpType
ACTF = mybir.ActivationFunctionType

T = 2048
D = 1024
NT = 16
ND = 8
NG = 4
NQ = 4
NK = 16
QS = 512
EPS = 1e-5
BSHIFT = 50.0

ENGINES = ("tensor", "scalar", "vector", "sync", "gpsimd")


class Prog:
    def __init__(self):
        self.counts = {}
        self.ops = {e: [] for e in ENGINES}

    def emit(self, eng, fn, waits=(), inc=None):
        if inc is not None:
            s, amt = inc
            self.counts[s] = self.counts.get(s, 0) + amt
        self.ops[eng].append((fn, [w for w in waits if w is not None], inc))

    def val(self, s):
        return self.counts.get(s, 0)


def _build_nc():
    nc = bass.Bass("TRN2", target_bir_lowering=False)

    x_h = nc.dram_tensor("x", [T, D], F32, kind="ExternalInput").ap()
    wqkvT_h = nc.dram_tensor("wqkvT", [D, 3 * D], F32, kind="ExternalInput").ap()
    bqkvT_h = nc.dram_tensor("bqkvT", [128, 16], F32, kind="ExternalInput").ap()
    bvTp_h = nc.dram_tensor("bvTp", [128, 8], F32, kind="ExternalInput").ap()
    woTp_h = nc.dram_tensor("woTp", [D, D], F32, kind="ExternalInput").ap()
    bo_h = nc.dram_tensor("bo", [1, D], F32, kind="ExternalInput").ap()
    win_h = nc.dram_tensor("win", [D, 64], F32, kind="ExternalInput").ap()
    binc_h = nc.dram_tensor("binc", [64, 1], F32, kind="ExternalInput").ap()
    w1blk_h = nc.dram_tensor("w1blk", [64, 1024], F32, kind="ExternalInput").ap()
    b1T_h = nc.dram_tensor("b1T", [128, 8], F32, kind="ExternalInput").ap()
    w2blk_h = nc.dram_tensor("w2blk", [1024, 64], F32, kind="ExternalInput").ap()
    b2c_h = nc.dram_tensor("b2c", [64, 1], F32, kind="ExternalInput").ap()
    woutA_h = nc.dram_tensor("woutA", [65, 1024], F32, kind="ExternalInput").ap()
    onesb_h = nc.dram_tensor("onesb", [128, 128], BF16, kind="ExternalInput").ap()
    identb_h = nc.dram_tensor("identb", [128, 128], BF16, kind="ExternalInput").ap()
    sel2_h = nc.dram_tensor("sel2", [128, 2], BF16, kind="ExternalInput").ap()
    out_h = nc.dram_tensor("out", [T, D], F32, kind="ExternalOutput").ap()

    x_tiles_h = x_h.rearrange("(i p) d -> i p d", p=128)
    out_tiles_h = out_h.rearrange("(i p) d -> i p d", p=128)
    wq_view3 = wqkvT_h.rearrange("d (r e) -> d r e", r=3)

    OFF = {}
    off = 0

    def alloc(name, nbytes, align=32):
        nonlocal off
        off = (off + align - 1) // align * align
        OFF[name] = off
        off += nbytes

    alloc("x", 16 * 1024 * 4)
    alloc("xlnT", 8 * 2048 * 2)
    alloc("heads", 4 * 2048 * 2)
    alloc("qkT", 2 * 4 * 2048 * 2)
    alloc("vs", 2 * 16 * 256 * 2)
    alloc("wqkv", 8 * 768 * 2)
    alloc("woT", 2 * 2 * 1024 * 2)
    alloc("pt", 3 * 1024 * 2)
    alloc("xlnsc", 2 * 1024 * 2)
    alloc("rd", 512 * 4)
    alloc("ffnw", 8 * 64 * 2)
    alloc("w1blk", 1024 * 2)
    alloc("w2blk", 8 * 64 * 2)
    alloc("woutA", 1024 * 2)
    alloc("onesb", 128 * 2)
    alloc("identb", 128 * 2)
    alloc("sel2", 2 * 2)
    alloc("bqkvT", 16 * 4)
    alloc("bq32", 8 * 4)
    alloc("bvbf", 8 * 2)
    alloc("ssum", 16 * 4)
    alloc("ssq", 16 * 4)
    alloc("sm", 16 * 4)
    alloc("sv", 16 * 4)
    alloc("sr", 16 * 4)
    alloc("smq", 16 * 4)
    alloc("sp1", 16 * 4)
    alloc("smr", 16 * 4)
    alloc("n16", 16 * 4)
    alloc("nqk", 4 * 4)
    alloc("biasg", 4 * 4)
    alloc("rowbuf", 1024 * 4)
    alloc("binc", 4)
    alloc("b2c", 4)
    alloc("b1T", 8 * 4)
    alloc("bnd", 4)
    alloc("bndb", 2)
    total_bytes = (off + 31) // 32 * 32

    arena_t = nc.alloc_sbuf_tensor("arena", [128, total_bytes], U8)
    arena = arena_t.ap()

    def view(name, shape, dtype, parts=128, p0=0):
        esz = 4 if dtype == F32 else 2
        n = 1
        for s in shape:
            n *= s
        a = arena[p0:p0 + parts, OFF[name]:OFF[name] + n * esz].bitcast(dtype)
        if len(shape) > 1:
            pat = " ".join(f"s{i}" for i in range(len(shape)))
            kw = {f"s{i}": shape[i] for i in range(len(shape))}
            a = a.rearrange(f"p ({pat}) -> p {pat}", **kw)
        return a

    x_s = view("x", [16, 1024], F32)
    xlnT = view("xlnT", [8, 2048], BF16)
    heads = view("heads", [4, 2048], BF16)
    qkTb = view("qkT", [2, 4, 2048], BF16)
    vsb = view("vs", [2, 16, 256], BF16)
    wqkv = view("wqkv", [8, 768], BF16)
    hT = view("wqkv", [8, 512], BF16)
    woTs = view("woT", [2, 2, 1024], BF16)
    pt3_ = view("pt", [3, 1024], BF16)
    ptx_ = arena[:, OFF["ffnw"]:OFF["ffnw"] + 2048].bitcast(BF16)

    def pt_s(j):
        return pt3_[:, j, :] if j < 3 else ptx_
    xlnsc = view("xlnsc", [2, 1024], BF16)
    sqb = view("xlnsc", [2048], BF16)
    rd = view("rd", [512], F32)
    win_s = view("ffnw", [8, 64], BF16)
    w1blk_s = view("w1blk", [1024], BF16, parts=64)
    w2blk_s = view("w2blk", [8, 64], BF16)
    woutA_s = view("woutA", [1024], BF16, parts=65)
    sT = view("vs", [2048], BF16, parts=64)
    gT_full = arena[0:65, OFF["vs"] + 4096:OFF["vs"] + 8192].bitcast(BF16)
    onesb = view("onesb", [128], BF16)
    identb = view("identb", [128], BF16)
    sel2 = view("sel2", [2], BF16)
    bqkvT = view("bqkvT", [16], F32)
    bq32 = view("bq32", [8], F32)
    bvbf = view("bvbf", [8], BF16)
    ssum = view("ssum", [16], F32)
    ssq = view("ssq", [16], F32)
    sm = view("sm", [16], F32)
    sv = view("sv", [16], F32)
    sr = view("sr", [16], F32)
    smq = view("smq", [16], F32)
    sp1 = view("sp1", [16], F32)
    smr = view("smr", [16], F32)
    n16 = view("n16", [16], F32)
    nqk = view("nqk", [4], F32)
    biasg = view("biasg", [4], F32)
    bo_s = view("rowbuf", [1024], F32, parts=1, p0=0)
    bo2b = arena[0:1, OFF["rd"]:OFF["rd"] + 2048].bitcast(BF16)
    binc_s = view("binc", [1], F32, parts=64)
    b2c_s = view("b2c", [1], F32, parts=64)
    b1T_s = view("b1T", [8], F32)
    bnd_s = view("bnd", [1], F32, parts=1)
    bndb_s = view("bndb", [1], BF16, parts=1)

    ps_a = nc.alloc_psum_tensor("ps_a", [128, 1024], F32).ap()
    ps_b = nc.alloc_psum_tensor("ps_b", [128, 1024], F32).ap()
    ps_c = nc.alloc_psum_tensor("ps_c", [128, 512], F32).ap()
    ps_d = nc.alloc_psum_tensor("ps_d", [128, 512], F32).ap()
    ps_e = nc.alloc_psum_tensor("ps_e", [128, 512], F32).ap()
    ps_f = nc.alloc_psum_tensor("ps_f", [128, 512], F32).ap()
    sc_slot = [ps_a, ps_b]
    ps_eb = ps_e.bitcast(BF16)
    ps_fb = ps_f.bitcast(BF16)

    p = Prog()

    for i in range(NT):
        p.emit("sync", lambda e, i=i: e.dma_start(x_s[:, i, :], x_tiles_h[i]),
               inc=("sx", 16))
    for dst, srch in ((bqkvT, bqkvT_h), (bo_s, bo_h), (binc_s, binc_h),
                      (b2c_s, b2c_h), (b1T_s, b1T_h), (onesb, onesb_h),
                      (identb, identb_h), (sel2, sel2_h)):
        p.emit("sync", lambda e, d=dst, s=srch: e.dma_start(d[:, :], s),
               inc=("swB", 16))
    swB_done = p.val("swB")
    p.emit("gpsimd", lambda e: e.dma_start(bvbf[:, :], bvTp_h), inc=("swA", 16))

    def emit_wqkv_dma(g, extra=()):
        for c in range(ND):
            src = wq_view3[128 * c:128 * c + 128, :, 256 * g:256 * g + 256]
            dst = wqkv[:, c, :].rearrange("p (r e) -> p r e", r=3)
            p.emit("gpsimd", lambda e, s=src, d=dst: e.dma_start(d, s),
                   waits=list(extra) if c == 0 else (), inc=("swq", 16))
        return p.val("swq")

    def emit_woT_dma(g, extra=()):
        for j, c in enumerate((2 * g, 2 * g + 1)):
            p.emit("gpsimd", lambda e, g=g, j=j, c=c: e.dma_start(
                woTs[:, g % 2, j, :], woTp_h[128 * c:128 * c + 128, :]),
                waits=list(extra) if j == 0 else (), inc=("swo", 16))
        return p.val("swo")

    wqkv_done = {0: emit_wqkv_dma(0)}
    p.emit("vector", lambda e: e.tensor_scalar_mul(bq32[:, :], bqkvT[:, 0:8], 1.0 / 32.0),
           waits=[("swB", swB_done)], inc=("smisc", 1))

    sq_psum = [ps_a, ps_b]

    def emit_ln(is_ln2, tile_waits, tr_banks=None, dma_tr=False):
        for i in range(NT):
            p.emit("scalar", lambda e, i=i: e.activation(
                sq_psum[i % 2][:, :], x_s[:, i, :], ACTF.Square,
                accum_out=ssq[:, i:i + 1]), waits=tile_waits(i), inc=("ssq1", 1))
        for i in range(NT):
            p.emit("vector", lambda e, i=i: e.reduce_sum(
                ssum[:, i:i + 1], x_s[:, i, :], axis=AX),
                waits=tile_waits(i), inc=("sln", 1))
        p.emit("vector", lambda e: e.tensor_scalar_mul(
            sm[:, :], ssum[:, :], 1.0 / 1024.0),
            waits=[("sln", p.val("sln"))], inc=("sln", 1))
        p.emit("vector", lambda e: e.tensor_scalar(
            sv[:, :], ssq[:, :], 1.0 / 1024.0, EPS, ALU.mult, ALU.add),
            waits=[("ssq1", p.val("ssq1"))], inc=("sln", 1))
        p.emit("vector", lambda e: e.tensor_mul(smq[:, :], sm[:, :], sm[:, :]),
               waits=[("sln", p.val("sln"))], inc=("sln", 1))
        p.emit("vector", lambda e: e.tensor_sub(sv[:, :], sv[:, :], smq[:, :]),
               waits=[("sln", p.val("sln"))], inc=("sln", 1))
        p.emit("scalar", lambda e: e.activation(sv[:, :], sv[:, :], ACTF.Ln),
               waits=[("sln", p.val("sln"))], inc=("ssq2", 1))
        p.emit("scalar", lambda e: e.activation(sr[:, :], sv[:, :], ACTF.Exp, scale=-0.5),
               waits=[("ssq2", p.val("ssq2"))], inc=("ssq2", 1))
        p.emit("vector", lambda e: e.engine_nop(),
               waits=[("ssq2", p.val("ssq2"))], inc=("sln", 1))
        if not is_ln2:
            p.emit("vector", lambda e: e.tensor_scalar(
                sp1[:, :], sr[:, :], 1.0, None, ALU.add),
                waits=[("sln", p.val("sln"))], inc=("sln", 1))
            p.emit("vector", lambda e: e.tensor_mul(smr[:, :], sm[:, :], sr[:, :]),
                   waits=[("sln", p.val("sln"))], inc=("sln", 1))
        stats_done = p.val("sln")
        tr_after = {}
        tr_sem = "strd" if dma_tr else "str"
        for i in range(NT):
            aw = [("sln", stats_done)]
            if i >= 2:
                aw.append((tr_sem, tr_after[i - 2]))
            p.emit("vector", lambda e, i=i: e.tensor_scalar(
                xlnsc[:, i % 2, :], x_s[:, i, :], sm[:, i:i + 1], sr[:, i:i + 1],
                ALU.subtract, ALU.mult), waits=aw, inc=("sln", 1))
            lnsc_done = p.val("sln")
            if not is_ln2:
                p.emit("vector", lambda e, i=i: e.tensor_scalar(
                    x_s[:, i, :], x_s[:, i, :], sp1[:, i:i + 1], smr[:, i:i + 1],
                    ALU.mult, ALU.subtract), inc=("sln", 1))
            if dma_tr:
                for c in range(ND):
                    p.emit("scalar", lambda e, i=i, c=c: e.dma_start(
                        xlnT[:, c, 128 * i:128 * (i + 1)],
                        xlnsc[:, i % 2, 128 * c:128 * (c + 1)], transpose=True),
                        waits=[("sln", lnsc_done)] if c == 0 else (),
                        inc=("strd", 16))
                tr_after[i] = p.val("strd")
                rec = ("strd", p.val("strd"))
            else:
                nb = len(tr_banks)
                pu = tr_banks[i % nb]
                trw = [("sln", lnsc_done)]
                if i >= nb:
                    trw.append(("strev", p.val("strev") - (nb - 1)))
                for c in range(ND):
                    p.emit("tensor", lambda e, i=i, c=c, pu=pu: e.transpose(
                        pu[:, 128 * c:128 * (c + 1)],
                        xlnsc[:, i % 2, 128 * c:128 * (c + 1)],
                        identb[:, :]), waits=trw if c == 0 else (), inc=("str", 1))
                tr_after[i] = p.val("str")
                p.emit("vector", lambda e, i=i, pu=pu: e.tensor_copy(
                    xlnT[:, :, 128 * i:128 * (i + 1)],
                    pu[:, 0:1024].rearrange("p (c t) -> p c t", c=8)),
                    waits=[("str", p.val("str"))], inc=("strev", 1))
                rec = ("strev", p.val("strev"))
            tr_after.setdefault(i, None)
            if not dma_tr:
                tr_after[i] = p.val("str")
            yield_rec = rec
            tile_recs.append(yield_rec)

    tile_recs = []
    ps_ab = ps_a.bitcast(BF16)
    ps_bb = ps_b.bitcast(BF16)
    tile_recs.clear()
    emit_ln(False, lambda i: [("sx", 16 * (i + 1)), ("swB", swB_done)],
            tr_banks=[ps_ab, ps_bb])
    ln1_tr = {i: tile_recs[i][1] for i in range(NT)}
    ln1_strev = p.val("strev")

    vdone = {}

    def qkv_chunks(g, banks=None):
        banks = banks or [ps_f]
        nb = len(banks)
        qkT = qkTb[:, g % 2]
        vs = vsb[:, g % 2]

        def mk_qk(d, ts_):
            def emit():
                u = banks[p.val("suu") % nb]
                uw = [("sue", max(0, p.val("sue") - (nb - 1))),
                      ("swq", wqkv_done[g]),
                      ("strev", ln1_tr[4 * ts_ + 3])]
                r = 0 if d < 2 else 1
                sub = d % 2
                for c in range(ND):
                    lhsT = wqkv[:, c, :].rearrange(
                        "p (r e) -> p r e", r=3)[:, r, 128 * sub:128 * (sub + 1)]
                    rhs = xlnT[:, c, QS * ts_:QS * (ts_ + 1)]
                    p.emit("tensor", lambda e, l=lhsT, r_=rhs, c=c, u=u: e.matmul(
                        u[:, :], l, r_, start=(c == 0), stop=(c == ND - 1)),
                        waits=uw if c == 0 else (),
                        inc=("suu", 1) if c == ND - 1 else None)
                mmd = p.val("suu")
                if d < 2:
                    p.emit("vector", lambda e, d=d, t=ts_, u=u: e.tensor_scalar(
                        qkT[:, d, QS * t:QS * (t + 1)], u[:, :], 1.0 / 32.0,
                        bq32[:, 2 * g + d:2 * g + d + 1], ALU.mult, ALU.add),
                        waits=[("suu", mmd)], inc=("sue", 1))
                else:
                    p.emit("vector", lambda e, d=d, t=ts_, u=u: e.tensor_scalar(
                        qkT[:, d, QS * t:QS * (t + 1)], u[:, :], 1.0,
                        bqkvT[:, 8 + 2 * g + (d - 2):8 + 2 * g + (d - 2) + 1],
                        ALU.mult, ALU.add),
                        waits=[("suu", mmd)], inc=("sue", 1))
            return emit

        def mk_v(i):
            def emit():
                u = banks[p.val("suu") % nb]
                uw = [("sue", max(0, p.val("sue") - (nb - 1))),
                      ("swq", wqkv_done[g]),
                      ("strev", ln1_tr[i])]
                for c in range(ND):
                    lhsT = xlnT[:, c, 128 * i:128 * (i + 1)]
                    rhs = wqkv[:, c, :].rearrange("p (r e) -> p r e", r=3)[:, 2, :]
                    p.emit("tensor", lambda e, l=lhsT, r_=rhs, c=c, u=u: e.matmul(
                        u[:, 0:256], l, r_, start=(c == 0), stop=(c == ND - 1)),
                        waits=uw if c == 0 else (),
                        inc=("suu", 1) if c == ND - 1 else None)
                mmd = p.val("suu")
                p.emit("vector", lambda e, i=i, u=u: e.tensor_copy(
                    vs[:, i, :], u[:, 0:256]),
                    waits=[("suu", mmd)], inc=("sue", 1))
                vdone[(g, i)] = p.val("sue")
            return emit

        qk = [mk_qk(d, ts_) for d in range(4) for ts_ in range(NQ)]
        vv = [mk_v(i) for i in range(NT)]
        return qk, vv

    def bound_chunks(g, banks=None):
        banks = banks or [ps_f]
        nb = len(banks)
        qkT = qkTb[:, g % 2]
        chunks = []

        def mk_sq(ci):
            def emit():
                p.emit("gpsimd", lambda e, ci=ci: e.tensor_tensor(
                    sqb[:, :], qkT[:, ci, :], qkT[:, ci, :], ALU.mult),
                    waits=[("sue", p.val("sue")), ("sgq", p.val("sgq")),
                           ("sln", p.val("sln"))],
                    inc=("sgq", 1))
            return emit

        def mk_mm(ci, j):
            def emit():
                u = banks[p.val("suu") % nb]
                w = [("sgq", p.val("sgq")),
                     ("sue", max(0, p.val("sue") - (nb - 1))),
                     ("sbv", p.val("sbv"))]
                p.emit("tensor", lambda e, j=j, u=u: e.matmul(
                    u[0:2, :], sel2[:, :], sqb[:, 512 * j:512 * (j + 1)],
                    start=True, stop=True), waits=w, inc=("suu", 1))
                p.emit("vector", lambda e, ci=ci, j=j, u=u: e.reduce_max(
                    n16[0:2, 4 * ci + j:4 * ci + j + 1], u[0:2, :], axis=AX),
                    waits=[("suu", p.val("suu"))], inc=("sbv", 1))
                p.emit("vector", lambda e: e.engine_nop(),
                       waits=[("sbv", p.val("sbv"))], inc=("sue", 1))
            return emit

        def mk_fin():
            def emit():
                p.emit("vector", lambda e: e.reduce_max(
                    nqk[0:2, 0:1], n16[0:2, 0:8], axis=AX),
                    waits=[("sbv", p.val("sbv"))], inc=("sbv", 1))
                p.emit("vector", lambda e: e.reduce_max(
                    nqk[0:2, 1:2], n16[0:2, 8:16], axis=AX),
                    waits=[("sbv", p.val("sbv"))], inc=("sbv", 1))
                shmask = [1, 0] + list(range(2, 32))
                p.emit("vector", lambda e, m=shmask: e.stream_shuffle(
                    nqk[0:2, 2:4], nqk[0:2, 0:2], m),
                    waits=[("sbv", p.val("sbv"))], inc=("sbv", 1))
                p.emit("vector", lambda e: e.tensor_max(
                    nqk[0:1, 0:2], nqk[0:1, 0:2], nqk[0:1, 2:4]),
                    waits=[("sbv", p.val("sbv"))], inc=("sbv", 1))
                p.emit("vector", lambda e: e.tensor_mul(
                    bnd_s[:, :], nqk[0:1, 0:1], nqk[0:1, 1:2]),
                    waits=[("sbv", p.val("sbv"))], inc=("sbv", 1))
                p.emit("scalar", lambda e: e.activation(
                    bnd_s[:, :], bnd_s[:, :], ACTF.Ln),
                    waits=[("sbv", p.val("sbv"))], inc=("sbs", 1))
                p.emit("scalar", lambda e: e.activation(
                    bnd_s[:, :], bnd_s[:, :], ACTF.Exp, scale=0.5),
                    waits=[("sbs", p.val("sbs"))], inc=("sbs", 1))
                p.emit("vector", lambda e: e.tensor_scalar(
                    bnd_s[:, :], bnd_s[:, :], -1.0, BSHIFT, ALU.mult, ALU.add),
                    waits=[("sbs", p.val("sbs"))], inc=("sbv", 1))
                p.emit("vector", lambda e: e.tensor_copy(bndb_s[:, :], bnd_s[:, :]),
                       waits=[("sbv", p.val("sbv"))], inc=("sbv", 1))
                u = banks[p.val("suu") % nb]
                uw = [("sbv", p.val("sbv")),
                      ("sue", max(0, p.val("sue") - (nb - 1)))]
                p.emit("tensor", lambda e, u=u: e.matmul(
                    u[:, 0:1], onesb[0:1, :], bndb_s[:, :],
                    start=True, stop=True), waits=uw, inc=("suu", 1))
                p.emit("vector", lambda e, g=g, u=u: e.tensor_copy(
                    biasg[:, g:g + 1], u[:, 0:1]),
                    waits=[("suu", p.val("suu"))], inc=("sue", 1))
                p.emit("vector", lambda e: e.engine_nop(),
                       waits=[("sue", p.val("sue"))], inc=("sbnd", 1))
            return emit

        for ci in range(4):
            chunks.append(mk_sq(ci))
            for j in range(4):
                chunks.append(mk_mm(ci, j))
        chunks.append(mk_fin())
        return chunks

    nrm_done = {}
    opadd = {}
    woT_done = {}

    def partial_chunks(g, with_bias, banks=None):
        banks = banks or [ps_f]
        nb = len(banks)
        chunks = []
        hslots = ((2 * g) % 4, (2 * g + 1) % 4)

        def mk(t, half):
            def emit():
                u = banks[p.val("suu") % nb]
                uw = [("sue", max(0, p.val("sue") - (nb - 1))),
                      ("swo", woT_done[g]),
                      ("snrm", nrm_done[g * NQ + NQ - 1])]
                for j in range(2):
                    p.emit("tensor", lambda e, j=j, t=t, h=half, u=u: e.matmul(
                        u[:, :], heads[:, hslots[j], 128 * t:128 * (t + 1)],
                        woTs[:, g % 2, j, 512 * h:512 * (h + 1)],
                        start=(j == 0), stop=(j == 1 and not with_bias)),
                        waits=uw if j == 0 else (),
                        inc=("suu", 1) if (j == 1 and not with_bias) else None)
                if with_bias:
                    p.emit("tensor", lambda e, t=t, h=half, u=u: e.matmul(
                        u[:, :], onesb[0:1, :], bo2b[:, 512 * h:512 * (h + 1)],
                        start=False, stop=True),
                        waits=[("sbob", 1)], inc=("suu", 1))
                mmd = p.val("suu")
                p.emit("vector", lambda e, t=t, h=half, u=u: e.tensor_add(
                    x_s[:, t, 512 * h:512 * (h + 1)],
                    x_s[:, t, 512 * h:512 * (h + 1)], u[:, :]),
                    waits=[("suu", mmd)], inc=("sue", 1))
                opadd[(g, t, half)] = p.val("sue")
            return emit

        def mk_bv():
            def emit():
                for half in range(2):
                    u = banks[p.val("suu") % nb]
                    uw = [("sue", max(0, p.val("sue") - (nb - 1))),
                          ("swo", woT_done[g]),
                          ("snrm", nrm_done[g * NQ + NQ - 1])]
                    for j in range(2):
                        p.emit("tensor", lambda e, j=j, h=half, u=u: e.matmul(
                            u[0:1, :], bvbf[:, 2 * g + j:2 * g + j + 1],
                            woTs[:, g % 2, j, 512 * h:512 * (h + 1)],
                            start=(j == 0), stop=(j == 1)),
                            waits=uw if j == 0 else (),
                            inc=("suu", 1) if j == 1 else None)
                    p.emit("vector", lambda e, h=half, u=u: e.tensor_add(
                        bo_s[:, 512 * h:512 * (h + 1)],
                        bo_s[:, 512 * h:512 * (h + 1)], u[0:1, :]),
                        waits=[("suu", p.val("suu")), ("sbvv", p.val("sbvv")),
                               ("swB", swB_done)],
                        inc=("sue", 1))
                    p.emit("vector", lambda e: e.engine_nop(),
                           waits=[("sue", p.val("sue"))], inc=("sbvv", 1))
            return emit

        chunks.append(mk_bv())
        for t in range(NT):
            for half in range(2):
                chunks.append(mk(t, half))
        return chunks

    exp_done = {}
    pv_done = {}

    def emit_attn_group(g, side):
        qkT = qkTb[:, g % 2]
        vs = vsb[:, g % 2]
        base_pti = g * NQ * NK * 2
        si = 0
        total_iters = NQ * NK

        def emit_scores_exp(q, k):
            for pr in range(2):
                pti = base_pti + (q * NK + k) * 2 + pr
                slot = pti % 2
                pslot = pti % 4
                w = [("sbnd", g + 1)]
                if pti >= 2:
                    w.append(("sexp", exp_done[pti - 2]))
                if pti == 0:
                    w.append(("ssq1", 16))
                p.emit("tensor", lambda e, s=slot, pr=pr, k=k, q=q, qkT=qkT: e.matmul(
                    sc_slot[s][:, 0:512],
                    qkT[0:64, 2 + pr, 128 * k:128 * (k + 1)],
                    qkT[0:64, pr, QS * q:QS * (q + 1)],
                    start=True, stop=True, tile_position=(0, 0)),
                    waits=w, inc=None)
                p.emit("tensor", lambda e, s=slot, pr=pr, k=k, q=q, qkT=qkT: e.matmul(
                    sc_slot[s][:, 512:1024],
                    qkT[64:128, 2 + pr, 128 * k:128 * (k + 1)],
                    qkT[64:128, pr, QS * q:QS * (q + 1)],
                    start=True, stop=True, tile_position=(64, 0)),
                    inc=("ssc", 1))
                scd = p.val("ssc")
                ew = [("ssc", scd)]
                if pti >= 4:
                    ew.append(("spv", pv_done[pti - 4]))
                p.emit("scalar", lambda e, s=slot, ps_=pslot, g=g: e.activation(
                    pt_s(ps_), sc_slot[s][:, :], ACTF.Exp,
                    bias=biasg[:, g:g + 1], scale=1.0),
                    waits=ew, inc=("sexp", 1))
                exp_done[pti] = p.val("sexp")

        def emit_v_dn(q, k):
            gq = g * NQ + q
            vw = [("sexp", exp_done[base_pti + (q * NK + k) * 2 + 1]),
                  ("sue", vdone[(g, k)])]
            if k == 0 and gq > 0:
                vw.append(("snrm", nrm_done[gq - 1]))
            if k == 0 and gq == 0:
                vw.append(("sue", head_sue))
            sA = (base_pti + (q * NK + k) * 2) % 4
            sB = (sA + 1) % 4
            for half, po in ((0, ps_c), (1, ps_d)):
                for a in range(4):
                    ps_slot = sA if a < 2 else sB
                    p.emit("tensor", lambda e, po=po, a=a, k=k, half=half, s=ps_slot, vs=vs: e.matmul(
                        po[32 * a:32 * (a + 1), :],
                        vs[:, k, 128 * half + 32 * a:128 * half + 32 * (a + 1)],
                        pt_s(s)[:, 512 * (a % 2):512 * (a % 2) + 512],
                        start=(k == 0), stop=(k == NK - 1),
                        tile_position=(0, 32 * a)),
                        waits=vw if (half == 0 and a == 0) else (), inc=None)
            for a in range(4):
                ps_slot = sA if a < 2 else sB
                p.emit("tensor", lambda e, a=a, k=k, s=ps_slot: e.matmul(
                    ps_e[32 * a:32 * (a + 1), :],
                    onesb[:, 0:32],
                    pt_s(s)[:, 512 * (a % 2):512 * (a % 2) + 512],
                    start=(k == 0), stop=(k == NK - 1),
                    tile_position=(0, 32 * a)),
                    inc=("spv", 1) if a == 3 else None)
            pv_done[base_pti + (q * NK + k) * 2] = p.val("spv")
            pv_done[base_pti + (q * NK + k) * 2 + 1] = p.val("spv")

        def emit_norm(q):
            gq = g * NQ + q
            lastpv = pv_done[base_pti + (q * NK + NK - 1) * 2]
            p.emit("vector", lambda e: e.reciprocal(rd[:, :], ps_e[:, :]),
                   waits=[("spv", lastpv)], inc=("snv", 1))
            p.emit("vector", lambda e, g=g, q=q: e.tensor_mul(
                heads[:, (2 * g) % 4, QS * q:QS * (q + 1)], ps_c[:, :], rd[:, :]),
                waits=[("snv", p.val("snv"))], inc=("snv", 1))
            p.emit("vector", lambda e, g=g, q=q: e.tensor_mul(
                heads[:, (2 * g + 1) % 4, QS * q:QS * (q + 1)], ps_d[:, :], rd[:, :]),
                inc=("snrm", 1))
            nrm_done[gq] = p.val("snrm")

        steps = [(q, k) for q in range(NQ) for k in range(NK)]
        for idx in range(total_iters + 1):
            if idx < total_iters:
                emit_scores_exp(*steps[idx])
            if idx >= 1:
                qp, kp = steps[idx - 1]
                emit_v_dn(qp, kp)
                if kp == NK - 1:
                    emit_norm(qp)
            target = ((idx + 1) * len(side) + total_iters) // (total_iters + 1)
            while si < min(len(side), target):
                side[si]()
                si += 1
        while si < len(side):
            side[si]()
            si += 1

    HEAD2B = bool(int(os.environ.get("MK_HEAD2B", "1")))
    hqk, hv = qkv_chunks(0, banks=[ps_f, ps_e] if HEAD2B else [ps_f])
    for ch in hqk:
        ch()
    for ch in bound_chunks(0, banks=[ps_f, ps_e] if HEAD2B else [ps_f]):
        ch()
    for ch in hv:
        ch()
    head_sue = p.val("sue")

    for g in range(NG):
        side = []
        if g + 1 < NG:
            wqkv_done[g + 1] = emit_wqkv_dma(g + 1, extra=[("suu", p.val("suu"))])
        woT_done[g] = emit_woT_dma(
            g, extra=[("sue", opadd.get((g - 2, NT - 1, 1), 0))])
        if g >= 1:
            side += partial_chunks(g - 1, with_bias=False)
        if g + 1 < NG:
            qk, vv = qkv_chunks(g + 1)
            side += qk
            side += bound_chunks(g + 1)
            side += vv
        emit_attn_group(g, side)
    attn_done = p.val("snrm")

    p.emit("gpsimd", lambda e: e.dma_start(
        win_s[:, :, :], win_h.rearrange("(c p) m -> p c m", p=128)),
        waits=[("spv", p.val("spv"))], inc=("swA", 16))
    p.emit("gpsimd", lambda e: e.dma_start(w1blk_s[:, :], w1blk_h), inc=("swA", 16))
    p.emit("gpsimd", lambda e: e.dma_start(
        w2blk_s[:, :, :], w2blk_h.rearrange("(c p) m -> p c m", p=128)), inc=("swA", 16))
    p.emit("gpsimd", lambda e: e.dma_start(woutA_s[:, :], woutA_h), inc=("swA", 16))
    swA_ffn = p.val("swA")

    TAIL2B = bool(int(os.environ.get("MK_TAIL2B", "1")))
    tail_partial = partial_chunks(3, with_bias=True,
                                  banks=[ps_f, ps_e] if TAIL2B else [ps_f])
    tail_partial[0]()
    p.emit("vector", lambda e: e.tensor_copy(bo2b[:, :], bo_s[:, :]),
           waits=[("sbvv", p.val("sbvv"))], inc=("sbob", 1))
    for ch in tail_partial[1:]:
        ch()
    op_all = p.val("sue")

    def ln2_waits(i):
        return [("sue", opadd[(3, i, 1)])]

    LN2_DMA = bool(int(os.environ.get("MK_LN2DMA", "1")))
    tile_recs.clear()
    if LN2_DMA:
        emit_ln(True, ln2_waits, dma_tr=True)
        ln2_tr = {i: ("strd", tile_recs[i][1]) for i in range(NT)}
    else:
        emit_ln(True, ln2_waits, tr_banks=[ps_c.bitcast(BF16), ps_d.bitcast(BF16)])
        ln2_tr = {i: ("strev", p.val("strev")) for i in range(NT)}

    p.emit("vector", lambda e: e.memset(gT_full[64:65, :], 1.0),
           waits=[("snrm", attn_done)], inc=("sgone", 1))

    for s in range(NQ):
        sl = slice(QS * s, QS * (s + 1))
        for c in range(ND):
            p.emit("tensor", lambda e, c=c, sl=sl: e.matmul(
                ps_c[0:64, :], win_s[:, c, :], xlnT[:, c, sl],
                start=(c == 0), stop=(c == ND - 1)),
                waits=[ln2_tr[4 * s + 3], ("swA", swA_ffn),
                       ("snv", p.val("snv"))] if c == 0 else (),
                inc=("sff", 1) if c == ND - 1 else None)
        p.emit("vector", lambda e, sl=sl: e.tensor_scalar(
            sT[:, sl], ps_c[0:64, :], 1.0, binc_s[:, :], ALU.mult, ALU.add),
            waits=[("sff", p.val("sff"))], inc=("sfe", 1))
        st_done = p.val("sfe")
        for c in range(ND):
            ph = [ps_a, ps_b][c % 2]
            phv = ph[:, 512 * ((c // 2) % 2):512 * ((c // 2) % 2) + 512]
            w = [("sfe", st_done)]
            if c >= 4:
                w.append(("sfe", p.val("sfe") - 3))
            p.emit("tensor", lambda e, c=c, sl=sl, phv=phv: e.matmul(
                phv, w1blk_s[:, 128 * c:128 * (c + 1)], sT[:, sl],
                start=True, stop=True), waits=w, inc=("sff", 1))
            p.emit("vector", lambda e, c=c, phv=phv: e.tensor_scalar(
                hT[:, c, :], phv, b1T_s[:, c:c + 1], 0.0, ALU.add, ALU.max),
                waits=[("sff", p.val("sff"))], inc=("sfe", 1))
        for c in range(ND):
            p.emit("tensor", lambda e, c=c: e.matmul(
                ps_d[0:64, :], w2blk_s[:, c, :], hT[:, c, :],
                start=(c == 0), stop=(c == ND - 1)),
                waits=[("sfe", p.val("sfe"))] if c == 0 else (),
                inc=("sff", 1) if c == ND - 1 else None)
        p.emit("vector", lambda e, sl=sl: e.tensor_scalar(
            gT_full[0:64, sl], ps_d[0:64, :], 1.0, b2c_s[:, :], ALU.mult, ALU.add),
            waits=[("sff", p.val("sff"))], inc=("sfe", 1))
        gt_done = p.val("sfe")
        for tt in range(4):
            i = 4 * s + tt
            for half in range(2):
                u = [ps_e, ps_f][(2 * i + half) % 2]
                w = [("sfe", gt_done), ("sgone", 1), ("sue", op_all)]
                if p.val("sfev2") >= 1:
                    w.append(("sfev2", p.val("sfev2") - 1))
                p.emit("tensor", lambda e, i=i, h=half, u=u: e.matmul(
                    u[:, :], gT_full[:, 128 * i:128 * (i + 1)],
                    woutA_s[:, 512 * h:512 * (h + 1)],
                    start=True, stop=True), waits=w, inc=("sff", 1))
                p.emit("vector", lambda e, i=i, h=half, u=u: e.tensor_add(
                    x_s[:, i, 512 * h:512 * (h + 1)],
                    x_s[:, i, 512 * h:512 * (h + 1)], u[:, :]),
                    waits=[("sff", p.val("sff"))], inc=("sfev2", 1))
            p.emit("sync", lambda e, i=i: e.dma_start(out_tiles_h[i], x_s[:, i, :]),
                   waits=[("sfev2", p.val("sfev2"))], inc=("sout", 16))

    p.emit("sync", lambda e: None, waits=[("sout", 16 * NT)])

    semh = {}
    from contextlib import ExitStack
    ctx = ExitStack()
    for name in p.counts:
        semh[name] = ctx.enter_context(nc.semaphore(name))
    block = ctx.enter_context(nc.Block())
    with ctx:
        def runner(eng_obj, name):
            for fn, waits, inc in p.ops[name]:
                for s, v in waits:
                    if v > 0:
                        eng_obj.wait_ge(semh[s], v)
                ins = fn(eng_obj)
                if inc is not None:
                    s, amt = inc
                    ins.then_inc(semh[s], amt)

        @block.tensor
        def _(t):
            runner(t, "tensor")

        @block.scalar
        def _(s):
            runner(s, "scalar")

        @block.vector
        def _(v):
            runner(v, "vector")

        @block.sync
        def _(s):
            runner(s, "sync")

        @block.gpsimd
        def _(g):
            runner(g, "gpsimd")

    return nc


_NC_CACHE = {}


def _get_nc():
    key = "v1"
    if key not in _NC_CACHE:
        _NC_CACHE[key] = _build_nc()
    return _NC_CACHE[key]


def _prep_inputs(inputs):
    f32 = np.float32
    x = np.ascontiguousarray(np.asarray(inputs["L0_attn_x"], dtype=f32))
    w_qkv = np.asarray(inputs["L0_attn_w_qkv"], dtype=f32)
    b_qkv = np.asarray(inputs["L0_attn_b_qkv"], dtype=f32)
    w_o = np.asarray(inputs["L0_attn_w_o"], dtype=f32)
    b_o = np.asarray(inputs["L0_attn_b_o"], dtype=f32)
    Win = np.asarray(inputs["L0_ffn_Win"], dtype=f32)
    bin_ = np.asarray(inputs["L0_ffn_bin_"], dtype=f32)
    W1 = np.asarray(inputs["L0_ffn_W1"], dtype=f32)
    b1 = np.asarray(inputs["L0_ffn_b1"], dtype=f32)
    W2 = np.asarray(inputs["L0_ffn_W2"], dtype=f32)
    b2 = np.asarray(inputs["L0_ffn_b2"], dtype=f32)
    Wout = np.asarray(inputs["L0_ffn_Wout"], dtype=f32)
    bout = np.asarray(inputs["L0_ffn_bout"], dtype=f32)

    d_idx = np.arange(1024)
    h = d_idx // 64
    c = d_idx % 64
    newpos = (h // 4) * 256 + (c // 32) * 128 + (h % 4) * 32 + (c % 32)
    src = np.empty(1024, dtype=np.int64)
    src[newpos] = d_idx

    wqkvT = np.ascontiguousarray(w_qkv.T)
    wqkvT_p = wqkvT.copy()
    wqkvT_p[:, 2048:3072] = wqkvT[:, 2048 + src]
    bqkvT = np.ascontiguousarray(b_qkv[:2048].reshape(16, 128).T)
    bvTp = np.ascontiguousarray(b_qkv[2048:][src].reshape(8, 128).T)
    woTp = np.ascontiguousarray(w_o.T[src, :])
    bo = np.ascontiguousarray(b_o.reshape(1, 1024))

    w1blk = np.zeros((64, 1024), dtype=f32)
    mm = np.arange(64)
    for j in range(16):
        w1blk[mm, 16 * mm + j] = W1[:, j]
    b1T = np.ascontiguousarray(b1.reshape(1024).reshape(8, 128).T)
    w2blk = np.zeros((1024, 64), dtype=f32)
    for j in range(16):
        w2blk[16 * mm + j, mm] = W2[:, j]
    woutA = np.ascontiguousarray(np.vstack([Wout, bout.reshape(1, 1024)]))
    binc = np.ascontiguousarray(bin_.reshape(64, 1))
    b2c = np.ascontiguousarray(b2.reshape(64, 1))

    onesb = np.ones((128, 128), dtype=ml_dtypes.bfloat16)
    identb = np.eye(128, dtype=ml_dtypes.bfloat16)
    sel2 = np.zeros((128, 2), dtype=ml_dtypes.bfloat16)
    sel2[0:64, 0] = 1.0
    sel2[64:128, 1] = 1.0

    shared = dict(wqkvT=wqkvT_p, bqkvT=bqkvT, bvTp=bvTp, woTp=woTp, bo=bo,
                  win=np.ascontiguousarray(Win), binc=binc, w1blk=w1blk,
                  b1T=b1T, w2blk=w2blk, b2c=b2c, woutA=woutA,
                  onesb=onesb, identb=identb, sel2=sel2)
    in_maps = []
    for cidx in range(8):
        m = dict(shared)
        m["x"] = np.ascontiguousarray(x[cidx])
        in_maps.append(m)
    return in_maps


def kernel(**inputs):
    in_maps = _prep_inputs(inputs)
    nc = _get_nc()
    res = run_bass_kernel_spmd(nc, in_maps, core_ids=list(range(8)))
    out = np.stack([np.asarray(res.results[cidx]["out"]) for cidx in range(8)])
    return out.astype(np.float32)


# version 05
# speedup vs baseline: 1.0441x; 1.0441x over previous
import os
import numpy as np
import ml_dtypes
import concourse.bass as bass
import concourse.mybir as mybir
from concourse.bass_utils import run_bass_kernel_spmd

F32 = mybir.dt.float32
BF16 = mybir.dt.bfloat16
U8 = mybir.dt.uint8
AX = mybir.AxisListType.X
ALU = mybir.AluOpType
ACTF = mybir.ActivationFunctionType

T = 2048
D = 1024
NT = 16
ND = 8
NG = 4
NQ = 4
NK = 16
QS = 512
EPS = 1e-5
BSHIFT = 50.0

ENGINES = ("tensor", "scalar", "vector", "sync", "gpsimd")


class Prog:
    def __init__(self):
        self.counts = {}
        self.ops = {e: [] for e in ENGINES}

    def emit(self, eng, fn, waits=(), inc=None):
        if inc is not None:
            s, amt = inc
            self.counts[s] = self.counts.get(s, 0) + amt
        self.ops[eng].append((fn, [w for w in waits if w is not None], inc))

    def val(self, s):
        return self.counts.get(s, 0)


def _build_nc():
    nc = bass.Bass("TRN2", target_bir_lowering=False)

    x_h = nc.dram_tensor("x", [T, D], F32, kind="ExternalInput").ap()
    wqkvT_h = nc.dram_tensor("wqkvT", [D, 3 * D], F32, kind="ExternalInput").ap()
    bqkvT_h = nc.dram_tensor("bqkvT", [128, 16], F32, kind="ExternalInput").ap()
    bvTp_h = nc.dram_tensor("bvTp", [128, 8], F32, kind="ExternalInput").ap()
    woTp_h = nc.dram_tensor("woTp", [D, D], F32, kind="ExternalInput").ap()
    bo_h = nc.dram_tensor("bo", [1, D], F32, kind="ExternalInput").ap()
    win_h = nc.dram_tensor("win", [D, 64], F32, kind="ExternalInput").ap()
    binc_h = nc.dram_tensor("binc", [64, 1], F32, kind="ExternalInput").ap()
    w1blk_h = nc.dram_tensor("w1blk", [64, 1024], F32, kind="ExternalInput").ap()
    b1T_h = nc.dram_tensor("b1T", [128, 8], F32, kind="ExternalInput").ap()
    w2blk_h = nc.dram_tensor("w2blk", [1024, 64], F32, kind="ExternalInput").ap()
    b2c_h = nc.dram_tensor("b2c", [64, 1], F32, kind="ExternalInput").ap()
    woutA_h = nc.dram_tensor("woutA", [65, 1024], F32, kind="ExternalInput").ap()
    onesb_h = nc.dram_tensor("onesb", [128, 128], BF16, kind="ExternalInput").ap()
    identb_h = nc.dram_tensor("identb", [128, 128], BF16, kind="ExternalInput").ap()
    sel2_h = nc.dram_tensor("sel2", [128, 2], BF16, kind="ExternalInput").ap()
    out_h = nc.dram_tensor("out", [T, D], F32, kind="ExternalOutput").ap()

    x_tiles_h = x_h.rearrange("(i p) d -> i p d", p=128)
    out_tiles_h = out_h.rearrange("(i p) d -> i p d", p=128)
    wq_view3 = wqkvT_h.rearrange("d (r e) -> d r e", r=3)

    OFF = {}
    off = 0

    def alloc(name, nbytes, align=32):
        nonlocal off
        off = (off + align - 1) // align * align
        OFF[name] = off
        off += nbytes

    alloc("x", 16 * 1024 * 4)
    alloc("xlnT", 8 * 2048 * 2)
    alloc("heads", 4 * 2048 * 2)
    alloc("qkT", 2 * 4 * 2048 * 2)
    alloc("vs", 2 * 16 * 256 * 2)
    alloc("wqkv", 8 * 768 * 2)
    alloc("woT", 2 * 2 * 1024 * 2)
    alloc("pt", 3 * 1024 * 2)
    alloc("xlnsc", 2 * 1024 * 2)
    alloc("rd", 512 * 4)
    alloc("ffnw", 8 * 64 * 2)
    alloc("w1blk", 1024 * 2)
    alloc("w2blk", 8 * 64 * 2)
    alloc("woutA", 1024 * 2)
    alloc("onesb", 128 * 2)
    alloc("identb", 128 * 2)
    alloc("sel2", 2 * 2)
    alloc("bqkvT", 16 * 4)
    alloc("bq32", 8 * 4)
    alloc("bvbf", 8 * 2)
    alloc("ssum", 16 * 4)
    alloc("ssq", 16 * 4)
    alloc("sm", 16 * 4)
    alloc("sv", 16 * 4)
    alloc("sr", 16 * 4)
    alloc("smq", 16 * 4)
    alloc("sp1", 16 * 4)
    alloc("smr", 16 * 4)
    alloc("n16", 16 * 4)
    alloc("nqk", 4 * 4)
    alloc("biasg", 4 * 4)
    alloc("rowbuf", 1024 * 4)
    alloc("binc", 4)
    alloc("b2c", 4)
    alloc("b1T", 8 * 4)
    alloc("bnd", 4)
    alloc("bndb", 2)
    total_bytes = (off + 31) // 32 * 32

    arena_t = nc.alloc_sbuf_tensor("arena", [128, total_bytes], U8)
    arena = arena_t.ap()

    def view(name, shape, dtype, parts=128, p0=0):
        esz = 4 if dtype == F32 else 2
        n = 1
        for s in shape:
            n *= s
        a = arena[p0:p0 + parts, OFF[name]:OFF[name] + n * esz].bitcast(dtype)
        if len(shape) > 1:
            pat = " ".join(f"s{i}" for i in range(len(shape)))
            kw = {f"s{i}": shape[i] for i in range(len(shape))}
            a = a.rearrange(f"p ({pat}) -> p {pat}", **kw)
        return a

    x_s = view("x", [16, 1024], F32)
    xlnT = view("xlnT", [8, 2048], BF16)
    heads = view("heads", [4, 2048], BF16)
    qkTb = view("qkT", [2, 4, 2048], BF16)
    vsb = view("vs", [2, 16, 256], BF16)
    wqkv = view("wqkv", [8, 768], BF16)
    hT = view("wqkv", [8, 512], BF16)
    woTs = view("woT", [2, 2, 1024], BF16)
    pt3_ = view("pt", [3, 1024], BF16)
    ptx_ = arena[:, OFF["ffnw"]:OFF["ffnw"] + 2048].bitcast(BF16)

    def pt_s(j):
        return pt3_[:, j, :] if j < 3 else ptx_
    xlnsc = view("xlnsc", [2, 1024], BF16)
    sqb = view("xlnsc", [2048], BF16)
    rd = view("rd", [512], F32)
    win_s = view("ffnw", [8, 64], BF16)
    w1blk_s = view("w1blk", [1024], BF16, parts=64)
    w2blk_s = view("w2blk", [8, 64], BF16)
    woutA_s = view("woutA", [1024], BF16, parts=65)
    sT = view("vs", [2048], BF16, parts=64)
    gT_full = arena[0:65, OFF["vs"] + 4096:OFF["vs"] + 8192].bitcast(BF16)
    onesb = view("onesb", [128], BF16)
    identb = view("identb", [128], BF16)
    sel2 = view("sel2", [2], BF16)
    bqkvT = view("bqkvT", [16], F32)
    bq32 = view("bq32", [8], F32)
    bvbf = view("bvbf", [8], BF16)
    ssum = view("ssum", [16], F32)
    ssq = view("ssq", [16], F32)
    sm = view("sm", [16], F32)
    sv = view("sv", [16], F32)
    sr = view("sr", [16], F32)
    smq = view("smq", [16], F32)
    sp1 = view("sp1", [16], F32)
    smr = view("smr", [16], F32)
    n16 = view("n16", [16], F32)
    nqk = view("nqk", [4], F32)
    biasg = view("biasg", [4], F32)
    bo_s = view("rowbuf", [1024], F32, parts=1, p0=0)
    bo2b = arena[0:1, OFF["rd"]:OFF["rd"] + 2048].bitcast(BF16)
    binc_s = view("binc", [1], F32, parts=64)
    b2c_s = view("b2c", [1], F32, parts=64)
    b1T_s = view("b1T", [8], F32)
    bnd_s = view("bnd", [1], F32, parts=1)
    bndb_s = view("bndb", [1], BF16, parts=1)

    ps_a = nc.alloc_psum_tensor("ps_a", [128, 1024], F32).ap()
    ps_b = nc.alloc_psum_tensor("ps_b", [128, 1024], F32).ap()
    ps_c = nc.alloc_psum_tensor("ps_c", [128, 512], F32).ap()
    ps_d = nc.alloc_psum_tensor("ps_d", [128, 512], F32).ap()
    ps_e = nc.alloc_psum_tensor("ps_e", [128, 512], F32).ap()
    ps_f = nc.alloc_psum_tensor("ps_f", [128, 512], F32).ap()
    sc_slot = [ps_a, ps_b]
    ps_eb = ps_e.bitcast(BF16)
    ps_fb = ps_f.bitcast(BF16)

    p = Prog()

    for i in range(NT):
        p.emit("sync", lambda e, i=i: e.dma_start(x_s[:, i, :], x_tiles_h[i]),
               inc=("sx", 16))
    for dst, srch in ((bqkvT, bqkvT_h), (bo_s, bo_h), (binc_s, binc_h),
                      (b2c_s, b2c_h), (b1T_s, b1T_h), (onesb, onesb_h),
                      (identb, identb_h), (sel2, sel2_h)):
        p.emit("sync", lambda e, d=dst, s=srch: e.dma_start(d[:, :], s),
               inc=("swB", 16))
    swB_done = p.val("swB")
    p.emit("gpsimd", lambda e: e.dma_start(bvbf[:, :], bvTp_h), inc=("swA", 16))

    def emit_wqkv_dma(g, extra=()):
        for c in range(ND):
            src = wq_view3[128 * c:128 * c + 128, :, 256 * g:256 * g + 256]
            dst = wqkv[:, c, :].rearrange("p (r e) -> p r e", r=3)
            p.emit("gpsimd", lambda e, s=src, d=dst: e.dma_start(d, s),
                   waits=list(extra) if c == 0 else (), inc=("swq", 16))
        return p.val("swq")

    def emit_woT_dma(g, extra=()):
        for j, c in enumerate((2 * g, 2 * g + 1)):
            p.emit("gpsimd", lambda e, g=g, j=j, c=c: e.dma_start(
                woTs[:, g % 2, j, :], woTp_h[128 * c:128 * c + 128, :]),
                waits=list(extra) if j == 0 else (), inc=("swo", 16))
        return p.val("swo")

    wqkv_done = {0: emit_wqkv_dma(0)}
    p.emit("vector", lambda e: e.tensor_scalar_mul(bq32[:, :], bqkvT[:, 0:8], 1.0 / 32.0),
           waits=[("swB", swB_done)], inc=("smisc", 1))

    sq_psum = [ps_a, ps_b]

    def emit_ln(is_ln2, tile_waits, tr_banks=None, dma_tr=False):
        for i in range(NT):
            p.emit("scalar", lambda e, i=i: e.activation(
                sq_psum[i % 2][:, :], x_s[:, i, :], ACTF.Square,
                accum_out=ssq[:, i:i + 1]), waits=tile_waits(i), inc=("ssq1", 1))
        for i in range(NT):
            p.emit("scalar", lambda e, i=i: e.activation(
                sq_psum[i % 2][:, :], x_s[:, i, :], ACTF.Copy,
                accum_out=ssum[:, i:i + 1]), waits=tile_waits(i), inc=("ssq1", 1))
        p.emit("vector", lambda e: e.tensor_scalar_mul(
            sm[:, :], ssum[:, :], 1.0 / 1024.0),
            waits=[("ssq1", p.val("ssq1"))], inc=("sln", 1))
        p.emit("vector", lambda e: e.tensor_scalar(
            sv[:, :], ssq[:, :], 1.0 / 1024.0, EPS, ALU.mult, ALU.add),
            waits=[("ssq1", p.val("ssq1"))], inc=("sln", 1))
        p.emit("vector", lambda e: e.tensor_mul(smq[:, :], sm[:, :], sm[:, :]),
               waits=[("sln", p.val("sln"))], inc=("sln", 1))
        p.emit("vector", lambda e: e.tensor_sub(sv[:, :], sv[:, :], smq[:, :]),
               waits=[("sln", p.val("sln"))], inc=("sln", 1))
        p.emit("scalar", lambda e: e.activation(sv[:, :], sv[:, :], ACTF.Ln),
               waits=[("sln", p.val("sln"))], inc=("ssq2", 1))
        p.emit("scalar", lambda e: e.activation(sr[:, :], sv[:, :], ACTF.Exp, scale=-0.5),
               waits=[("ssq2", p.val("ssq2"))], inc=("ssq2", 1))
        p.emit("vector", lambda e: e.engine_nop(),
               waits=[("ssq2", p.val("ssq2"))], inc=("sln", 1))
        if not is_ln2:
            p.emit("vector", lambda e: e.tensor_scalar(
                sp1[:, :], sr[:, :], 1.0, None, ALU.add),
                waits=[("sln", p.val("sln"))], inc=("sln", 1))
            p.emit("vector", lambda e: e.tensor_mul(smr[:, :], sm[:, :], sr[:, :]),
                   waits=[("sln", p.val("sln"))], inc=("sln", 1))
        stats_done = p.val("sln")
        tr_after = {}
        tr_sem = "strd" if dma_tr else "str"
        for i in range(NT):
            aw = [("sln", stats_done)]
            if i >= 2:
                aw.append((tr_sem, tr_after[i - 2]))
            p.emit("vector", lambda e, i=i: e.tensor_scalar(
                xlnsc[:, i % 2, :], x_s[:, i, :], sm[:, i:i + 1], sr[:, i:i + 1],
                ALU.subtract, ALU.mult), waits=aw, inc=("sln", 1))
            lnsc_done = p.val("sln")
            if not is_ln2:
                p.emit("vector", lambda e, i=i: e.tensor_scalar(
                    x_s[:, i, :], x_s[:, i, :], sp1[:, i:i + 1], smr[:, i:i + 1],
                    ALU.mult, ALU.subtract), inc=("sln", 1))
            if dma_tr:
                for c in range(ND):
                    qn = "scalar" if c % 2 else "sync"
                    p.emit(qn, lambda e, i=i, c=c: e.dma_start(
                        xlnT[:, c, 128 * i:128 * (i + 1)],
                        xlnsc[:, i % 2, 128 * c:128 * (c + 1)], transpose=True),
                        waits=[("sln", lnsc_done)] if c < 2 else (),
                        inc=("strd", 16))
                tr_after[i] = p.val("strd")
                rec = ("strd", p.val("strd"))
            else:
                nb = len(tr_banks)
                pu = tr_banks[i % nb]
                trw = [("sln", lnsc_done)]
                if i >= nb:
                    trw.append(("strev", p.val("strev") - (nb - 1)))
                for c in range(ND):
                    p.emit("tensor", lambda e, i=i, c=c, pu=pu: e.transpose(
                        pu[:, 128 * c:128 * (c + 1)],
                        xlnsc[:, i % 2, 128 * c:128 * (c + 1)],
                        identb[:, :]), waits=trw if c == 0 else (), inc=("str", 1))
                tr_after[i] = p.val("str")
                p.emit("vector", lambda e, i=i, pu=pu: e.tensor_copy(
                    xlnT[:, :, 128 * i:128 * (i + 1)],
                    pu[:, 0:1024].rearrange("p (c t) -> p c t", c=8)),
                    waits=[("str", p.val("str"))], inc=("strev", 1))
                rec = ("strev", p.val("strev"))
            tr_after.setdefault(i, None)
            if not dma_tr:
                tr_after[i] = p.val("str")
            yield_rec = rec
            tile_recs.append(yield_rec)

    tile_recs = []
    ps_ab = ps_a.bitcast(BF16)
    ps_bb = ps_b.bitcast(BF16)
    tile_recs.clear()
    emit_ln(False, lambda i: [("sx", 16 * (i + 1)), ("swB", swB_done)],
            tr_banks=[ps_ab, ps_bb])
    ln1_tr = {i: tile_recs[i][1] for i in range(NT)}
    ln1_strev = p.val("strev")

    vdone = {}

    def qkv_chunks(g, banks=None):
        banks = banks or [ps_f]
        nb = len(banks)
        qkT = qkTb[:, g % 2]
        vs = vsb[:, g % 2]

        def mk_qk(d, ts_):
            def emit():
                u = banks[p.val("suu") % nb]
                uw = [("sue", max(0, p.val("sue") - (nb - 1))),
                      ("swq", wqkv_done[g]),
                      ("strev", ln1_tr[4 * ts_ + 3])]
                r = 0 if d < 2 else 1
                sub = d % 2
                for c in range(ND):
                    lhsT = wqkv[:, c, :].rearrange(
                        "p (r e) -> p r e", r=3)[:, r, 128 * sub:128 * (sub + 1)]
                    rhs = xlnT[:, c, QS * ts_:QS * (ts_ + 1)]
                    p.emit("tensor", lambda e, l=lhsT, r_=rhs, c=c, u=u: e.matmul(
                        u[:, :], l, r_, start=(c == 0), stop=(c == ND - 1)),
                        waits=uw if c == 0 else (),
                        inc=("suu", 1) if c == ND - 1 else None)
                mmd = p.val("suu")
                if d < 2:
                    p.emit("vector", lambda e, d=d, t=ts_, u=u: e.tensor_scalar(
                        qkT[:, d, QS * t:QS * (t + 1)], u[:, :], 1.0 / 32.0,
                        bq32[:, 2 * g + d:2 * g + d + 1], ALU.mult, ALU.add),
                        waits=[("suu", mmd)], inc=("sue", 1))
                else:
                    p.emit("vector", lambda e, d=d, t=ts_, u=u: e.tensor_scalar(
                        qkT[:, d, QS * t:QS * (t + 1)], u[:, :], 1.0,
                        bqkvT[:, 8 + 2 * g + (d - 2):8 + 2 * g + (d - 2) + 1],
                        ALU.mult, ALU.add),
                        waits=[("suu", mmd)], inc=("sue", 1))
            return emit

        def mk_v(i):
            def emit():
                u = banks[p.val("suu") % nb]
                uw = [("sue", max(0, p.val("sue") - (nb - 1))),
                      ("swq", wqkv_done[g]),
                      ("strev", ln1_tr[i])]
                for c in range(ND):
                    lhsT = xlnT[:, c, 128 * i:128 * (i + 1)]
                    rhs = wqkv[:, c, :].rearrange("p (r e) -> p r e", r=3)[:, 2, :]
                    p.emit("tensor", lambda e, l=lhsT, r_=rhs, c=c, u=u: e.matmul(
                        u[:, 0:256], l, r_, start=(c == 0), stop=(c == ND - 1)),
                        waits=uw if c == 0 else (),
                        inc=("suu", 1) if c == ND - 1 else None)
                mmd = p.val("suu")
                p.emit("vector", lambda e, i=i, u=u: e.tensor_copy(
                    vs[:, i, :], u[:, 0:256]),
                    waits=[("suu", mmd)], inc=("sue", 1))
                vdone[(g, i)] = p.val("sue")
            return emit

        qk = [mk_qk(d, ts_) for d in range(4) for ts_ in range(NQ)]
        vv = [mk_v(i) for i in range(NT)]
        return qk, vv

    def bound_chunks(g, banks=None):
        banks = banks or [ps_f]
        nb = len(banks)
        qkT = qkTb[:, g % 2]
        chunks = []

        def mk_sq(ci):
            def emit():
                p.emit("gpsimd", lambda e, ci=ci: e.tensor_tensor(
                    sqb[:, :], qkT[:, ci, :], qkT[:, ci, :], ALU.mult),
                    waits=[("sue", p.val("sue")), ("sgq", p.val("sgq")),
                           ("sln", p.val("sln"))],
                    inc=("sgq", 1))
            return emit

        def mk_mm(ci, j):
            def emit():
                u = banks[p.val("suu") % nb]
                w = [("sgq", p.val("sgq")),
                     ("sue", max(0, p.val("sue") - (nb - 1))),
                     ("sbv", p.val("sbv"))]
                p.emit("tensor", lambda e, j=j, u=u: e.matmul(
                    u[0:2, :], sel2[:, :], sqb[:, 512 * j:512 * (j + 1)],
                    start=True, stop=True), waits=w, inc=("suu", 1))
                p.emit("vector", lambda e, ci=ci, j=j, u=u: e.reduce_max(
                    n16[0:2, 4 * ci + j:4 * ci + j + 1], u[0:2, :], axis=AX),
                    waits=[("suu", p.val("suu"))], inc=("sbv", 1))
                p.emit("vector", lambda e: e.engine_nop(),
                       waits=[("sbv", p.val("sbv"))], inc=("sue", 1))
            return emit

        def mk_fin():
            def emit():
                p.emit("vector", lambda e: e.reduce_max(
                    nqk[0:2, 0:1], n16[0:2, 0:8], axis=AX),
                    waits=[("sbv", p.val("sbv"))], inc=("sbv", 1))
                p.emit("vector", lambda e: e.reduce_max(
                    nqk[0:2, 1:2], n16[0:2, 8:16], axis=AX),
                    waits=[("sbv", p.val("sbv"))], inc=("sbv", 1))
                shmask = [1, 0] + list(range(2, 32))
                p.emit("vector", lambda e, m=shmask: e.stream_shuffle(
                    nqk[0:2, 2:4], nqk[0:2, 0:2], m),
                    waits=[("sbv", p.val("sbv"))], inc=("sbv", 1))
                p.emit("vector", lambda e: e.tensor_max(
                    nqk[0:1, 0:2], nqk[0:1, 0:2], nqk[0:1, 2:4]),
                    waits=[("sbv", p.val("sbv"))], inc=("sbv", 1))
                p.emit("vector", lambda e: e.tensor_mul(
                    bnd_s[:, :], nqk[0:1, 0:1], nqk[0:1, 1:2]),
                    waits=[("sbv", p.val("sbv"))], inc=("sbv", 1))
                p.emit("scalar", lambda e: e.activation(
                    bnd_s[:, :], bnd_s[:, :], ACTF.Ln),
                    waits=[("sbv", p.val("sbv"))], inc=("sbs", 1))
                p.emit("scalar", lambda e: e.activation(
                    bnd_s[:, :], bnd_s[:, :], ACTF.Exp, scale=0.5),
                    waits=[("sbs", p.val("sbs"))], inc=("sbs", 1))
                p.emit("vector", lambda e: e.tensor_scalar(
                    bnd_s[:, :], bnd_s[:, :], -1.0, BSHIFT, ALU.mult, ALU.add),
                    waits=[("sbs", p.val("sbs"))], inc=("sbv", 1))
                p.emit("vector", lambda e: e.tensor_copy(bndb_s[:, :], bnd_s[:, :]),
                       waits=[("sbv", p.val("sbv"))], inc=("sbv", 1))
                u = banks[p.val("suu") % nb]
                uw = [("sbv", p.val("sbv")),
                      ("sue", max(0, p.val("sue") - (nb - 1)))]
                p.emit("tensor", lambda e, u=u: e.matmul(
                    u[:, 0:1], onesb[0:1, :], bndb_s[:, :],
                    start=True, stop=True), waits=uw, inc=("suu", 1))
                p.emit("vector", lambda e, g=g, u=u: e.tensor_copy(
                    biasg[:, g:g + 1], u[:, 0:1]),
                    waits=[("suu", p.val("suu"))], inc=("sue", 1))
                p.emit("vector", lambda e: e.engine_nop(),
                       waits=[("sue", p.val("sue"))], inc=("sbnd", 1))
            return emit

        for ci in range(4):
            chunks.append(mk_sq(ci))
            for j in range(4):
                chunks.append(mk_mm(ci, j))
        chunks.append(mk_fin())
        return chunks

    nrm_done = {}
    opadd = {}
    woT_done = {}

    def partial_chunks(g, with_bias, banks=None):
        banks = banks or [ps_f]
        nb = len(banks)
        chunks = []
        hslots = ((2 * g) % 4, (2 * g + 1) % 4)

        def mk(t, half):
            def emit():
                u = banks[p.val("suu") % nb]
                uw = [("sue", max(0, p.val("sue") - (nb - 1))),
                      ("swo", woT_done[g]),
                      ("snrm", nrm_done[g * NQ + NQ - 1])]
                for j in range(2):
                    p.emit("tensor", lambda e, j=j, t=t, h=half, u=u: e.matmul(
                        u[:, :], heads[:, hslots[j], 128 * t:128 * (t + 1)],
                        woTs[:, g % 2, j, 512 * h:512 * (h + 1)],
                        start=(j == 0), stop=(j == 1 and not with_bias)),
                        waits=uw if j == 0 else (),
                        inc=("suu", 1) if (j == 1 and not with_bias) else None)
                if with_bias:
                    p.emit("tensor", lambda e, t=t, h=half, u=u: e.matmul(
                        u[:, :], onesb[0:1, :], bo2b[:, 512 * h:512 * (h + 1)],
                        start=False, stop=True),
                        waits=[("sbob", 1)], inc=("suu", 1))
                mmd = p.val("suu")
                p.emit("vector", lambda e, t=t, h=half, u=u: e.tensor_add(
                    x_s[:, t, 512 * h:512 * (h + 1)],
                    x_s[:, t, 512 * h:512 * (h + 1)], u[:, :]),
                    waits=[("suu", mmd)], inc=("sue", 1))
                opadd[(g, t, half)] = p.val("sue")
            return emit

        def mk_bv():
            def emit():
                for half in range(2):
                    u = banks[p.val("suu") % nb]
                    uw = [("sue", max(0, p.val("sue") - (nb - 1))),
                          ("swo", woT_done[g]),
                          ("snrm", nrm_done[g * NQ + NQ - 1])]
                    for j in range(2):
                        p.emit("tensor", lambda e, j=j, h=half, u=u: e.matmul(
                            u[0:1, :], bvbf[:, 2 * g + j:2 * g + j + 1],
                            woTs[:, g % 2, j, 512 * h:512 * (h + 1)],
                            start=(j == 0), stop=(j == 1)),
                            waits=uw if j == 0 else (),
                            inc=("suu", 1) if j == 1 else None)
                    p.emit("vector", lambda e, h=half, u=u: e.tensor_add(
                        bo_s[:, 512 * h:512 * (h + 1)],
                        bo_s[:, 512 * h:512 * (h + 1)], u[0:1, :]),
                        waits=[("suu", p.val("suu")), ("sbvv", p.val("sbvv")),
                               ("swB", swB_done)],
                        inc=("sue", 1))
                    p.emit("vector", lambda e: e.engine_nop(),
                           waits=[("sue", p.val("sue"))], inc=("sbvv", 1))
            return emit

        chunks.append(mk_bv())
        for t in range(NT):
            for half in range(2):
                chunks.append(mk(t, half))
        return chunks

    exp_done = {}
    pv_done = {}

    def emit_attn_group(g, side):
        qkT = qkTb[:, g % 2]
        vs = vsb[:, g % 2]
        base_pti = g * NQ * NK * 2
        si = 0
        total_iters = NQ * NK

        def emit_scores_exp(q, k):
            for pr in range(2):
                pti = base_pti + (q * NK + k) * 2 + pr
                slot = pti % 2
                pslot = pti % 4
                w = [("sbnd", g + 1)]
                if pti >= 2:
                    w.append(("sexp", exp_done[pti - 2]))
                if pti == 0:
                    w.append(("ssq1", 16))
                p.emit("tensor", lambda e, s=slot, pr=pr, k=k, q=q, qkT=qkT: e.matmul(
                    sc_slot[s][:, 0:512],
                    qkT[0:64, 2 + pr, 128 * k:128 * (k + 1)],
                    qkT[0:64, pr, QS * q:QS * (q + 1)],
                    start=True, stop=True, tile_position=(0, 0)),
                    waits=w, inc=None)
                p.emit("tensor", lambda e, s=slot, pr=pr, k=k, q=q, qkT=qkT: e.matmul(
                    sc_slot[s][:, 512:1024],
                    qkT[64:128, 2 + pr, 128 * k:128 * (k + 1)],
                    qkT[64:128, pr, QS * q:QS * (q + 1)],
                    start=True, stop=True, tile_position=(64, 0)),
                    inc=("ssc", 1))
                scd = p.val("ssc")
                ew = [("ssc", scd)]
                if pti >= 4:
                    ew.append(("spv", pv_done[pti - 4]))
                p.emit("scalar", lambda e, s=slot, ps_=pslot, g=g: e.activation(
                    pt_s(ps_), sc_slot[s][:, :], ACTF.Exp,
                    bias=biasg[:, g:g + 1], scale=1.0),
                    waits=ew, inc=("sexp", 1))
                exp_done[pti] = p.val("sexp")

        def emit_v_dn(q, k):
            gq = g * NQ + q
            vw = [("sexp", exp_done[base_pti + (q * NK + k) * 2 + 1]),
                  ("sue", vdone[(g, k)])]
            if k == 0 and gq > 0:
                vw.append(("snrm", nrm_done[gq - 1]))
            if k == 0 and gq == 0:
                vw.append(("sue", head_sue))
            sA = (base_pti + (q * NK + k) * 2) % 4
            sB = (sA + 1) % 4
            for half, po in ((0, ps_c), (1, ps_d)):
                for a in range(4):
                    ps_slot = sA if a < 2 else sB
                    p.emit("tensor", lambda e, po=po, a=a, k=k, half=half, s=ps_slot, vs=vs: e.matmul(
                        po[32 * a:32 * (a + 1), :],
                        vs[:, k, 128 * half + 32 * a:128 * half + 32 * (a + 1)],
                        pt_s(s)[:, 512 * (a % 2):512 * (a % 2) + 512],
                        start=(k == 0), stop=(k == NK - 1),
                        tile_position=(0, 32 * a)),
                        waits=vw if (half == 0 and a == 0) else (), inc=None)
            for a in range(4):
                ps_slot = sA if a < 2 else sB
                p.emit("tensor", lambda e, a=a, k=k, s=ps_slot: e.matmul(
                    ps_e[32 * a:32 * (a + 1), :],
                    onesb[:, 0:32],
                    pt_s(s)[:, 512 * (a % 2):512 * (a % 2) + 512],
                    start=(k == 0), stop=(k == NK - 1),
                    tile_position=(0, 32 * a)),
                    inc=("spv", 1) if a == 3 else None)
            pv_done[base_pti + (q * NK + k) * 2] = p.val("spv")
            pv_done[base_pti + (q * NK + k) * 2 + 1] = p.val("spv")

        def emit_norm(q):
            gq = g * NQ + q
            lastpv = pv_done[base_pti + (q * NK + NK - 1) * 2]
            p.emit("vector", lambda e: e.reciprocal(rd[:, :], ps_e[:, :]),
                   waits=[("spv", lastpv)], inc=("snv", 1))
            p.emit("vector", lambda e, g=g, q=q: e.tensor_mul(
                heads[:, (2 * g) % 4, QS * q:QS * (q + 1)], ps_c[:, :], rd[:, :]),
                waits=[("snv", p.val("snv"))], inc=("snv", 1))
            p.emit("vector", lambda e, g=g, q=q: e.tensor_mul(
                heads[:, (2 * g + 1) % 4, QS * q:QS * (q + 1)], ps_d[:, :], rd[:, :]),
                inc=("snrm", 1))
            nrm_done[gq] = p.val("snrm")

        steps = [(q, k) for q in range(NQ) for k in range(NK)]
        for idx in range(total_iters + 1):
            if idx < total_iters:
                emit_scores_exp(*steps[idx])
            if idx >= 1:
                qp, kp = steps[idx - 1]
                emit_v_dn(qp, kp)
                if kp == NK - 1:
                    emit_norm(qp)
            target = ((idx + 1) * len(side) + total_iters) // (total_iters + 1)
            while si < min(len(side), target):
                side[si]()
                si += 1
        while si < len(side):
            side[si]()
            si += 1

    HEAD2B = bool(int(os.environ.get("MK_HEAD2B", "1")))
    hqk, hv = qkv_chunks(0, banks=[ps_f, ps_e] if HEAD2B else [ps_f])
    for ch in hqk:
        ch()
    for ch in bound_chunks(0, banks=[ps_f, ps_e] if HEAD2B else [ps_f]):
        ch()
    for ch in hv:
        ch()
    head_sue = p.val("sue")

    for g in range(NG):
        side = []
        if g + 1 < NG:
            wqkv_done[g + 1] = emit_wqkv_dma(g + 1, extra=[("suu", p.val("suu"))])
        woT_done[g] = emit_woT_dma(
            g, extra=[("sue", opadd.get((g - 2, NT - 1, 1), 0))])
        if g >= 1:
            side += partial_chunks(g - 1, with_bias=False)
        if g + 1 < NG:
            qk, vv = qkv_chunks(g + 1)
            side += qk
            side += bound_chunks(g + 1)
            side += vv
        emit_attn_group(g, side)
    attn_done = p.val("snrm")

    p.emit("gpsimd", lambda e: e.dma_start(
        win_s[:, :, :], win_h.rearrange("(c p) m -> p c m", p=128)),
        waits=[("spv", p.val("spv"))], inc=("swA", 16))
    p.emit("gpsimd", lambda e: e.dma_start(w1blk_s[:, :], w1blk_h), inc=("swA", 16))
    p.emit("gpsimd", lambda e: e.dma_start(
        w2blk_s[:, :, :], w2blk_h.rearrange("(c p) m -> p c m", p=128)), inc=("swA", 16))
    p.emit("gpsimd", lambda e: e.dma_start(woutA_s[:, :], woutA_h), inc=("swA", 16))
    swA_ffn = p.val("swA")

    TAIL2B = bool(int(os.environ.get("MK_TAIL2B", "1")))
    tail_partial = partial_chunks(3, with_bias=True,
                                  banks=[ps_f, ps_e] if TAIL2B else [ps_f])
    tail_partial[0]()
    p.emit("vector", lambda e: e.tensor_copy(bo2b[:, :], bo_s[:, :]),
           waits=[("sbvv", p.val("sbvv"))], inc=("sbob", 1))
    for ch in tail_partial[1:]:
        ch()
    op_all = p.val("sue")

    def ln2_waits(i):
        return [("sue", opadd[(3, i, 1)])]

    LN2_DMA = bool(int(os.environ.get("MK_LN2DMA", "1")))
    tile_recs.clear()
    if LN2_DMA:
        emit_ln(True, ln2_waits, dma_tr=True)
        ln2_tr = {i: ("strd", tile_recs[i][1]) for i in range(NT)}
    else:
        emit_ln(True, ln2_waits, tr_banks=[ps_c.bitcast(BF16), ps_d.bitcast(BF16)])
        ln2_tr = {i: ("strev", p.val("strev")) for i in range(NT)}

    p.emit("vector", lambda e: e.memset(gT_full[64:65, :], 1.0),
           waits=[("snrm", attn_done)], inc=("sgone", 1))

    for s in range(NQ):
        sl = slice(QS * s, QS * (s + 1))
        for c in range(ND):
            p.emit("tensor", lambda e, c=c, sl=sl: e.matmul(
                ps_c[0:64, :], win_s[:, c, :], xlnT[:, c, sl],
                start=(c == 0), stop=(c == ND - 1)),
                waits=[ln2_tr[4 * s + 3], ("swA", swA_ffn),
                       ("snv", p.val("snv"))] if c == 0 else (),
                inc=("sff", 1) if c == ND - 1 else None)
        p.emit("vector", lambda e, sl=sl: e.tensor_scalar(
            sT[:, sl], ps_c[0:64, :], 1.0, binc_s[:, :], ALU.mult, ALU.add),
            waits=[("sff", p.val("sff"))], inc=("sfe", 1))
        st_done = p.val("sfe")
        for c in range(ND):
            ph = [ps_a, ps_b][c % 2]
            phv = ph[:, 512 * ((c // 2) % 2):512 * ((c // 2) % 2) + 512]
            w = [("sfe", st_done)]
            if c >= 4:
                w.append(("sfe", p.val("sfe") - 3))
            p.emit("tensor", lambda e, c=c, sl=sl, phv=phv: e.matmul(
                phv, w1blk_s[:, 128 * c:128 * (c + 1)], sT[:, sl],
                start=True, stop=True), waits=w, inc=("sff", 1))
            p.emit("vector", lambda e, c=c, phv=phv: e.tensor_scalar(
                hT[:, c, :], phv, b1T_s[:, c:c + 1], 0.0, ALU.add, ALU.max),
                waits=[("sff", p.val("sff"))], inc=("sfe", 1))
        for c in range(ND):
            p.emit("tensor", lambda e, c=c: e.matmul(
                ps_d[0:64, :], w2blk_s[:, c, :], hT[:, c, :],
                start=(c == 0), stop=(c == ND - 1)),
                waits=[("sfe", p.val("sfe"))] if c == 0 else (),
                inc=("sff", 1) if c == ND - 1 else None)
        p.emit("vector", lambda e, sl=sl: e.tensor_scalar(
            gT_full[0:64, sl], ps_d[0:64, :], 1.0, b2c_s[:, :], ALU.mult, ALU.add),
            waits=[("sff", p.val("sff"))], inc=("sfe", 1))
        gt_done = p.val("sfe")
        for tt in range(4):
            i = 4 * s + tt
            for half in range(2):
                u = [ps_e, ps_f][(2 * i + half) % 2]
                w = [("sfe", gt_done), ("sgone", 1), ("sue", op_all)]
                if p.val("sfev2") >= 1:
                    w.append(("sfev2", p.val("sfev2") - 1))
                p.emit("tensor", lambda e, i=i, h=half, u=u: e.matmul(
                    u[:, :], gT_full[:, 128 * i:128 * (i + 1)],
                    woutA_s[:, 512 * h:512 * (h + 1)],
                    start=True, stop=True), waits=w, inc=("sff", 1))
                p.emit("vector", lambda e, i=i, h=half, u=u: e.tensor_add(
                    x_s[:, i, 512 * h:512 * (h + 1)],
                    x_s[:, i, 512 * h:512 * (h + 1)], u[:, :]),
                    waits=[("sff", p.val("sff"))], inc=("sfev2", 1))
            p.emit("sync", lambda e, i=i: e.dma_start(out_tiles_h[i], x_s[:, i, :]),
                   waits=[("sfev2", p.val("sfev2"))], inc=("sout", 16))

    p.emit("sync", lambda e: None, waits=[("sout", 16 * NT)])

    semh = {}
    from contextlib import ExitStack
    ctx = ExitStack()
    for name in p.counts:
        semh[name] = ctx.enter_context(nc.semaphore(name))
    block = ctx.enter_context(nc.Block())
    with ctx:
        def runner(eng_obj, name):
            for fn, waits, inc in p.ops[name]:
                for s, v in waits:
                    if v > 0:
                        eng_obj.wait_ge(semh[s], v)
                ins = fn(eng_obj)
                if inc is not None:
                    s, amt = inc
                    ins.then_inc(semh[s], amt)

        @block.tensor
        def _(t):
            runner(t, "tensor")

        @block.scalar
        def _(s):
            runner(s, "scalar")

        @block.vector
        def _(v):
            runner(v, "vector")

        @block.sync
        def _(s):
            runner(s, "sync")

        @block.gpsimd
        def _(g):
            runner(g, "gpsimd")

    return nc


_NC_CACHE = {}


def _get_nc():
    key = "v1"
    if key not in _NC_CACHE:
        _NC_CACHE[key] = _build_nc()
    return _NC_CACHE[key]


def _prep_inputs(inputs):
    f32 = np.float32
    x = np.ascontiguousarray(np.asarray(inputs["L0_attn_x"], dtype=f32))
    w_qkv = np.asarray(inputs["L0_attn_w_qkv"], dtype=f32)
    b_qkv = np.asarray(inputs["L0_attn_b_qkv"], dtype=f32)
    w_o = np.asarray(inputs["L0_attn_w_o"], dtype=f32)
    b_o = np.asarray(inputs["L0_attn_b_o"], dtype=f32)
    Win = np.asarray(inputs["L0_ffn_Win"], dtype=f32)
    bin_ = np.asarray(inputs["L0_ffn_bin_"], dtype=f32)
    W1 = np.asarray(inputs["L0_ffn_W1"], dtype=f32)
    b1 = np.asarray(inputs["L0_ffn_b1"], dtype=f32)
    W2 = np.asarray(inputs["L0_ffn_W2"], dtype=f32)
    b2 = np.asarray(inputs["L0_ffn_b2"], dtype=f32)
    Wout = np.asarray(inputs["L0_ffn_Wout"], dtype=f32)
    bout = np.asarray(inputs["L0_ffn_bout"], dtype=f32)

    d_idx = np.arange(1024)
    h = d_idx // 64
    c = d_idx % 64
    newpos = (h // 4) * 256 + (c // 32) * 128 + (h % 4) * 32 + (c % 32)
    src = np.empty(1024, dtype=np.int64)
    src[newpos] = d_idx

    wqkvT = np.ascontiguousarray(w_qkv.T)
    wqkvT_p = wqkvT.copy()
    wqkvT_p[:, 2048:3072] = wqkvT[:, 2048 + src]
    bqkvT = np.ascontiguousarray(b_qkv[:2048].reshape(16, 128).T)
    bvTp = np.ascontiguousarray(b_qkv[2048:][src].reshape(8, 128).T)
    woTp = np.ascontiguousarray(w_o.T[src, :])
    bo = np.ascontiguousarray(b_o.reshape(1, 1024))

    w1blk = np.zeros((64, 1024), dtype=f32)
    mm = np.arange(64)
    for j in range(16):
        w1blk[mm, 16 * mm + j] = W1[:, j]
    b1T = np.ascontiguousarray(b1.reshape(1024).reshape(8, 128).T)
    w2blk = np.zeros((1024, 64), dtype=f32)
    for j in range(16):
        w2blk[16 * mm + j, mm] = W2[:, j]
    woutA = np.ascontiguousarray(np.vstack([Wout, bout.reshape(1, 1024)]))
    binc = np.ascontiguousarray(bin_.reshape(64, 1))
    b2c = np.ascontiguousarray(b2.reshape(64, 1))

    onesb = np.ones((128, 128), dtype=ml_dtypes.bfloat16)
    identb = np.eye(128, dtype=ml_dtypes.bfloat16)
    sel2 = np.zeros((128, 2), dtype=ml_dtypes.bfloat16)
    sel2[0:64, 0] = 1.0
    sel2[64:128, 1] = 1.0

    shared = dict(wqkvT=wqkvT_p, bqkvT=bqkvT, bvTp=bvTp, woTp=woTp, bo=bo,
                  win=np.ascontiguousarray(Win), binc=binc, w1blk=w1blk,
                  b1T=b1T, w2blk=w2blk, b2c=b2c, woutA=woutA,
                  onesb=onesb, identb=identb, sel2=sel2)
    in_maps = []
    for cidx in range(8):
        m = dict(shared)
        m["x"] = np.ascontiguousarray(x[cidx])
        in_maps.append(m)
    return in_maps


def kernel(**inputs):
    in_maps = _prep_inputs(inputs)
    nc = _get_nc()
    res = run_bass_kernel_spmd(nc, in_maps, core_ids=list(range(8)))
    out = np.stack([np.asarray(res.results[cidx]["out"]) for cidx in range(8)])
    return out.astype(np.float32)


# version 06
# speedup vs baseline: 1.0851x; 1.0392x over previous
import os
import numpy as np
import ml_dtypes
import concourse.bass as bass
import concourse.mybir as mybir
from concourse.bass_utils import run_bass_kernel_spmd

F32 = mybir.dt.float32
BF16 = mybir.dt.bfloat16
U8 = mybir.dt.uint8
AX = mybir.AxisListType.X
ALU = mybir.AluOpType
ACTF = mybir.ActivationFunctionType

T = 2048
D = 1024
NT = 16
ND = 8
NG = 4
NQ = 4
NK = 16
QS = 512
EPS = 1e-5
BSHIFT = 50.0

ENGINES = ("tensor", "scalar", "vector", "sync", "gpsimd")


class Prog:
    def __init__(self):
        self.counts = {}
        self.ops = {e: [] for e in ENGINES}

    def emit(self, eng, fn, waits=(), inc=None):
        if inc is not None:
            s, amt = inc
            self.counts[s] = self.counts.get(s, 0) + amt
        self.ops[eng].append((fn, [w for w in waits if w is not None], inc))

    def val(self, s):
        return self.counts.get(s, 0)


def _build_nc():
    nc = bass.Bass("TRN2", target_bir_lowering=False)

    x_h = nc.dram_tensor("x", [T, D], F32, kind="ExternalInput").ap()
    wqkvT_h = nc.dram_tensor("wqkvT", [D, 3 * D], F32, kind="ExternalInput").ap()
    bqkvT_h = nc.dram_tensor("bqkvT", [128, 16], F32, kind="ExternalInput").ap()
    bvTp_h = nc.dram_tensor("bvTp", [128, 8], F32, kind="ExternalInput").ap()
    woTp_h = nc.dram_tensor("woTp", [D, D], F32, kind="ExternalInput").ap()
    bo_h = nc.dram_tensor("bo", [1, D], F32, kind="ExternalInput").ap()
    win_h = nc.dram_tensor("win", [D, 64], F32, kind="ExternalInput").ap()
    binc_h = nc.dram_tensor("binc", [64, 1], F32, kind="ExternalInput").ap()
    w1blk_h = nc.dram_tensor("w1blk", [64, 1024], F32, kind="ExternalInput").ap()
    b1T_h = nc.dram_tensor("b1T", [128, 8], F32, kind="ExternalInput").ap()
    w2blk_h = nc.dram_tensor("w2blk", [1024, 64], F32, kind="ExternalInput").ap()
    b2c_h = nc.dram_tensor("b2c", [64, 1], F32, kind="ExternalInput").ap()
    woutA_h = nc.dram_tensor("woutA", [65, 1024], F32, kind="ExternalInput").ap()
    onesb_h = nc.dram_tensor("onesb", [128, 128], BF16, kind="ExternalInput").ap()
    identb_h = nc.dram_tensor("identb", [128, 128], BF16, kind="ExternalInput").ap()
    sel2_h = nc.dram_tensor("sel2", [128, 2], BF16, kind="ExternalInput").ap()
    out_h = nc.dram_tensor("out", [T, D], F32, kind="ExternalOutput").ap()

    x_tiles_h = x_h.rearrange("(i p) d -> i p d", p=128)
    out_tiles_h = out_h.rearrange("(i p) d -> i p d", p=128)
    wq_view3 = wqkvT_h.rearrange("d (r e) -> d r e", r=3)

    OFF = {}
    off = 0

    def alloc(name, nbytes, align=32):
        nonlocal off
        off = (off + align - 1) // align * align
        OFF[name] = off
        off += nbytes

    alloc("x", 16 * 1024 * 4)
    alloc("xlnT", 8 * 2048 * 2)
    alloc("heads", 4 * 2048 * 2)
    alloc("qkT", 2 * 4 * 2048 * 2)
    alloc("vs", 2 * 16 * 256 * 2)
    alloc("wqkv", 8 * 768 * 2)
    alloc("woT", 2 * 2 * 1024 * 2)
    alloc("pt", 3 * 1024 * 2)
    alloc("xlnsc", 2 * 1024 * 2)
    alloc("rd", 512 * 4)
    alloc("ffnw", 8 * 64 * 2)
    alloc("w1blk", 1024 * 2)
    alloc("w2blk", 8 * 64 * 2)
    alloc("woutA", 1024 * 2)
    alloc("onesb", 128 * 2)
    alloc("identb", 128 * 2)
    alloc("sel2", 2 * 2)
    alloc("bqkvT", 16 * 4)
    alloc("bq32", 8 * 4)
    alloc("bvbf", 8 * 2)
    alloc("ssum", 16 * 4)
    alloc("ssq", 16 * 4)
    alloc("sm", 16 * 4)
    alloc("sv", 16 * 4)
    alloc("sr", 16 * 4)
    alloc("smq", 16 * 4)
    alloc("sp1", 16 * 4)
    alloc("smr", 16 * 4)
    alloc("n16", 16 * 4)
    alloc("nqk", 4 * 4)
    alloc("biasg", 4 * 4)
    alloc("rowbuf", 1024 * 4)
    alloc("binc", 4)
    alloc("b2c", 4)
    alloc("b1T", 8 * 4)
    alloc("bnd", 4)
    alloc("bndb", 2)
    total_bytes = (off + 31) // 32 * 32

    arena_t = nc.alloc_sbuf_tensor("arena", [128, total_bytes], U8)
    arena = arena_t.ap()

    def view(name, shape, dtype, parts=128, p0=0):
        esz = 4 if dtype == F32 else 2
        n = 1
        for s in shape:
            n *= s
        a = arena[p0:p0 + parts, OFF[name]:OFF[name] + n * esz].bitcast(dtype)
        if len(shape) > 1:
            pat = " ".join(f"s{i}" for i in range(len(shape)))
            kw = {f"s{i}": shape[i] for i in range(len(shape))}
            a = a.rearrange(f"p ({pat}) -> p {pat}", **kw)
        return a

    x_s = view("x", [16, 1024], F32)
    xlnT = view("xlnT", [8, 2048], BF16)
    heads = view("heads", [4, 2048], BF16)
    qkTb = view("qkT", [2, 4, 2048], BF16)
    vsb = view("vs", [2, 16, 256], BF16)
    wqkv = view("wqkv", [8, 768], BF16)
    hT = view("wqkv", [8, 512], BF16)
    woTs = view("woT", [2, 2, 1024], BF16)
    pt3_ = view("pt", [3, 1024], BF16)
    ptx_ = arena[:, OFF["ffnw"]:OFF["ffnw"] + 2048].bitcast(BF16)

    def pt_s(j):
        return pt3_[:, j, :] if j < 3 else ptx_
    xlnsc = view("xlnsc", [2, 1024], BF16)
    sqb = view("xlnsc", [2048], BF16)
    rd = view("rd", [512], F32)
    win_s = view("ffnw", [8, 64], BF16)
    w1blk_s = view("w1blk", [1024], BF16, parts=64)
    w2blk_s = view("w2blk", [8, 64], BF16)
    woutA_s = view("woutA", [1024], BF16, parts=65)
    sT = view("vs", [2048], BF16, parts=64)
    gT_full = arena[0:65, OFF["vs"] + 4096:OFF["vs"] + 8192].bitcast(BF16)
    onesb = view("onesb", [128], BF16)
    identb = view("identb", [128], BF16)
    sel2 = view("sel2", [2], BF16)
    bqkvT = view("bqkvT", [16], F32)
    bq32 = view("bq32", [8], F32)
    bvbf = view("bvbf", [8], BF16)
    ssum = view("ssum", [16], F32)
    ssq = view("ssq", [16], F32)
    sm = view("sm", [16], F32)
    sv = view("sv", [16], F32)
    sr = view("sr", [16], F32)
    smq = view("smq", [16], F32)
    sp1 = view("sp1", [16], F32)
    smr = view("smr", [16], F32)
    n16 = view("n16", [16], F32)
    nqk = view("nqk", [4], F32)
    biasg = view("biasg", [4], F32)
    bo_s = view("rowbuf", [1024], F32, parts=1, p0=0)
    bo2b = arena[0:1, OFF["vs"]:OFF["vs"] + 2048].bitcast(BF16)
    binc_s = view("binc", [1], F32, parts=64)
    b2c_s = view("b2c", [1], F32, parts=64)
    b1T_s = view("b1T", [8], F32)
    bnd_s = view("bnd", [1], F32, parts=1)
    bndb_s = view("bndb", [1], BF16, parts=1)

    ps_a = nc.alloc_psum_tensor("ps_a", [128, 1024], F32).ap()
    ps_b = nc.alloc_psum_tensor("ps_b", [128, 1024], F32).ap()
    ps_c = nc.alloc_psum_tensor("ps_c", [128, 512], F32).ap()
    ps_d = nc.alloc_psum_tensor("ps_d", [128, 512], F32).ap()
    ps_e = nc.alloc_psum_tensor("ps_e", [128, 512], F32).ap()
    ps_f = nc.alloc_psum_tensor("ps_f", [128, 512], F32).ap()
    sc_slot = [ps_a, ps_b]
    ps_eb = ps_e.bitcast(BF16)
    ps_fb = ps_f.bitcast(BF16)

    p = Prog()

    for i in range(NT):
        p.emit("sync", lambda e, i=i: e.dma_start(x_s[:, i, :], x_tiles_h[i]),
               inc=("sx", 16))
    for dst, srch in ((bqkvT, bqkvT_h), (bo_s, bo_h), (binc_s, binc_h),
                      (b2c_s, b2c_h), (b1T_s, b1T_h), (onesb, onesb_h),
                      (identb, identb_h), (sel2, sel2_h)):
        p.emit("sync", lambda e, d=dst, s=srch: e.dma_start(d[:, :], s),
               inc=("swB", 16))
    swB_done = p.val("swB")
    p.emit("gpsimd", lambda e: e.dma_start(bvbf[:, :], bvTp_h), inc=("swA", 16))

    def emit_wqkv_dma(g, extra=()):
        for c in range(ND):
            src = wq_view3[128 * c:128 * c + 128, :, 256 * g:256 * g + 256]
            dst = wqkv[:, c, :].rearrange("p (r e) -> p r e", r=3)
            p.emit("gpsimd", lambda e, s=src, d=dst: e.dma_start(d, s),
                   waits=list(extra) if c == 0 else (), inc=("swq", 16))
        return p.val("swq")

    def emit_woT_dma(g, extra=()):
        for j, c in enumerate((2 * g, 2 * g + 1)):
            p.emit("gpsimd", lambda e, g=g, j=j, c=c: e.dma_start(
                woTs[:, g % 2, j, :], woTp_h[128 * c:128 * c + 128, :]),
                waits=list(extra) if j == 0 else (), inc=("swo", 16))
        return p.val("swo")

    wqkv_done = {0: emit_wqkv_dma(0)}
    p.emit("vector", lambda e: e.tensor_scalar_mul(bq32[:, :], bqkvT[:, 0:8], 1.0 / 32.0),
           waits=[("swB", swB_done)], inc=("smisc", 1))

    sq_psum = [ps_a, ps_b]

    def emit_ln(is_ln2, tile_waits, tr_banks=None, dma_tr=False):
        for i in range(NT):
            p.emit("scalar", lambda e, i=i: e.activation(
                sq_psum[i % 2][:, :], x_s[:, i, :], ACTF.Square,
                accum_out=ssq[:, i:i + 1]), waits=tile_waits(i), inc=("ssq1", 1))
        for i in range(NT):
            p.emit("scalar", lambda e, i=i: e.activation(
                sq_psum[i % 2][:, :], x_s[:, i, :], ACTF.Copy,
                accum_out=ssum[:, i:i + 1]), waits=tile_waits(i), inc=("ssq1", 1))
        p.emit("vector", lambda e: e.tensor_scalar_mul(
            sm[:, :], ssum[:, :], 1.0 / 1024.0),
            waits=[("ssq1", p.val("ssq1"))], inc=("sln", 1))
        p.emit("vector", lambda e: e.tensor_scalar(
            sv[:, :], ssq[:, :], 1.0 / 1024.0, EPS, ALU.mult, ALU.add),
            waits=[("ssq1", p.val("ssq1"))], inc=("sln", 1))
        p.emit("vector", lambda e: e.tensor_mul(smq[:, :], sm[:, :], sm[:, :]),
               waits=[("sln", p.val("sln"))], inc=("sln", 1))
        p.emit("vector", lambda e: e.tensor_sub(sv[:, :], sv[:, :], smq[:, :]),
               waits=[("sln", p.val("sln"))], inc=("sln", 1))
        p.emit("scalar", lambda e: e.activation(sv[:, :], sv[:, :], ACTF.Ln),
               waits=[("sln", p.val("sln"))], inc=("ssq2", 1))
        p.emit("scalar", lambda e: e.activation(sr[:, :], sv[:, :], ACTF.Exp, scale=-0.5),
               waits=[("ssq2", p.val("ssq2"))], inc=("ssq2", 1))
        p.emit("vector", lambda e: e.engine_nop(),
               waits=[("ssq2", p.val("ssq2"))], inc=("sln", 1))
        if not is_ln2:
            p.emit("vector", lambda e: e.tensor_scalar(
                sp1[:, :], sr[:, :], 1.0, None, ALU.add),
                waits=[("sln", p.val("sln"))], inc=("sln", 1))
            p.emit("vector", lambda e: e.tensor_mul(smr[:, :], sm[:, :], sr[:, :]),
                   waits=[("sln", p.val("sln"))], inc=("sln", 1))
        stats_done = p.val("sln")
        tr_after = {}
        tr_sem = "strd" if dma_tr else "str"
        for i in range(NT):
            aw = [("sln", stats_done)]
            if i >= 2:
                aw.append((tr_sem, tr_after[i - 2]))
            p.emit("vector", lambda e, i=i: e.tensor_scalar(
                xlnsc[:, i % 2, :], x_s[:, i, :], sm[:, i:i + 1], sr[:, i:i + 1],
                ALU.subtract, ALU.mult), waits=aw, inc=("sln", 1))
            lnsc_done = p.val("sln")
            if not is_ln2:
                p.emit("vector", lambda e, i=i: e.tensor_scalar(
                    x_s[:, i, :], x_s[:, i, :], sp1[:, i:i + 1], smr[:, i:i + 1],
                    ALU.mult, ALU.subtract), inc=("sln", 1))
            if dma_tr:
                for c in range(ND):
                    qn = "scalar" if c % 2 else "sync"
                    p.emit(qn, lambda e, i=i, c=c: e.dma_start(
                        xlnT[:, c, 128 * i:128 * (i + 1)],
                        xlnsc[:, i % 2, 128 * c:128 * (c + 1)], transpose=True),
                        waits=[("sln", lnsc_done)] if c < 2 else (),
                        inc=("strd", 16))
                tr_after[i] = p.val("strd")
                rec = ("strd", p.val("strd"))
            else:
                nb = len(tr_banks)
                pu = tr_banks[i % nb]
                trw = [("sln", lnsc_done)]
                if i >= nb:
                    trw.append(("strev", p.val("strev") - (nb - 1)))
                for c in range(ND):
                    p.emit("tensor", lambda e, i=i, c=c, pu=pu: e.transpose(
                        pu[:, 128 * c:128 * (c + 1)],
                        xlnsc[:, i % 2, 128 * c:128 * (c + 1)],
                        identb[:, :]), waits=trw if c == 0 else (), inc=("str", 1))
                tr_after[i] = p.val("str")
                p.emit("vector", lambda e, i=i, pu=pu: e.tensor_copy(
                    xlnT[:, :, 128 * i:128 * (i + 1)],
                    pu[:, 0:1024].rearrange("p (c t) -> p c t", c=8)),
                    waits=[("str", p.val("str"))], inc=("strev", 1))
                rec = ("strev", p.val("strev"))
            tr_after.setdefault(i, None)
            if not dma_tr:
                tr_after[i] = p.val("str")
            yield_rec = rec
            tile_recs.append(yield_rec)

    tile_recs = []
    ps_ab = ps_a.bitcast(BF16)
    ps_bb = ps_b.bitcast(BF16)
    tile_recs.clear()
    emit_ln(False, lambda i: [("sx", 16 * (i + 1)), ("swB", swB_done)],
            tr_banks=[ps_ab, ps_bb])
    ln1_tr = {i: tile_recs[i][1] for i in range(NT)}
    ln1_strev = p.val("strev")

    vdone = {}

    def qkv_chunks(g, banks=None):
        banks = banks or [ps_f]
        nb = len(banks)
        qkT = qkTb[:, g % 2]
        vs = vsb[:, g % 2]

        def mk_qk(d, ts_):
            def emit():
                u = banks[p.val("suu") % nb]
                uw = [("sue", max(0, p.val("sue") - (nb - 1))),
                      ("swq", wqkv_done[g]),
                      ("strev", ln1_tr[4 * ts_ + 3])]
                r = 0 if d < 2 else 1
                sub = d % 2
                for c in range(ND):
                    lhsT = wqkv[:, c, :].rearrange(
                        "p (r e) -> p r e", r=3)[:, r, 128 * sub:128 * (sub + 1)]
                    rhs = xlnT[:, c, QS * ts_:QS * (ts_ + 1)]
                    p.emit("tensor", lambda e, l=lhsT, r_=rhs, c=c, u=u: e.matmul(
                        u[:, :], l, r_, start=(c == 0), stop=(c == ND - 1)),
                        waits=uw if c == 0 else (),
                        inc=("suu", 1) if c == ND - 1 else None)
                mmd = p.val("suu")
                if d < 2:
                    p.emit("vector", lambda e, d=d, t=ts_, u=u: e.tensor_scalar(
                        qkT[:, d, QS * t:QS * (t + 1)], u[:, :], 1.0 / 32.0,
                        bq32[:, 2 * g + d:2 * g + d + 1], ALU.mult, ALU.add),
                        waits=[("suu", mmd)], inc=("sue", 1))
                else:
                    p.emit("vector", lambda e, d=d, t=ts_, u=u: e.tensor_scalar(
                        qkT[:, d, QS * t:QS * (t + 1)], u[:, :], 1.0,
                        bqkvT[:, 8 + 2 * g + (d - 2):8 + 2 * g + (d - 2) + 1],
                        ALU.mult, ALU.add),
                        waits=[("suu", mmd)], inc=("sue", 1))
            return emit

        def mk_v(i):
            def emit():
                u = banks[p.val("suu") % nb]
                uw = [("sue", max(0, p.val("sue") - (nb - 1))),
                      ("swq", wqkv_done[g]),
                      ("strev", ln1_tr[i])]
                for c in range(ND):
                    lhsT = xlnT[:, c, 128 * i:128 * (i + 1)]
                    rhs = wqkv[:, c, :].rearrange("p (r e) -> p r e", r=3)[:, 2, :]
                    p.emit("tensor", lambda e, l=lhsT, r_=rhs, c=c, u=u: e.matmul(
                        u[:, 0:256], l, r_, start=(c == 0), stop=(c == ND - 1)),
                        waits=uw if c == 0 else (),
                        inc=("suu", 1) if c == ND - 1 else None)
                mmd = p.val("suu")
                p.emit("vector", lambda e, i=i, u=u: e.tensor_copy(
                    vs[:, i, :], u[:, 0:256]),
                    waits=[("suu", mmd)], inc=("sue", 1))
                vdone[(g, i)] = p.val("sue")
            return emit

        qk = [(lambda: True, mk_qk(d, ts_)) for d in range(4) for ts_ in range(NQ)]
        vv = [(lambda: True, mk_v(i)) for i in range(NT)]
        return qk, vv

    def bound_chunks(g, banks=None):
        banks = banks or [ps_f]
        nb = len(banks)
        qkT = qkTb[:, g % 2]
        chunks = []

        def mk_sq(ci):
            def emit():
                p.emit("gpsimd", lambda e, ci=ci: e.tensor_tensor(
                    sqb[:, :], qkT[:, ci, :], qkT[:, ci, :], ALU.mult),
                    waits=[("sue", p.val("sue")), ("sgq", p.val("sgq")),
                           ("sln", p.val("sln"))],
                    inc=("sgq", 1))
            return emit

        def mk_mm(ci, j):
            def emit():
                u = banks[p.val("suu") % nb]
                w = [("sgq", p.val("sgq")),
                     ("sue", max(0, p.val("sue") - (nb - 1))),
                     ("sbv", p.val("sbv"))]
                p.emit("tensor", lambda e, j=j, u=u: e.matmul(
                    u[0:2, :], sel2[:, :], sqb[:, 512 * j:512 * (j + 1)],
                    start=True, stop=True), waits=w, inc=("suu", 1))
                p.emit("vector", lambda e, ci=ci, j=j, u=u: e.reduce_max(
                    n16[0:2, 4 * ci + j:4 * ci + j + 1], u[0:2, :], axis=AX),
                    waits=[("suu", p.val("suu"))], inc=("sbv", 1))
                p.emit("vector", lambda e: e.engine_nop(),
                       waits=[("sbv", p.val("sbv"))], inc=("sue", 1))
            return emit

        def mk_fin():
            def emit():
                p.emit("vector", lambda e: e.reduce_max(
                    nqk[0:2, 0:1], n16[0:2, 0:8], axis=AX),
                    waits=[("sbv", p.val("sbv"))], inc=("sbv", 1))
                p.emit("vector", lambda e: e.reduce_max(
                    nqk[0:2, 1:2], n16[0:2, 8:16], axis=AX),
                    waits=[("sbv", p.val("sbv"))], inc=("sbv", 1))
                shmask = [1, 0] + list(range(2, 32))
                p.emit("vector", lambda e, m=shmask: e.stream_shuffle(
                    nqk[0:2, 2:4], nqk[0:2, 0:2], m),
                    waits=[("sbv", p.val("sbv"))], inc=("sbv", 1))
                p.emit("vector", lambda e: e.tensor_max(
                    nqk[0:1, 0:2], nqk[0:1, 0:2], nqk[0:1, 2:4]),
                    waits=[("sbv", p.val("sbv"))], inc=("sbv", 1))
                p.emit("vector", lambda e: e.tensor_mul(
                    bnd_s[:, :], nqk[0:1, 0:1], nqk[0:1, 1:2]),
                    waits=[("sbv", p.val("sbv"))], inc=("sbv", 1))
                p.emit("scalar", lambda e: e.activation(
                    bnd_s[:, :], bnd_s[:, :], ACTF.Ln),
                    waits=[("sbv", p.val("sbv"))], inc=("sbs", 1))
                p.emit("scalar", lambda e: e.activation(
                    bnd_s[:, :], bnd_s[:, :], ACTF.Exp, scale=0.5),
                    waits=[("sbs", p.val("sbs"))], inc=("sbs", 1))
                p.emit("vector", lambda e: e.tensor_scalar(
                    bnd_s[:, :], bnd_s[:, :], -1.0, BSHIFT, ALU.mult, ALU.add),
                    waits=[("sbs", p.val("sbs"))], inc=("sbv", 1))
                p.emit("vector", lambda e: e.tensor_copy(bndb_s[:, :], bnd_s[:, :]),
                       waits=[("sbv", p.val("sbv"))], inc=("sbv", 1))
                u = banks[p.val("suu") % nb]
                uw = [("sbv", p.val("sbv")),
                      ("sue", max(0, p.val("sue") - (nb - 1)))]
                p.emit("tensor", lambda e, u=u: e.matmul(
                    u[:, 0:1], onesb[0:1, :], bndb_s[:, :],
                    start=True, stop=True), waits=uw, inc=("suu", 1))
                p.emit("vector", lambda e, g=g, u=u: e.tensor_copy(
                    biasg[:, g:g + 1], u[:, 0:1]),
                    waits=[("suu", p.val("suu"))], inc=("sue", 1))
                p.emit("vector", lambda e: e.engine_nop(),
                       waits=[("sue", p.val("sue"))], inc=("sbnd", 1))
            return emit

        for ci in range(4):
            chunks.append((lambda: True, mk_sq(ci)))
            for j in range(4):
                chunks.append((lambda: True, mk_mm(ci, j)))
        chunks.append((lambda: True, mk_fin()))
        return chunks

    nrm_done = {}
    opadd = {}
    woT_done = {}

    def partial_chunks(g, with_bias, banks=None):
        banks = banks or [ps_f]
        nb = len(banks)
        use_e = any(b is ps_e for b in banks)
        chunks = []
        hslots = ((2 * g) % 4, (2 * g + 1) % 4)

        def mk(t, half):
            key = g * NQ + t // 4
            def ready():
                return key in nrm_done
            def emit():
                u = banks[p.val("suu") % nb]
                uw = [("sue", max(0, p.val("sue") - (nb - 1))),
                      ("swo", woT_done[g]),
                      ("snrm", nrm_done[key])]
                for j in range(2):
                    p.emit("tensor", lambda e, j=j, t=t, h=half, u=u: e.matmul(
                        u[:, :], heads[:, hslots[j], 128 * t:128 * (t + 1)],
                        woTs[:, g % 2, j, 512 * h:512 * (h + 1)],
                        start=(j == 0), stop=(j == 1 and not with_bias)),
                        waits=uw if j == 0 else (),
                        inc=("suu", 1) if (j == 1 and not with_bias) else None)
                if with_bias:
                    p.emit("tensor", lambda e, t=t, h=half, u=u: e.matmul(
                        u[:, :], onesb[0:1, :], bo2b[:, 512 * h:512 * (h + 1)],
                        start=False, stop=True),
                        waits=[("sbob", 1)], inc=("suu", 1))
                mmd = p.val("suu")
                p.emit("vector", lambda e, t=t, h=half, u=u: e.tensor_add(
                    x_s[:, t, 512 * h:512 * (h + 1)],
                    x_s[:, t, 512 * h:512 * (h + 1)], u[:, :]),
                    waits=[("suu", mmd)], inc=("sue", 1))
                opadd[(g, t, half)] = p.val("sue")
            return (ready, emit)

        def mk_bv():
            def emit():
                for half in range(2):
                    u = banks[p.val("suu") % nb]
                    uw = [("sue", max(0, p.val("sue") - (nb - 1))),
                          ("swo", woT_done[g])]
                    if use_e:
                        uw.append(("snrm", nrm_done[g * NQ + NQ - 1]))
                    for j in range(2):
                        p.emit("tensor", lambda e, j=j, h=half, u=u: e.matmul(
                            u[0:1, :], bvbf[:, 2 * g + j:2 * g + j + 1],
                            woTs[:, g % 2, j, 512 * h:512 * (h + 1)],
                            start=(j == 0), stop=(j == 1)),
                            waits=uw if j == 0 else (),
                            inc=("suu", 1) if j == 1 else None)
                    p.emit("vector", lambda e, h=half, u=u: e.tensor_add(
                        bo_s[:, 512 * h:512 * (h + 1)],
                        bo_s[:, 512 * h:512 * (h + 1)], u[0:1, :]),
                        waits=[("suu", p.val("suu")), ("sbvv", p.val("sbvv")),
                               ("swB", swB_done)],
                        inc=("sue", 1))
                    p.emit("vector", lambda e: e.engine_nop(),
                           waits=[("sue", p.val("sue"))], inc=("sbvv", 1))
            return (lambda: True, emit)

        chunks.append(mk_bv())
        if with_bias:
            def cast_emit():
                p.emit("vector", lambda e: e.tensor_copy(bo2b[:, :], bo_s[:, :]),
                       waits=[("sbvv", p.val("sbvv"))], inc=("sbob", 1))
            chunks.append((lambda: True, cast_emit))
        for t in range(NT):
            for half in range(2):
                chunks.append(mk(t, half))
        return chunks

    exp_done = {}
    pv_done = {}

    def emit_attn_group(g, side):
        qkT = qkTb[:, g % 2]
        vs = vsb[:, g % 2]
        base_pti = g * NQ * NK * 2
        si = 0
        total_iters = NQ * NK

        def emit_scores_exp(q, k):
            for pr in range(2):
                pti = base_pti + (q * NK + k) * 2 + pr
                slot = pti % 2
                pslot = pti % 4
                w = [("sbnd", g + 1)]
                if pti >= 2:
                    w.append(("sexp", exp_done[pti - 2]))
                if pti == 0:
                    w.append(("ssq1", 16))
                p.emit("tensor", lambda e, s=slot, pr=pr, k=k, q=q, qkT=qkT: e.matmul(
                    sc_slot[s][:, 0:512],
                    qkT[0:64, 2 + pr, 128 * k:128 * (k + 1)],
                    qkT[0:64, pr, QS * q:QS * (q + 1)],
                    start=True, stop=True, tile_position=(0, 0)),
                    waits=w, inc=None)
                p.emit("tensor", lambda e, s=slot, pr=pr, k=k, q=q, qkT=qkT: e.matmul(
                    sc_slot[s][:, 512:1024],
                    qkT[64:128, 2 + pr, 128 * k:128 * (k + 1)],
                    qkT[64:128, pr, QS * q:QS * (q + 1)],
                    start=True, stop=True, tile_position=(64, 0)),
                    inc=("ssc", 1))
                scd = p.val("ssc")
                ew = [("ssc", scd)]
                if pti >= 4:
                    ew.append(("spv", pv_done[pti - 4]))
                p.emit("scalar", lambda e, s=slot, ps_=pslot, g=g: e.activation(
                    pt_s(ps_), sc_slot[s][:, :], ACTF.Exp,
                    bias=biasg[:, g:g + 1], scale=1.0),
                    waits=ew, inc=("sexp", 1))
                exp_done[pti] = p.val("sexp")

        def emit_v_dn(q, k):
            gq = g * NQ + q
            vw = [("sexp", exp_done[base_pti + (q * NK + k) * 2 + 1]),
                  ("sue", vdone[(g, k)])]
            if k == 0 and gq > 0:
                vw.append(("snrm", nrm_done[gq - 1]))
            if k == 0 and gq == 0:
                vw.append(("sue", head_sue))
            sA = (base_pti + (q * NK + k) * 2) % 4
            sB = (sA + 1) % 4
            for half, po in ((0, ps_c), (1, ps_d)):
                for a in range(4):
                    ps_slot = sA if a < 2 else sB
                    p.emit("tensor", lambda e, po=po, a=a, k=k, half=half, s=ps_slot, vs=vs: e.matmul(
                        po[32 * a:32 * (a + 1), :],
                        vs[:, k, 128 * half + 32 * a:128 * half + 32 * (a + 1)],
                        pt_s(s)[:, 512 * (a % 2):512 * (a % 2) + 512],
                        start=(k == 0), stop=(k == NK - 1),
                        tile_position=(0, 32 * a)),
                        waits=vw if (half == 0 and a == 0) else (), inc=None)
            for a in range(4):
                ps_slot = sA if a < 2 else sB
                p.emit("tensor", lambda e, a=a, k=k, s=ps_slot: e.matmul(
                    ps_e[32 * a:32 * (a + 1), :],
                    onesb[:, 0:32],
                    pt_s(s)[:, 512 * (a % 2):512 * (a % 2) + 512],
                    start=(k == 0), stop=(k == NK - 1),
                    tile_position=(0, 32 * a)),
                    inc=("spv", 1) if a == 3 else None)
            pv_done[base_pti + (q * NK + k) * 2] = p.val("spv")
            pv_done[base_pti + (q * NK + k) * 2 + 1] = p.val("spv")

        def emit_norm(q):
            gq = g * NQ + q
            lastpv = pv_done[base_pti + (q * NK + NK - 1) * 2]
            p.emit("vector", lambda e: e.reciprocal(rd[:, :], ps_e[:, :]),
                   waits=[("spv", lastpv)], inc=("snv", 1))
            p.emit("vector", lambda e, g=g, q=q: e.tensor_mul(
                heads[:, (2 * g) % 4, QS * q:QS * (q + 1)], ps_c[:, :], rd[:, :]),
                waits=[("snv", p.val("snv"))], inc=("snv", 1))
            p.emit("vector", lambda e, g=g, q=q: e.tensor_mul(
                heads[:, (2 * g + 1) % 4, QS * q:QS * (q + 1)], ps_d[:, :], rd[:, :]),
                inc=("snrm", 1))
            nrm_done[gq] = p.val("snrm")

        steps = [(q, k) for q in range(NQ) for k in range(NK)]
        for idx in range(total_iters + 1):
            if idx < total_iters:
                emit_scores_exp(*steps[idx])
            if idx >= 1:
                qp, kp = steps[idx - 1]
                emit_v_dn(qp, kp)
                if kp == NK - 1:
                    emit_norm(qp)
            target = ((idx + 1) * len(side) + total_iters) // (total_iters + 1)
            while si < min(len(side), target) and side[si][0]():
                side[si][1]()
                si += 1
        while si < len(side):
            assert side[si][0]()
            side[si][1]()
            si += 1

    HEAD2B = bool(int(os.environ.get("MK_HEAD2B", "1")))
    hqk, hv = qkv_chunks(0, banks=[ps_f, ps_e] if HEAD2B else [ps_f])
    for _, ch in hqk:
        ch()
    for _, ch in bound_chunks(0, banks=[ps_f, ps_e] if HEAD2B else [ps_f]):
        ch()
    for _, ch in hv:
        ch()
    head_sue = p.val("sue")

    for g in range(NG):
        side = []
        if g + 1 < NG:
            wqkv_done[g + 1] = emit_wqkv_dma(g + 1, extra=[("suu", p.val("suu"))])
        if g < NG - 1:
            woT_done[g] = emit_woT_dma(
                g, extra=[("sue", opadd.get((g - 2, NT - 1, 1), 0))])
        if g >= 1:
            side += partial_chunks(g - 1, with_bias=False)
        if g + 1 < NG:
            qk, vv = qkv_chunks(g + 1)
            side += qk
            side += bound_chunks(g + 1)
            side += vv
        if g == NG - 1:
            woT_done[3] = emit_woT_dma(
                3, extra=[("sue", opadd.get((1, NT - 1, 1), 0))])
            side += partial_chunks(3, with_bias=True)
        emit_attn_group(g, side)
    attn_done = p.val("snrm")

    p.emit("gpsimd", lambda e: e.dma_start(
        win_s[:, :, :], win_h.rearrange("(c p) m -> p c m", p=128)),
        waits=[("spv", p.val("spv"))], inc=("swA", 16))
    p.emit("gpsimd", lambda e: e.dma_start(w1blk_s[:, :], w1blk_h), inc=("swA", 16))
    p.emit("gpsimd", lambda e: e.dma_start(
        w2blk_s[:, :, :], w2blk_h.rearrange("(c p) m -> p c m", p=128)), inc=("swA", 16))
    p.emit("gpsimd", lambda e: e.dma_start(woutA_s[:, :], woutA_h), inc=("swA", 16))
    swA_ffn = p.val("swA")
    op_all = p.val("sue")

    def ln2_waits(i):
        return [("sue", opadd[(3, i, 1)])]

    tile_recs.clear()
    emit_ln(True, ln2_waits, tr_banks=[ps_c.bitcast(BF16), ps_d.bitcast(BF16)])
    ln2_tr = {i: ("strev", p.val("strev")) for i in range(NT)}

    p.emit("vector", lambda e: e.memset(gT_full[64:65, :], 1.0),
           waits=[("snrm", attn_done)], inc=("sgone", 1))

    for s in range(NQ):
        sl = slice(QS * s, QS * (s + 1))
        for c in range(ND):
            p.emit("tensor", lambda e, c=c, sl=sl: e.matmul(
                ps_c[0:64, :], win_s[:, c, :], xlnT[:, c, sl],
                start=(c == 0), stop=(c == ND - 1)),
                waits=[ln2_tr[4 * s + 3], ("swA", swA_ffn),
                       ("snv", p.val("snv"))] if c == 0 else (),
                inc=("sff", 1) if c == ND - 1 else None)
        p.emit("vector", lambda e, sl=sl: e.tensor_scalar(
            sT[:, sl], ps_c[0:64, :], 1.0, binc_s[:, :], ALU.mult, ALU.add),
            waits=[("sff", p.val("sff"))], inc=("sfe", 1))
        st_done = p.val("sfe")
        for c in range(ND):
            ph = [ps_a, ps_b][c % 2]
            phv = ph[:, 512 * ((c // 2) % 2):512 * ((c // 2) % 2) + 512]
            w = [("sfe", st_done)]
            if c >= 4:
                w.append(("sfe", p.val("sfe") - 3))
            p.emit("tensor", lambda e, c=c, sl=sl, phv=phv: e.matmul(
                phv, w1blk_s[:, 128 * c:128 * (c + 1)], sT[:, sl],
                start=True, stop=True), waits=w, inc=("sff", 1))
            p.emit("vector", lambda e, c=c, phv=phv: e.tensor_scalar(
                hT[:, c, :], phv, b1T_s[:, c:c + 1], 0.0, ALU.add, ALU.max),
                waits=[("sff", p.val("sff"))], inc=("sfe", 1))
        for c in range(ND):
            p.emit("tensor", lambda e, c=c: e.matmul(
                ps_d[0:64, :], w2blk_s[:, c, :], hT[:, c, :],
                start=(c == 0), stop=(c == ND - 1)),
                waits=[("sfe", p.val("sfe"))] if c == 0 else (),
                inc=("sff", 1) if c == ND - 1 else None)
        p.emit("vector", lambda e, sl=sl: e.tensor_scalar(
            gT_full[0:64, sl], ps_d[0:64, :], 1.0, b2c_s[:, :], ALU.mult, ALU.add),
            waits=[("sff", p.val("sff"))], inc=("sfe", 1))
        gt_done = p.val("sfe")
        for tt in range(4):
            i = 4 * s + tt
            for half in range(2):
                u = [ps_e, ps_f][(2 * i + half) % 2]
                w = [("sfe", gt_done), ("sgone", 1), ("sue", op_all)]
                if p.val("sfev2") >= 1:
                    w.append(("sfev2", p.val("sfev2") - 1))
                p.emit("tensor", lambda e, i=i, h=half, u=u: e.matmul(
                    u[:, :], gT_full[:, 128 * i:128 * (i + 1)],
                    woutA_s[:, 512 * h:512 * (h + 1)],
                    start=True, stop=True), waits=w, inc=("sff", 1))
                p.emit("vector", lambda e, i=i, h=half, u=u: e.tensor_add(
                    x_s[:, i, 512 * h:512 * (h + 1)],
                    x_s[:, i, 512 * h:512 * (h + 1)], u[:, :]),
                    waits=[("sff", p.val("sff"))], inc=("sfev2", 1))
            p.emit("sync", lambda e, i=i: e.dma_start(out_tiles_h[i], x_s[:, i, :]),
                   waits=[("sfev2", p.val("sfev2"))], inc=("sout", 16))

    p.emit("sync", lambda e: None, waits=[("sout", 16 * NT)])

    semh = {}
    from contextlib import ExitStack
    ctx = ExitStack()
    for name in p.counts:
        semh[name] = ctx.enter_context(nc.semaphore(name))
    block = ctx.enter_context(nc.Block())
    with ctx:
        def runner(eng_obj, name):
            for fn, waits, inc in p.ops[name]:
                for s, v in waits:
                    if v > 0:
                        eng_obj.wait_ge(semh[s], v)
                ins = fn(eng_obj)
                if inc is not None:
                    s, amt = inc
                    ins.then_inc(semh[s], amt)

        @block.tensor
        def _(t):
            runner(t, "tensor")

        @block.scalar
        def _(s):
            runner(s, "scalar")

        @block.vector
        def _(v):
            runner(v, "vector")

        @block.sync
        def _(s):
            runner(s, "sync")

        @block.gpsimd
        def _(g):
            runner(g, "gpsimd")

    return nc


_NC_CACHE = {}


def _get_nc():
    key = "v1"
    if key not in _NC_CACHE:
        _NC_CACHE[key] = _build_nc()
    return _NC_CACHE[key]


def _prep_inputs(inputs):
    f32 = np.float32
    x = np.ascontiguousarray(np.asarray(inputs["L0_attn_x"], dtype=f32))
    w_qkv = np.asarray(inputs["L0_attn_w_qkv"], dtype=f32)
    b_qkv = np.asarray(inputs["L0_attn_b_qkv"], dtype=f32)
    w_o = np.asarray(inputs["L0_attn_w_o"], dtype=f32)
    b_o = np.asarray(inputs["L0_attn_b_o"], dtype=f32)
    Win = np.asarray(inputs["L0_ffn_Win"], dtype=f32)
    bin_ = np.asarray(inputs["L0_ffn_bin_"], dtype=f32)
    W1 = np.asarray(inputs["L0_ffn_W1"], dtype=f32)
    b1 = np.asarray(inputs["L0_ffn_b1"], dtype=f32)
    W2 = np.asarray(inputs["L0_ffn_W2"], dtype=f32)
    b2 = np.asarray(inputs["L0_ffn_b2"], dtype=f32)
    Wout = np.asarray(inputs["L0_ffn_Wout"], dtype=f32)
    bout = np.asarray(inputs["L0_ffn_bout"], dtype=f32)

    d_idx = np.arange(1024)
    h = d_idx // 64
    c = d_idx % 64
    newpos = (h // 4) * 256 + (c // 32) * 128 + (h % 4) * 32 + (c % 32)
    src = np.empty(1024, dtype=np.int64)
    src[newpos] = d_idx

    wqkvT = np.ascontiguousarray(w_qkv.T)
    wqkvT_p = wqkvT.copy()
    wqkvT_p[:, 2048:3072] = wqkvT[:, 2048 + src]
    bqkvT = np.ascontiguousarray(b_qkv[:2048].reshape(16, 128).T)
    bvTp = np.ascontiguousarray(b_qkv[2048:][src].reshape(8, 128).T)
    woTp = np.ascontiguousarray(w_o.T[src, :])
    bo = np.ascontiguousarray(b_o.reshape(1, 1024))

    w1blk = np.zeros((64, 1024), dtype=f32)
    mm = np.arange(64)
    for j in range(16):
        w1blk[mm, 16 * mm + j] = W1[:, j]
    b1T = np.ascontiguousarray(b1.reshape(1024).reshape(8, 128).T)
    w2blk = np.zeros((1024, 64), dtype=f32)
    for j in range(16):
        w2blk[16 * mm + j, mm] = W2[:, j]
    woutA = np.ascontiguousarray(np.vstack([Wout, bout.reshape(1, 1024)]))
    binc = np.ascontiguousarray(bin_.reshape(64, 1))
    b2c = np.ascontiguousarray(b2.reshape(64, 1))

    onesb = np.ones((128, 128), dtype=ml_dtypes.bfloat16)
    identb = np.eye(128, dtype=ml_dtypes.bfloat16)
    sel2 = np.zeros((128, 2), dtype=ml_dtypes.bfloat16)
    sel2[0:64, 0] = 1.0
    sel2[64:128, 1] = 1.0

    shared = dict(wqkvT=wqkvT_p, bqkvT=bqkvT, bvTp=bvTp, woTp=woTp, bo=bo,
                  win=np.ascontiguousarray(Win), binc=binc, w1blk=w1blk,
                  b1T=b1T, w2blk=w2blk, b2c=b2c, woutA=woutA,
                  onesb=onesb, identb=identb, sel2=sel2)
    in_maps = []
    for cidx in range(8):
        m = dict(shared)
        m["x"] = np.ascontiguousarray(x[cidx])
        in_maps.append(m)
    return in_maps


def kernel(**inputs):
    in_maps = _prep_inputs(inputs)
    nc = _get_nc()
    res = run_bass_kernel_spmd(nc, in_maps, core_ids=list(range(8)))
    out = np.stack([np.asarray(res.results[cidx]["out"]) for cidx in range(8)])
    return out.astype(np.float32)


# version 07
# speedup vs baseline: 1.1072x; 1.0204x over previous
import os
import numpy as np
import ml_dtypes
import concourse.bass as bass
import concourse.mybir as mybir
from concourse.bass_utils import run_bass_kernel_spmd

F32 = mybir.dt.float32
BF16 = mybir.dt.bfloat16
U8 = mybir.dt.uint8
AX = mybir.AxisListType.X
ALU = mybir.AluOpType
ACTF = mybir.ActivationFunctionType

T = 2048
D = 1024
NT = 16
ND = 8
NG = 4
NQ = 4
NK = 16
QS = 512
EPS = 1e-5
BSHIFT = 50.0

ENGINES = ("tensor", "scalar", "vector", "sync", "gpsimd")


class Prog:
    def __init__(self):
        self.counts = {}
        self.ops = {e: [] for e in ENGINES}

    def emit(self, eng, fn, waits=(), inc=None):
        if inc is not None:
            s, amt = inc
            self.counts[s] = self.counts.get(s, 0) + amt
        self.ops[eng].append((fn, [w for w in waits if w is not None], inc))

    def val(self, s):
        return self.counts.get(s, 0)


def _build_nc():
    nc = bass.Bass("TRN2", target_bir_lowering=False)

    x_h = nc.dram_tensor("x", [T, D], F32, kind="ExternalInput").ap()
    wqkvT_h = nc.dram_tensor("wqkvT", [D, 3 * D], F32, kind="ExternalInput").ap()
    bqkvT_h = nc.dram_tensor("bqkvT", [128, 16], F32, kind="ExternalInput").ap()
    bvTp_h = nc.dram_tensor("bvTp", [128, 8], F32, kind="ExternalInput").ap()
    woTp_h = nc.dram_tensor("woTp", [D, D], F32, kind="ExternalInput").ap()
    bo_h = nc.dram_tensor("bo", [1, D], F32, kind="ExternalInput").ap()
    win_h = nc.dram_tensor("win", [D, 64], F32, kind="ExternalInput").ap()
    binc_h = nc.dram_tensor("binc", [64, 1], F32, kind="ExternalInput").ap()
    w1blk_h = nc.dram_tensor("w1blk", [64, 1024], F32, kind="ExternalInput").ap()
    b1T_h = nc.dram_tensor("b1T", [128, 8], F32, kind="ExternalInput").ap()
    w2blk_h = nc.dram_tensor("w2blk", [1024, 64], F32, kind="ExternalInput").ap()
    b2c_h = nc.dram_tensor("b2c", [64, 1], F32, kind="ExternalInput").ap()
    woutA_h = nc.dram_tensor("woutA", [65, 1024], F32, kind="ExternalInput").ap()
    onesb_h = nc.dram_tensor("onesb", [128, 128], BF16, kind="ExternalInput").ap()
    identb_h = nc.dram_tensor("identb", [128, 128], BF16, kind="ExternalInput").ap()
    sel2_h = nc.dram_tensor("sel2", [128, 2], BF16, kind="ExternalInput").ap()
    out_h = nc.dram_tensor("out", [T, D], F32, kind="ExternalOutput").ap()

    x_tiles_h = x_h.rearrange("(i p) d -> i p d", p=128)
    out_tiles_h = out_h.rearrange("(i p) d -> i p d", p=128)
    wq_view3 = wqkvT_h.rearrange("d (r e) -> d r e", r=3)

    OFF = {}
    off = 0

    def alloc(name, nbytes, align=32):
        nonlocal off
        off = (off + align - 1) // align * align
        OFF[name] = off
        off += nbytes

    alloc("x", 16 * 1024 * 4)
    alloc("xlnT", 8 * 2048 * 2)
    alloc("heads", 4 * 2048 * 2)
    alloc("qkT", 2 * 4 * 2048 * 2)
    alloc("vs", 2 * 16 * 256 * 2)
    alloc("wqkv", 8 * 768 * 2)
    alloc("woT", 2 * 2 * 1024 * 2)
    alloc("pt", 3 * 1024 * 2)
    alloc("xlnsc", 2 * 1024 * 2)
    alloc("rd", 512 * 4)
    alloc("ffnw", 8 * 64 * 2)
    alloc("w1blk", 1024 * 2)
    alloc("w2blk", 8 * 64 * 2)
    alloc("woutA", 1024 * 2)
    alloc("onesb", 128 * 2)
    alloc("identb", 128 * 2)
    alloc("sel2", 2 * 2)
    alloc("bqkvT", 16 * 4)
    alloc("bq32", 8 * 4)
    alloc("bvbf", 8 * 2)
    alloc("ssum", 16 * 4)
    alloc("ssq", 16 * 4)
    alloc("sm", 16 * 4)
    alloc("sv", 16 * 4)
    alloc("sr", 16 * 4)
    alloc("smq", 16 * 4)
    alloc("sp1", 16 * 4)
    alloc("smr", 16 * 4)
    alloc("n16", 16 * 4)
    alloc("nqk", 4 * 4)
    alloc("biasg", 4 * 4)
    alloc("rowbuf", 1024 * 4)
    alloc("binc", 4)
    alloc("b2c", 4)
    alloc("b1T", 8 * 4)
    alloc("bnd", 4)
    alloc("bndb", 2)
    total_bytes = (off + 31) // 32 * 32

    arena_t = nc.alloc_sbuf_tensor("arena", [128, total_bytes], U8)
    arena = arena_t.ap()

    def view(name, shape, dtype, parts=128, p0=0):
        esz = 4 if dtype == F32 else 2
        n = 1
        for s in shape:
            n *= s
        a = arena[p0:p0 + parts, OFF[name]:OFF[name] + n * esz].bitcast(dtype)
        if len(shape) > 1:
            pat = " ".join(f"s{i}" for i in range(len(shape)))
            kw = {f"s{i}": shape[i] for i in range(len(shape))}
            a = a.rearrange(f"p ({pat}) -> p {pat}", **kw)
        return a

    x_s = view("x", [16, 1024], F32)
    xlnT = view("xlnT", [8, 2048], BF16)
    heads = view("heads", [4, 2048], BF16)
    qkTb = view("qkT", [2, 4, 2048], BF16)
    vsb = view("vs", [2, 16, 256], BF16)
    wqkv = view("wqkv", [8, 768], BF16)
    hT = view("wqkv", [8, 512], BF16)
    woTs = view("woT", [2, 2, 1024], BF16)
    pt3_ = view("pt", [3, 1024], BF16)
    ptx_ = arena[:, OFF["ffnw"]:OFF["ffnw"] + 2048].bitcast(BF16)

    def pt_s(j):
        return pt3_[:, j, :] if j < 3 else ptx_
    xlnsc = view("xlnsc", [2, 1024], BF16)
    sqb = view("xlnsc", [2048], BF16)
    rd = view("rd", [512], F32)
    win_s = view("ffnw", [8, 64], BF16)
    w1blk_s = view("w1blk", [1024], BF16, parts=64)
    w2blk_s = view("w2blk", [8, 64], BF16)
    woutA_s = view("woutA", [1024], BF16, parts=65)
    sT = view("vs", [2048], BF16, parts=64)
    gT_full = arena[0:65, OFF["vs"] + 4096:OFF["vs"] + 8192].bitcast(BF16)
    onesb = view("onesb", [128], BF16)
    identb = view("identb", [128], BF16)
    sel2 = view("sel2", [2], BF16)
    bqkvT = view("bqkvT", [16], F32)
    bq32 = view("bq32", [8], F32)
    bvbf = view("bvbf", [8], BF16)
    ssum = view("ssum", [16], F32)
    ssq = view("ssq", [16], F32)
    sm = view("sm", [16], F32)
    sv = view("sv", [16], F32)
    sr = view("sr", [16], F32)
    smq = view("smq", [16], F32)
    sp1 = view("sp1", [16], F32)
    smr = view("smr", [16], F32)
    n16 = view("n16", [16], F32)
    nqk = view("nqk", [4], F32)
    biasg = view("biasg", [4], F32)
    bo_s = view("rowbuf", [1024], F32, parts=1, p0=0)
    bo2b = arena[0:1, OFF["vs"]:OFF["vs"] + 2048].bitcast(BF16)
    binc_s = view("binc", [1], F32, parts=64)
    b2c_s = view("b2c", [1], F32, parts=64)
    b1T_s = view("b1T", [8], F32)
    bnd_s = view("bnd", [1], F32, parts=1)
    bndb_s = view("bndb", [1], BF16, parts=1)

    ps_a = nc.alloc_psum_tensor("ps_a", [128, 1024], F32).ap()
    ps_b = nc.alloc_psum_tensor("ps_b", [128, 1024], F32).ap()
    ps_c = nc.alloc_psum_tensor("ps_c", [128, 512], F32).ap()
    ps_d = nc.alloc_psum_tensor("ps_d", [128, 512], F32).ap()
    ps_e = nc.alloc_psum_tensor("ps_e", [128, 512], F32).ap()
    ps_f = nc.alloc_psum_tensor("ps_f", [128, 512], F32).ap()
    sc_slot = [ps_a, ps_b]
    ps_eb = ps_e.bitcast(BF16)
    ps_fb = ps_f.bitcast(BF16)

    p = Prog()

    for i in range(NT):
        p.emit("sync", lambda e, i=i: e.dma_start(x_s[:, i, :], x_tiles_h[i]),
               inc=("sx", 16))
    for dst, srch in ((bqkvT, bqkvT_h), (bo_s, bo_h), (binc_s, binc_h),
                      (b2c_s, b2c_h), (b1T_s, b1T_h), (onesb, onesb_h),
                      (identb, identb_h), (sel2, sel2_h)):
        p.emit("sync", lambda e, d=dst, s=srch: e.dma_start(d[:, :], s),
               inc=("swB", 16))
    swB_done = p.val("swB")
    p.emit("gpsimd", lambda e: e.dma_start(bvbf[:, :], bvTp_h), inc=("swA", 16))

    def emit_wqkv_dma(g, extra=()):
        for c in range(ND):
            src = wq_view3[128 * c:128 * c + 128, :, 256 * g:256 * g + 256]
            dst = wqkv[:, c, :].rearrange("p (r e) -> p r e", r=3)
            p.emit("gpsimd", lambda e, s=src, d=dst: e.dma_start(d, s),
                   waits=list(extra) if c == 0 else (), inc=("swq", 16))
        return p.val("swq")

    def emit_woT_dma(g, extra=()):
        for j, c in enumerate((2 * g, 2 * g + 1)):
            p.emit("gpsimd", lambda e, g=g, j=j, c=c: e.dma_start(
                woTs[:, g % 2, j, :], woTp_h[128 * c:128 * c + 128, :]),
                waits=list(extra) if j == 0 else (), inc=("swo", 16))
        return p.val("swo")

    wqkv_done = {0: emit_wqkv_dma(0)}
    p.emit("vector", lambda e: e.tensor_scalar_mul(bq32[:, :], bqkvT[:, 0:8], 1.0 / 32.0),
           waits=[("swB", swB_done)], inc=("smisc", 1))

    sq_psum = [ps_a, ps_b]

    def emit_ln(is_ln2, tile_waits, tr_banks=None, dma_tr=False):
        for i in range(NT):
            p.emit("scalar", lambda e, i=i: e.activation(
                sq_psum[i % 2][:, :], x_s[:, i, :], ACTF.Square,
                accum_out=ssq[:, i:i + 1]), waits=tile_waits(i), inc=("ssq1", 1))
        for i in range(NT):
            if i % 2 == 0:
                p.emit("vector", lambda e, i=i: e.reduce_sum(
                    ssum[:, i:i + 1], x_s[:, i, :], axis=AX),
                    waits=tile_waits(i), inc=("ssq1", 1))
            else:
                p.emit("scalar", lambda e, i=i: e.activation(
                    sq_psum[i % 2][:, :], x_s[:, i, :], ACTF.Copy,
                    accum_out=ssum[:, i:i + 1]), waits=tile_waits(i), inc=("ssq1", 1))
        p.emit("vector", lambda e: e.tensor_scalar_mul(
            sm[:, :], ssum[:, :], 1.0 / 1024.0),
            waits=[("ssq1", p.val("ssq1"))], inc=("sln", 1))
        p.emit("vector", lambda e: e.tensor_scalar(
            sv[:, :], ssq[:, :], 1.0 / 1024.0, EPS, ALU.mult, ALU.add),
            waits=[("ssq1", p.val("ssq1"))], inc=("sln", 1))
        p.emit("vector", lambda e: e.tensor_mul(smq[:, :], sm[:, :], sm[:, :]),
               waits=[("sln", p.val("sln"))], inc=("sln", 1))
        p.emit("vector", lambda e: e.tensor_sub(sv[:, :], sv[:, :], smq[:, :]),
               waits=[("sln", p.val("sln"))], inc=("sln", 1))
        p.emit("scalar", lambda e: e.activation(sv[:, :], sv[:, :], ACTF.Ln),
               waits=[("sln", p.val("sln"))], inc=("ssq2", 1))
        p.emit("scalar", lambda e: e.activation(sr[:, :], sv[:, :], ACTF.Exp, scale=-0.5),
               waits=[("ssq2", p.val("ssq2"))], inc=("ssq2", 1))
        p.emit("vector", lambda e: e.engine_nop(),
               waits=[("ssq2", p.val("ssq2"))], inc=("sln", 1))
        if not is_ln2:
            p.emit("vector", lambda e: e.tensor_scalar(
                sp1[:, :], sr[:, :], 1.0, None, ALU.add),
                waits=[("sln", p.val("sln"))], inc=("sln", 1))
            p.emit("vector", lambda e: e.tensor_mul(smr[:, :], sm[:, :], sr[:, :]),
                   waits=[("sln", p.val("sln"))], inc=("sln", 1))
        stats_done = p.val("sln")
        tr_after = {}
        tr_sem = "strd" if dma_tr else "str"
        for i in range(NT):
            aw = [("sln", stats_done)]
            if i >= 2:
                aw.append((tr_sem, tr_after[i - 2]))
            p.emit("vector", lambda e, i=i: e.tensor_scalar(
                xlnsc[:, i % 2, :], x_s[:, i, :], sm[:, i:i + 1], sr[:, i:i + 1],
                ALU.subtract, ALU.mult), waits=aw, inc=("sln", 1))
            lnsc_done = p.val("sln")
            if not is_ln2:
                p.emit("vector", lambda e, i=i: e.tensor_scalar(
                    x_s[:, i, :], x_s[:, i, :], sp1[:, i:i + 1], smr[:, i:i + 1],
                    ALU.mult, ALU.subtract), inc=("sln", 1))
            if dma_tr:
                for c in range(ND):
                    qn = "scalar" if c % 2 else "sync"
                    p.emit(qn, lambda e, i=i, c=c: e.dma_start(
                        xlnT[:, c, 128 * i:128 * (i + 1)],
                        xlnsc[:, i % 2, 128 * c:128 * (c + 1)], transpose=True),
                        waits=[("sln", lnsc_done)] if c < 2 else (),
                        inc=("strd", 16))
                tr_after[i] = p.val("strd")
                rec = ("strd", p.val("strd"))
            else:
                nb = len(tr_banks)
                pu = tr_banks[i % nb]
                trw = [("sln", lnsc_done)]
                if i >= nb:
                    trw.append(("strev", p.val("strev") - (nb - 1)))
                for c in range(ND):
                    p.emit("tensor", lambda e, i=i, c=c, pu=pu: e.transpose(
                        pu[:, 128 * c:128 * (c + 1)],
                        xlnsc[:, i % 2, 128 * c:128 * (c + 1)],
                        identb[:, :]), waits=trw if c == 0 else (), inc=("str", 1))
                tr_after[i] = p.val("str")
                p.emit("vector", lambda e, i=i, pu=pu: e.tensor_copy(
                    xlnT[:, :, 128 * i:128 * (i + 1)],
                    pu[:, 0:1024].rearrange("p (c t) -> p c t", c=8)),
                    waits=[("str", p.val("str"))], inc=("strev", 1))
                rec = ("strev", p.val("strev"))
            tr_after.setdefault(i, None)
            if not dma_tr:
                tr_after[i] = p.val("str")
            yield_rec = rec
            tile_recs.append(yield_rec)

    tile_recs = []
    ps_ab = ps_a.bitcast(BF16)
    ps_bb = ps_b.bitcast(BF16)
    tile_recs.clear()
    emit_ln(False, lambda i: [("sx", 16 * (i + 1)), ("swB", swB_done)],
            tr_banks=[ps_ab, ps_bb])
    ln1_tr = {i: tile_recs[i][1] for i in range(NT)}
    ln1_strev = p.val("strev")

    vdone = {}

    def qkv_chunks(g, banks=None):
        banks = banks or [ps_f]
        nb = len(banks)
        qkT = qkTb[:, g % 2]
        vs = vsb[:, g % 2]

        def mk_qk(d, ts_):
            def emit():
                u = banks[p.val("suu") % nb]
                uw = [("sue", max(0, p.val("sue") - (nb - 1))),
                      ("swq", wqkv_done[g]),
                      ("strev", ln1_tr[4 * ts_ + 3])]
                r = 0 if d < 2 else 1
                sub = d % 2
                for c in range(ND):
                    lhsT = wqkv[:, c, :].rearrange(
                        "p (r e) -> p r e", r=3)[:, r, 128 * sub:128 * (sub + 1)]
                    rhs = xlnT[:, c, QS * ts_:QS * (ts_ + 1)]
                    p.emit("tensor", lambda e, l=lhsT, r_=rhs, c=c, u=u: e.matmul(
                        u[:, :], l, r_, start=(c == 0), stop=(c == ND - 1)),
                        waits=uw if c == 0 else (),
                        inc=("suu", 1) if c == ND - 1 else None)
                mmd = p.val("suu")
                if d < 2:
                    p.emit("vector", lambda e, d=d, t=ts_, u=u: e.tensor_scalar(
                        qkT[:, d, QS * t:QS * (t + 1)], u[:, :], 1.0 / 32.0,
                        bq32[:, 2 * g + d:2 * g + d + 1], ALU.mult, ALU.add),
                        waits=[("suu", mmd)], inc=("sue", 1))
                else:
                    p.emit("vector", lambda e, d=d, t=ts_, u=u: e.tensor_scalar(
                        qkT[:, d, QS * t:QS * (t + 1)], u[:, :], 1.0,
                        bqkvT[:, 8 + 2 * g + (d - 2):8 + 2 * g + (d - 2) + 1],
                        ALU.mult, ALU.add),
                        waits=[("suu", mmd)], inc=("sue", 1))
            return emit

        def mk_v(i):
            def emit():
                u = banks[p.val("suu") % nb]
                uw = [("sue", max(0, p.val("sue") - (nb - 1))),
                      ("swq", wqkv_done[g]),
                      ("strev", ln1_tr[i])]
                for c in range(ND):
                    lhsT = xlnT[:, c, 128 * i:128 * (i + 1)]
                    rhs = wqkv[:, c, :].rearrange("p (r e) -> p r e", r=3)[:, 2, :]
                    p.emit("tensor", lambda e, l=lhsT, r_=rhs, c=c, u=u: e.matmul(
                        u[:, 0:256], l, r_, start=(c == 0), stop=(c == ND - 1)),
                        waits=uw if c == 0 else (),
                        inc=("suu", 1) if c == ND - 1 else None)
                mmd = p.val("suu")
                p.emit("vector", lambda e, i=i, u=u: e.tensor_copy(
                    vs[:, i, :], u[:, 0:256]),
                    waits=[("suu", mmd)], inc=("sue", 1))
                vdone[(g, i)] = p.val("sue")
            return emit

        qk = [(lambda: True, mk_qk(d, ts_)) for d in range(4) for ts_ in range(NQ)]
        vv = [(lambda: True, mk_v(i)) for i in range(NT)]
        return qk, vv

    def bound_chunks(g, banks=None):
        banks = banks or [ps_f]
        nb = len(banks)
        qkT = qkTb[:, g % 2]
        chunks = []

        def mk_sq(ci):
            def emit():
                p.emit("gpsimd", lambda e, ci=ci: e.tensor_tensor(
                    sqb[:, :], qkT[:, ci, :], qkT[:, ci, :], ALU.mult),
                    waits=[("sue", p.val("sue")), ("sgq", p.val("sgq")),
                           ("sln", p.val("sln"))],
                    inc=("sgq", 1))
            return emit

        def mk_mm(ci, j):
            def emit():
                u = banks[p.val("suu") % nb]
                w = [("sgq", p.val("sgq")),
                     ("sue", max(0, p.val("sue") - (nb - 1))),
                     ("sbv", p.val("sbv"))]
                p.emit("tensor", lambda e, j=j, u=u: e.matmul(
                    u[0:2, :], sel2[:, :], sqb[:, 512 * j:512 * (j + 1)],
                    start=True, stop=True), waits=w, inc=("suu", 1))
                p.emit("vector", lambda e, ci=ci, j=j, u=u: e.reduce_max(
                    n16[0:2, 4 * ci + j:4 * ci + j + 1], u[0:2, :], axis=AX),
                    waits=[("suu", p.val("suu"))], inc=("sbv", 1))
                p.emit("vector", lambda e: e.engine_nop(),
                       waits=[("sbv", p.val("sbv"))], inc=("sue", 1))
            return emit

        def mk_fin():
            def emit():
                p.emit("vector", lambda e: e.reduce_max(
                    nqk[0:2, 0:1], n16[0:2, 0:8], axis=AX),
                    waits=[("sbv", p.val("sbv"))], inc=("sbv", 1))
                p.emit("vector", lambda e: e.reduce_max(
                    nqk[0:2, 1:2], n16[0:2, 8:16], axis=AX),
                    waits=[("sbv", p.val("sbv"))], inc=("sbv", 1))
                shmask = [1, 0] + list(range(2, 32))
                p.emit("vector", lambda e, m=shmask: e.stream_shuffle(
                    nqk[0:2, 2:4], nqk[0:2, 0:2], m),
                    waits=[("sbv", p.val("sbv"))], inc=("sbv", 1))
                p.emit("vector", lambda e: e.tensor_max(
                    nqk[0:1, 0:2], nqk[0:1, 0:2], nqk[0:1, 2:4]),
                    waits=[("sbv", p.val("sbv"))], inc=("sbv", 1))
                p.emit("vector", lambda e: e.tensor_mul(
                    bnd_s[:, :], nqk[0:1, 0:1], nqk[0:1, 1:2]),
                    waits=[("sbv", p.val("sbv"))], inc=("sbv", 1))
                p.emit("scalar", lambda e: e.activation(
                    bnd_s[:, :], bnd_s[:, :], ACTF.Ln),
                    waits=[("sbv", p.val("sbv"))], inc=("sbs", 1))
                p.emit("scalar", lambda e: e.activation(
                    bnd_s[:, :], bnd_s[:, :], ACTF.Exp, scale=0.5),
                    waits=[("sbs", p.val("sbs"))], inc=("sbs", 1))
                p.emit("vector", lambda e: e.tensor_scalar(
                    bnd_s[:, :], bnd_s[:, :], -1.0, BSHIFT, ALU.mult, ALU.add),
                    waits=[("sbs", p.val("sbs"))], inc=("sbv", 1))
                p.emit("vector", lambda e: e.tensor_copy(bndb_s[:, :], bnd_s[:, :]),
                       waits=[("sbv", p.val("sbv"))], inc=("sbv", 1))
                u = banks[p.val("suu") % nb]
                uw = [("sbv", p.val("sbv")),
                      ("sue", max(0, p.val("sue") - (nb - 1)))]
                p.emit("tensor", lambda e, u=u: e.matmul(
                    u[:, 0:1], onesb[0:1, :], bndb_s[:, :],
                    start=True, stop=True), waits=uw, inc=("suu", 1))
                p.emit("vector", lambda e, g=g, u=u: e.tensor_copy(
                    biasg[:, g:g + 1], u[:, 0:1]),
                    waits=[("suu", p.val("suu"))], inc=("sue", 1))
                p.emit("vector", lambda e: e.engine_nop(),
                       waits=[("sue", p.val("sue"))], inc=("sbnd", 1))
            return emit

        for ci in range(4):
            chunks.append((lambda: True, mk_sq(ci)))
            for j in range(4):
                chunks.append((lambda: True, mk_mm(ci, j)))
        chunks.append((lambda: True, mk_fin()))
        return chunks

    nrm_done = {}
    opadd = {}
    woT_done = {}

    def partial_chunks(g, with_bias, banks=None):
        banks = banks or [ps_f]
        nb = len(banks)
        use_e = any(b is ps_e for b in banks)
        chunks = []
        hslots = ((2 * g) % 4, (2 * g + 1) % 4)

        def mk(t, half):
            key = g * NQ + t // 4
            def ready():
                return key in nrm_done
            def emit():
                u = banks[p.val("suu") % nb]
                uw = [("sue", max(0, p.val("sue") - (nb - 1))),
                      ("swo", woT_done[g]),
                      ("snrm", nrm_done[key])]
                for j in range(2):
                    p.emit("tensor", lambda e, j=j, t=t, h=half, u=u: e.matmul(
                        u[:, :], heads[:, hslots[j], 128 * t:128 * (t + 1)],
                        woTs[:, g % 2, j, 512 * h:512 * (h + 1)],
                        start=(j == 0), stop=(j == 1 and not with_bias)),
                        waits=uw if j == 0 else (),
                        inc=("suu", 1) if (j == 1 and not with_bias) else None)
                if with_bias:
                    p.emit("tensor", lambda e, t=t, h=half, u=u: e.matmul(
                        u[:, :], onesb[0:1, :], bo2b[:, 512 * h:512 * (h + 1)],
                        start=False, stop=True),
                        waits=[("sbob", 1)], inc=("suu", 1))
                mmd = p.val("suu")
                p.emit("vector", lambda e, t=t, h=half, u=u: e.tensor_add(
                    x_s[:, t, 512 * h:512 * (h + 1)],
                    x_s[:, t, 512 * h:512 * (h + 1)], u[:, :]),
                    waits=[("suu", mmd)], inc=("sue", 1))
                opadd[(g, t, half)] = p.val("sue")
            return (ready, emit)

        def mk_bv():
            def emit():
                for half in range(2):
                    u = banks[p.val("suu") % nb]
                    uw = [("sue", max(0, p.val("sue") - (nb - 1))),
                          ("swo", woT_done[g])]
                    if use_e:
                        uw.append(("snrm", nrm_done[g * NQ + NQ - 1]))
                    for j in range(2):
                        p.emit("tensor", lambda e, j=j, h=half, u=u: e.matmul(
                            u[0:1, :], bvbf[:, 2 * g + j:2 * g + j + 1],
                            woTs[:, g % 2, j, 512 * h:512 * (h + 1)],
                            start=(j == 0), stop=(j == 1)),
                            waits=uw if j == 0 else (),
                            inc=("suu", 1) if j == 1 else None)
                    p.emit("vector", lambda e, h=half, u=u: e.tensor_add(
                        bo_s[:, 512 * h:512 * (h + 1)],
                        bo_s[:, 512 * h:512 * (h + 1)], u[0:1, :]),
                        waits=[("suu", p.val("suu")), ("sbvv", p.val("sbvv")),
                               ("swB", swB_done)],
                        inc=("sue", 1))
                    p.emit("vector", lambda e: e.engine_nop(),
                           waits=[("sue", p.val("sue"))], inc=("sbvv", 1))
            return (lambda: True, emit)

        chunks.append(mk_bv())
        if with_bias:
            def cast_emit():
                p.emit("vector", lambda e: e.tensor_copy(bo2b[:, :], bo_s[:, :]),
                       waits=[("sbvv", p.val("sbvv"))], inc=("sbob", 1))
            chunks.append((lambda: True, cast_emit))
        for t in range(NT):
            for half in range(2):
                chunks.append(mk(t, half))
        return chunks

    exp_done = {}
    pv_done = {}

    def emit_attn_group(g, side):
        qkT = qkTb[:, g % 2]
        vs = vsb[:, g % 2]
        base_pti = g * NQ * NK * 2
        si = 0
        total_iters = NQ * NK

        def emit_scores_exp(q, k):
            for pr in range(2):
                pti = base_pti + (q * NK + k) * 2 + pr
                slot = pti % 2
                pslot = pti % 4
                w = [("sbnd", g + 1)]
                if pti >= 2:
                    w.append(("sexp", exp_done[pti - 2]))
                if pti == 0:
                    w.append(("ssq1", 16))
                p.emit("tensor", lambda e, s=slot, pr=pr, k=k, q=q, qkT=qkT: e.matmul(
                    sc_slot[s][:, 0:512],
                    qkT[0:64, 2 + pr, 128 * k:128 * (k + 1)],
                    qkT[0:64, pr, QS * q:QS * (q + 1)],
                    start=True, stop=True, tile_position=(0, 0)),
                    waits=w, inc=None)
                p.emit("tensor", lambda e, s=slot, pr=pr, k=k, q=q, qkT=qkT: e.matmul(
                    sc_slot[s][:, 512:1024],
                    qkT[64:128, 2 + pr, 128 * k:128 * (k + 1)],
                    qkT[64:128, pr, QS * q:QS * (q + 1)],
                    start=True, stop=True, tile_position=(64, 0)),
                    inc=("ssc", 1))
                scd = p.val("ssc")
                ew = [("ssc", scd)]
                if pti >= 4:
                    ew.append(("spv", pv_done[pti - 4]))
                p.emit("scalar", lambda e, s=slot, ps_=pslot, g=g: e.activation(
                    pt_s(ps_), sc_slot[s][:, :], ACTF.Exp,
                    bias=biasg[:, g:g + 1], scale=1.0),
                    waits=ew, inc=("sexp", 1))
                exp_done[pti] = p.val("sexp")

        def emit_v_dn(q, k):
            gq = g * NQ + q
            vw = [("sexp", exp_done[base_pti + (q * NK + k) * 2 + 1]),
                  ("sue", vdone[(g, k)])]
            if k == 0 and gq > 0:
                vw.append(("snrm", nrm_done[gq - 1]))
            if k == 0 and gq == 0:
                vw.append(("sue", head_sue))
            sA = (base_pti + (q * NK + k) * 2) % 4
            sB = (sA + 1) % 4
            for half, po in ((0, ps_c), (1, ps_d)):
                for a in range(4):
                    ps_slot = sA if a < 2 else sB
                    p.emit("tensor", lambda e, po=po, a=a, k=k, half=half, s=ps_slot, vs=vs: e.matmul(
                        po[32 * a:32 * (a + 1), :],
                        vs[:, k, 128 * half + 32 * a:128 * half + 32 * (a + 1)],
                        pt_s(s)[:, 512 * (a % 2):512 * (a % 2) + 512],
                        start=(k == 0), stop=(k == NK - 1),
                        tile_position=(0, 32 * a)),
                        waits=vw if (half == 0 and a == 0) else (), inc=None)
            for a in range(4):
                ps_slot = sA if a < 2 else sB
                p.emit("tensor", lambda e, a=a, k=k, s=ps_slot: e.matmul(
                    ps_e[32 * a:32 * (a + 1), :],
                    onesb[:, 0:32],
                    pt_s(s)[:, 512 * (a % 2):512 * (a % 2) + 512],
                    start=(k == 0), stop=(k == NK - 1),
                    tile_position=(0, 32 * a)),
                    inc=("spv", 1) if a == 3 else None)
            pv_done[base_pti + (q * NK + k) * 2] = p.val("spv")
            pv_done[base_pti + (q * NK + k) * 2 + 1] = p.val("spv")

        def emit_norm(q):
            gq = g * NQ + q
            lastpv = pv_done[base_pti + (q * NK + NK - 1) * 2]
            p.emit("vector", lambda e: e.reciprocal(rd[:, :], ps_e[:, :]),
                   waits=[("spv", lastpv)], inc=("snv", 1))
            p.emit("vector", lambda e, g=g, q=q: e.tensor_mul(
                heads[:, (2 * g) % 4, QS * q:QS * (q + 1)], ps_c[:, :], rd[:, :]),
                waits=[("snv", p.val("snv"))], inc=("snv", 1))
            p.emit("vector", lambda e, g=g, q=q: e.tensor_mul(
                heads[:, (2 * g + 1) % 4, QS * q:QS * (q + 1)], ps_d[:, :], rd[:, :]),
                inc=("snrm", 1))
            nrm_done[gq] = p.val("snrm")

        steps = [(q, k) for q in range(NQ) for k in range(NK)]
        for idx in range(total_iters + 1):
            if idx < total_iters:
                emit_scores_exp(*steps[idx])
            if idx >= 1:
                qp, kp = steps[idx - 1]
                emit_v_dn(qp, kp)
                if kp == NK - 1:
                    emit_norm(qp)
            target = ((idx + 1) * len(side) + total_iters) // (total_iters + 1)
            while si < min(len(side), target) and side[si][0]():
                side[si][1]()
                si += 1
        while si < len(side):
            assert side[si][0]()
            side[si][1]()
            si += 1

    HEAD2B = bool(int(os.environ.get("MK_HEAD2B", "1")))
    hqk, hv = qkv_chunks(0, banks=[ps_f, ps_e] if HEAD2B else [ps_f])
    for _, ch in hqk:
        ch()
    for _, ch in bound_chunks(0, banks=[ps_f, ps_e] if HEAD2B else [ps_f]):
        ch()
    for _, ch in hv:
        ch()
    head_sue = p.val("sue")

    for g in range(NG):
        side = []
        if g + 1 < NG:
            wqkv_done[g + 1] = emit_wqkv_dma(g + 1, extra=[("suu", p.val("suu"))])
        if g < NG - 1:
            woT_done[g] = emit_woT_dma(
                g, extra=[("sue", opadd.get((g - 2, NT - 1, 1), 0))])
        if g >= 1:
            side += partial_chunks(g - 1, with_bias=False)
        if g + 1 < NG:
            qk, vv = qkv_chunks(g + 1)
            side += qk
            side += bound_chunks(g + 1)
            side += vv
        if g == NG - 1:
            woT_done[3] = emit_woT_dma(
                3, extra=[("sue", opadd.get((1, NT - 1, 1), 0))])
            side += partial_chunks(3, with_bias=True)
        emit_attn_group(g, side)
    attn_done = p.val("snrm")

    p.emit("gpsimd", lambda e: e.dma_start(
        win_s[:, :, :], win_h.rearrange("(c p) m -> p c m", p=128)),
        waits=[("spv", p.val("spv"))], inc=("swA", 16))
    p.emit("gpsimd", lambda e: e.dma_start(w1blk_s[:, :], w1blk_h), inc=("swA", 16))
    p.emit("gpsimd", lambda e: e.dma_start(
        w2blk_s[:, :, :], w2blk_h.rearrange("(c p) m -> p c m", p=128)), inc=("swA", 16))
    p.emit("gpsimd", lambda e: e.dma_start(woutA_s[:, :], woutA_h), inc=("swA", 16))
    swA_ffn = p.val("swA")
    op_all = p.val("sue")

    def ln2_waits(i):
        return [("sue", opadd[(3, i, 1)])]

    tile_recs.clear()
    emit_ln(True, ln2_waits, tr_banks=[ps_eb, ps_fb])
    ln2_tr = {i: ("strev", tile_recs[i][1]) for i in range(NT)}

    p.emit("vector", lambda e: e.memset(gT_full[64:65, :], 1.0),
           waits=[("snrm", attn_done)], inc=("sgone", 1))

    for s in range(NQ):
        sl = slice(QS * s, QS * (s + 1))
        for c in range(ND):
            p.emit("tensor", lambda e, c=c, sl=sl: e.matmul(
                ps_c[0:64, :], win_s[:, c, :], xlnT[:, c, sl],
                start=(c == 0), stop=(c == ND - 1)),
                waits=[ln2_tr[4 * s + 3], ("swA", swA_ffn),
                       ("snv", p.val("snv"))] if c == 0 else (),
                inc=("sff", 1) if c == ND - 1 else None)
        p.emit("vector", lambda e, sl=sl: e.tensor_scalar(
            sT[:, sl], ps_c[0:64, :], 1.0, binc_s[:, :], ALU.mult, ALU.add),
            waits=[("sff", p.val("sff"))], inc=("sfe", 1))
        st_done = p.val("sfe")
        for c in range(ND):
            ph = [ps_a, ps_b][c % 2]
            phv = ph[:, 512 * ((c // 2) % 2):512 * ((c // 2) % 2) + 512]
            w = [("sfe", st_done)]
            if c >= 4:
                w.append(("sfe", p.val("sfe") - 3))
            p.emit("tensor", lambda e, c=c, sl=sl, phv=phv: e.matmul(
                phv, w1blk_s[:, 128 * c:128 * (c + 1)], sT[:, sl],
                start=True, stop=True), waits=w, inc=("sff", 1))
            p.emit("vector", lambda e, c=c, phv=phv: e.tensor_scalar(
                hT[:, c, :], phv, b1T_s[:, c:c + 1], 0.0, ALU.add, ALU.max),
                waits=[("sff", p.val("sff"))], inc=("sfe", 1))
        for c in range(ND):
            p.emit("tensor", lambda e, c=c: e.matmul(
                ps_d[0:64, :], w2blk_s[:, c, :], hT[:, c, :],
                start=(c == 0), stop=(c == ND - 1)),
                waits=[("sfe", p.val("sfe"))] if c == 0 else (),
                inc=("sff", 1) if c == ND - 1 else None)
        p.emit("vector", lambda e, sl=sl: e.tensor_scalar(
            gT_full[0:64, sl], ps_d[0:64, :], 1.0, b2c_s[:, :], ALU.mult, ALU.add),
            waits=[("sff", p.val("sff"))], inc=("sfe", 1))
        gt_done = p.val("sfe")
        wout_slots = [ps_a[:, 0:512], ps_a[:, 512:1024],
                      ps_b[:, 0:512], ps_b[:, 512:1024]]
        for tt in range(4):
            i = 4 * s + tt
            for half in range(2):
                n = 2 * i + half
                u = wout_slots[n % 4]
                w = [("sfe", gt_done), ("sgone", 1), ("sue", op_all),
                     ("ssq1", p.val("ssq1"))]
                if p.val("sfev2") >= 3:
                    w.append(("sfev2", p.val("sfev2") - 3))
                p.emit("tensor", lambda e, i=i, h=half, u=u: e.matmul(
                    u, gT_full[:, 128 * i:128 * (i + 1)],
                    woutA_s[:, 512 * h:512 * (h + 1)],
                    start=True, stop=True), waits=w, inc=("sff", 1))
                p.emit("vector", lambda e, i=i, h=half, u=u: e.tensor_add(
                    x_s[:, i, 512 * h:512 * (h + 1)],
                    x_s[:, i, 512 * h:512 * (h + 1)], u),
                    waits=[("sff", p.val("sff"))], inc=("sfev2", 1))
            p.emit("sync", lambda e, i=i: e.dma_start(out_tiles_h[i], x_s[:, i, :]),
                   waits=[("sfev2", p.val("sfev2"))], inc=("sout", 16))

    p.emit("sync", lambda e: None, waits=[("sout", 16 * NT)])

    semh = {}
    from contextlib import ExitStack
    ctx = ExitStack()
    for name in p.counts:
        semh[name] = ctx.enter_context(nc.semaphore(name))
    block = ctx.enter_context(nc.Block())
    with ctx:
        def runner(eng_obj, name):
            for fn, waits, inc in p.ops[name]:
                for s, v in waits:
                    if v > 0:
                        eng_obj.wait_ge(semh[s], v)
                ins = fn(eng_obj)
                if inc is not None:
                    s, amt = inc
                    ins.then_inc(semh[s], amt)

        @block.tensor
        def _(t):
            runner(t, "tensor")

        @block.scalar
        def _(s):
            runner(s, "scalar")

        @block.vector
        def _(v):
            runner(v, "vector")

        @block.sync
        def _(s):
            runner(s, "sync")

        @block.gpsimd
        def _(g):
            runner(g, "gpsimd")

    return nc


_NC_CACHE = {}


def _get_nc():
    key = "v1"
    if key not in _NC_CACHE:
        _NC_CACHE[key] = _build_nc()
    return _NC_CACHE[key]


def _prep_inputs(inputs):
    f32 = np.float32
    x = np.ascontiguousarray(np.asarray(inputs["L0_attn_x"], dtype=f32))
    w_qkv = np.asarray(inputs["L0_attn_w_qkv"], dtype=f32)
    b_qkv = np.asarray(inputs["L0_attn_b_qkv"], dtype=f32)
    w_o = np.asarray(inputs["L0_attn_w_o"], dtype=f32)
    b_o = np.asarray(inputs["L0_attn_b_o"], dtype=f32)
    Win = np.asarray(inputs["L0_ffn_Win"], dtype=f32)
    bin_ = np.asarray(inputs["L0_ffn_bin_"], dtype=f32)
    W1 = np.asarray(inputs["L0_ffn_W1"], dtype=f32)
    b1 = np.asarray(inputs["L0_ffn_b1"], dtype=f32)
    W2 = np.asarray(inputs["L0_ffn_W2"], dtype=f32)
    b2 = np.asarray(inputs["L0_ffn_b2"], dtype=f32)
    Wout = np.asarray(inputs["L0_ffn_Wout"], dtype=f32)
    bout = np.asarray(inputs["L0_ffn_bout"], dtype=f32)

    d_idx = np.arange(1024)
    h = d_idx // 64
    c = d_idx % 64
    newpos = (h // 4) * 256 + (c // 32) * 128 + (h % 4) * 32 + (c % 32)
    src = np.empty(1024, dtype=np.int64)
    src[newpos] = d_idx

    wqkvT = np.ascontiguousarray(w_qkv.T)
    wqkvT_p = wqkvT.copy()
    wqkvT_p[:, 2048:3072] = wqkvT[:, 2048 + src]
    bqkvT = np.ascontiguousarray(b_qkv[:2048].reshape(16, 128).T)
    bvTp = np.ascontiguousarray(b_qkv[2048:][src].reshape(8, 128).T)
    woTp = np.ascontiguousarray(w_o.T[src, :])
    bo = np.ascontiguousarray(b_o.reshape(1, 1024))

    w1blk = np.zeros((64, 1024), dtype=f32)
    mm = np.arange(64)
    for j in range(16):
        w1blk[mm, 16 * mm + j] = W1[:, j]
    b1T = np.ascontiguousarray(b1.reshape(1024).reshape(8, 128).T)
    w2blk = np.zeros((1024, 64), dtype=f32)
    for j in range(16):
        w2blk[16 * mm + j, mm] = W2[:, j]
    woutA = np.ascontiguousarray(np.vstack([Wout, bout.reshape(1, 1024)]))
    binc = np.ascontiguousarray(bin_.reshape(64, 1))
    b2c = np.ascontiguousarray(b2.reshape(64, 1))

    onesb = np.ones((128, 128), dtype=ml_dtypes.bfloat16)
    identb = np.eye(128, dtype=ml_dtypes.bfloat16)
    sel2 = np.zeros((128, 2), dtype=ml_dtypes.bfloat16)
    sel2[0:64, 0] = 1.0
    sel2[64:128, 1] = 1.0

    shared = dict(wqkvT=wqkvT_p, bqkvT=bqkvT, bvTp=bvTp, woTp=woTp, bo=bo,
                  win=np.ascontiguousarray(Win), binc=binc, w1blk=w1blk,
                  b1T=b1T, w2blk=w2blk, b2c=b2c, woutA=woutA,
                  onesb=onesb, identb=identb, sel2=sel2)
    in_maps = []
    for cidx in range(8):
        m = dict(shared)
        m["x"] = np.ascontiguousarray(x[cidx])
        in_maps.append(m)
    return in_maps


def kernel(**inputs):
    in_maps = _prep_inputs(inputs)
    nc = _get_nc()
    res = run_bass_kernel_spmd(nc, in_maps, core_ids=list(range(8)))
    out = np.stack([np.asarray(res.results[cidx]["out"]) for cidx in range(8)])
    return out.astype(np.float32)


# version 08
# speedup vs baseline: 1.1074x; 1.0002x over previous
import os
import numpy as np
import ml_dtypes
import concourse.bass as bass
import concourse.mybir as mybir
from concourse.bass_utils import run_bass_kernel_spmd

F32 = mybir.dt.float32
BF16 = mybir.dt.bfloat16
U8 = mybir.dt.uint8
AX = mybir.AxisListType.X
ALU = mybir.AluOpType
ACTF = mybir.ActivationFunctionType

T = 2048
D = 1024
NT = 16
ND = 8
NG = 4
NQ = 4
NK = 16
QS = 512
EPS = 1e-5
BSHIFT = 50.0

ENGINES = ("tensor", "scalar", "vector", "sync", "gpsimd")


class Prog:
    def __init__(self):
        self.counts = {}
        self.ops = {e: [] for e in ENGINES}

    def emit(self, eng, fn, waits=(), inc=None):
        if inc is not None:
            s, amt = inc
            self.counts[s] = self.counts.get(s, 0) + amt
        self.ops[eng].append((fn, [w for w in waits if w is not None], inc))

    def val(self, s):
        return self.counts.get(s, 0)


def _build_nc():
    nc = bass.Bass("TRN2", target_bir_lowering=False)

    x_h = nc.dram_tensor("x", [T, D], F32, kind="ExternalInput").ap()
    wqkvT_h = nc.dram_tensor("wqkvT", [D, 3 * D], F32, kind="ExternalInput").ap()
    bqkvT_h = nc.dram_tensor("bqkvT", [128, 16], F32, kind="ExternalInput").ap()
    bvTp_h = nc.dram_tensor("bvTp", [128, 8], F32, kind="ExternalInput").ap()
    woTp_h = nc.dram_tensor("woTp", [D, D], F32, kind="ExternalInput").ap()
    bo_h = nc.dram_tensor("bo", [1, D], F32, kind="ExternalInput").ap()
    win_h = nc.dram_tensor("win", [D, 64], F32, kind="ExternalInput").ap()
    binc_h = nc.dram_tensor("binc", [64, 1], F32, kind="ExternalInput").ap()
    w1blk_h = nc.dram_tensor("w1blk", [64, 1024], F32, kind="ExternalInput").ap()
    b1T_h = nc.dram_tensor("b1T", [128, 8], F32, kind="ExternalInput").ap()
    w2blk_h = nc.dram_tensor("w2blk", [1024, 64], F32, kind="ExternalInput").ap()
    b2c_h = nc.dram_tensor("b2c", [64, 1], F32, kind="ExternalInput").ap()
    woutA_h = nc.dram_tensor("woutA", [65, 1024], F32, kind="ExternalInput").ap()
    onesb_h = nc.dram_tensor("onesb", [128, 128], BF16, kind="ExternalInput").ap()
    identb_h = nc.dram_tensor("identb", [128, 128], BF16, kind="ExternalInput").ap()
    sel2_h = nc.dram_tensor("sel2", [128, 2], BF16, kind="ExternalInput").ap()
    out_h = nc.dram_tensor("out", [T, D], F32, kind="ExternalOutput").ap()

    x_tiles_h = x_h.rearrange("(i p) d -> i p d", p=128)
    out_tiles_h = out_h.rearrange("(i p) d -> i p d", p=128)
    wq_view3 = wqkvT_h.rearrange("d (r e) -> d r e", r=3)

    OFF = {}
    off = 0

    def alloc(name, nbytes, align=32):
        nonlocal off
        off = (off + align - 1) // align * align
        OFF[name] = off
        off += nbytes

    alloc("x", 16 * 1024 * 4)
    alloc("xlnT", 8 * 2048 * 2)
    alloc("heads", 4 * 2048 * 2)
    alloc("qkT", 2 * 4 * 2048 * 2)
    alloc("vs", 2 * 16 * 256 * 2)
    alloc("wqkv", 8 * 768 * 2)
    alloc("woT", 2 * 2 * 1024 * 2)
    alloc("pt", 3 * 1024 * 2)
    alloc("xlnsc", 2 * 1024 * 2)
    alloc("rd", 512 * 4)
    alloc("ffnw", 8 * 64 * 2)
    alloc("w1blk", 1024 * 2)
    alloc("w2blk", 8 * 64 * 2)
    alloc("woutA", 1024 * 2)
    alloc("onesb", 128 * 2)
    alloc("identb", 128 * 2)
    alloc("sel2", 2 * 2)
    alloc("bqkvT", 16 * 4)
    alloc("bq32", 8 * 4)
    alloc("bvbf", 8 * 2)
    alloc("ssum", 16 * 4)
    alloc("ssq", 16 * 4)
    alloc("sm", 16 * 4)
    alloc("sv", 16 * 4)
    alloc("sr", 16 * 4)
    alloc("smq", 16 * 4)
    alloc("sp1", 16 * 4)
    alloc("smr", 16 * 4)
    alloc("n16", 16 * 4)
    alloc("nqk", 4 * 4)
    alloc("biasg", 4 * 4)
    alloc("rowbuf", 1024 * 4)
    alloc("binc", 4)
    alloc("b2c", 4)
    alloc("b1T", 8 * 4)
    alloc("bnd", 4)
    alloc("bndb", 2)
    total_bytes = (off + 31) // 32 * 32

    arena_t = nc.alloc_sbuf_tensor("arena", [128, total_bytes], U8)
    arena = arena_t.ap()

    def view(name, shape, dtype, parts=128, p0=0):
        esz = 4 if dtype == F32 else 2
        n = 1
        for s in shape:
            n *= s
        a = arena[p0:p0 + parts, OFF[name]:OFF[name] + n * esz].bitcast(dtype)
        if len(shape) > 1:
            pat = " ".join(f"s{i}" for i in range(len(shape)))
            kw = {f"s{i}": shape[i] for i in range(len(shape))}
            a = a.rearrange(f"p ({pat}) -> p {pat}", **kw)
        return a

    x_s = view("x", [16, 1024], F32)
    xlnT = view("xlnT", [8, 2048], BF16)
    heads = view("heads", [4, 2048], BF16)
    qkTb = view("qkT", [2, 4, 2048], BF16)
    vsb = view("vs", [2, 16, 256], BF16)
    wqkv = view("wqkv", [8, 768], BF16)
    hT = view("wqkv", [8, 512], BF16)
    woTs = view("woT", [2, 2, 1024], BF16)
    pt3_ = view("pt", [3, 1024], BF16)
    ptx_ = arena[:, OFF["ffnw"]:OFF["ffnw"] + 2048].bitcast(BF16)

    def pt_s(j):
        return pt3_[:, j, :] if j < 3 else ptx_
    xlnsc = view("xlnsc", [2, 1024], BF16)
    sqb = view("xlnsc", [2048], BF16)
    rd = view("rd", [512], F32)
    win_s = view("ffnw", [8, 64], BF16)
    w1blk_s = view("w1blk", [1024], BF16, parts=64)
    w2blk_s = view("w2blk", [8, 64], BF16)
    woutA_s = view("woutA", [1024], BF16, parts=65)
    sT = view("vs", [2048], BF16, parts=64)
    gT_full = arena[0:65, OFF["vs"] + 4096:OFF["vs"] + 8192].bitcast(BF16)
    onesb = view("onesb", [128], BF16)
    identb = view("identb", [128], BF16)
    sel2 = view("sel2", [2], BF16)
    bqkvT = view("bqkvT", [16], F32)
    bq32 = view("bq32", [8], F32)
    bvbf = view("bvbf", [8], BF16)
    ssum = view("ssum", [16], F32)
    ssq = view("ssq", [16], F32)
    sm = view("sm", [16], F32)
    sv = view("sv", [16], F32)
    sr = view("sr", [16], F32)
    smq = view("smq", [16], F32)
    sp1 = view("sp1", [16], F32)
    smr = view("smr", [16], F32)
    n16 = view("n16", [16], F32)
    nqk = view("nqk", [4], F32)
    biasg = view("biasg", [4], F32)
    bo_s = view("rowbuf", [1024], F32, parts=1, p0=0)
    bo2b = arena[0:1, OFF["vs"]:OFF["vs"] + 2048].bitcast(BF16)
    binc_s = view("binc", [1], F32, parts=64)
    b2c_s = view("b2c", [1], F32, parts=64)
    b1T_s = view("b1T", [8], F32)
    bnd_s = view("bnd", [1], F32, parts=1)
    bndb_s = view("bndb", [1], BF16, parts=1)

    ps_a = nc.alloc_psum_tensor("ps_a", [128, 1024], F32).ap()
    ps_b = nc.alloc_psum_tensor("ps_b", [128, 1024], F32).ap()
    ps_c = nc.alloc_psum_tensor("ps_c", [128, 512], F32).ap()
    ps_d = nc.alloc_psum_tensor("ps_d", [128, 512], F32).ap()
    ps_e = nc.alloc_psum_tensor("ps_e", [128, 512], F32).ap()
    ps_f = nc.alloc_psum_tensor("ps_f", [128, 512], F32).ap()
    sc_slot = [ps_a, ps_b]
    ps_eb = ps_e.bitcast(BF16)
    ps_fb = ps_f.bitcast(BF16)

    p = Prog()

    for i in range(NT):
        p.emit("sync", lambda e, i=i: e.dma_start(x_s[:, i, :], x_tiles_h[i]),
               inc=(f"sx{i // 4}", 16))
    for dst, srch in ((bqkvT, bqkvT_h), (bo_s, bo_h), (binc_s, binc_h),
                      (b2c_s, b2c_h), (b1T_s, b1T_h), (onesb, onesb_h),
                      (identb, identb_h), (sel2, sel2_h)):
        p.emit("sync", lambda e, d=dst, s=srch: e.dma_start(d[:, :], s),
               inc=("swB", 16))
    swB_done = p.val("swB")
    p.emit("gpsimd", lambda e: e.dma_start(bvbf[:, :], bvTp_h), inc=("swA", 16))

    def emit_wqkv_dma(g, extra=()):
        for c in range(ND):
            src = wq_view3[128 * c:128 * c + 128, :, 256 * g:256 * g + 256]
            dst = wqkv[:, c, :].rearrange("p (r e) -> p r e", r=3)
            p.emit("gpsimd", lambda e, s=src, d=dst: e.dma_start(d, s),
                   waits=list(extra) if c == 0 else (), inc=(f"swq{g}", 16))
        return (f"swq{g}", p.val(f"swq{g}"))

    def emit_woT_dma(g, extra=()):
        for j, c in enumerate((2 * g, 2 * g + 1)):
            p.emit("gpsimd", lambda e, g=g, j=j, c=c: e.dma_start(
                woTs[:, g % 2, j, :], woTp_h[128 * c:128 * c + 128, :]),
                waits=list(extra) if j == 0 else (), inc=(f"swo{g}", 16))
        return (f"swo{g}", p.val(f"swo{g}"))

    wqkv_done = {0: emit_wqkv_dma(0)}
    p.emit("vector", lambda e: e.tensor_scalar_mul(bq32[:, :], bqkvT[:, 0:8], 1.0 / 32.0),
           waits=[("swB", swB_done)], inc=("smisc", 1))

    sq_psum = [ps_a, ps_b]

    def emit_ln(is_ln2, tile_waits, tr_banks=None, dma_tr=False):
        for i in range(NT):
            p.emit("scalar", lambda e, i=i: e.activation(
                sq_psum[i % 2][:, :], x_s[:, i, :], ACTF.Square,
                accum_out=ssq[:, i:i + 1]), waits=tile_waits(i), inc=("ssq1", 1))
        for i in range(NT):
            if i % 2 == 0:
                p.emit("vector", lambda e, i=i: e.reduce_sum(
                    ssum[:, i:i + 1], x_s[:, i, :], axis=AX),
                    waits=tile_waits(i), inc=("ssq1", 1))
            else:
                p.emit("scalar", lambda e, i=i: e.activation(
                    sq_psum[i % 2][:, :], x_s[:, i, :], ACTF.Copy,
                    accum_out=ssum[:, i:i + 1]), waits=tile_waits(i), inc=("ssq1", 1))
        p.emit("vector", lambda e: e.tensor_scalar_mul(
            sm[:, :], ssum[:, :], 1.0 / 1024.0),
            waits=[("ssq1", p.val("ssq1"))], inc=("sln", 1))
        p.emit("vector", lambda e: e.tensor_scalar(
            sv[:, :], ssq[:, :], 1.0 / 1024.0, EPS, ALU.mult, ALU.add),
            waits=[("ssq1", p.val("ssq1"))], inc=("sln", 1))
        p.emit("vector", lambda e: e.tensor_mul(smq[:, :], sm[:, :], sm[:, :]),
               waits=[("sln", p.val("sln"))], inc=("sln", 1))
        p.emit("vector", lambda e: e.tensor_sub(sv[:, :], sv[:, :], smq[:, :]),
               waits=[("sln", p.val("sln"))], inc=("sln", 1))
        p.emit("scalar", lambda e: e.activation(sv[:, :], sv[:, :], ACTF.Ln),
               waits=[("sln", p.val("sln"))], inc=("ssq2", 1))
        p.emit("scalar", lambda e: e.activation(sr[:, :], sv[:, :], ACTF.Exp, scale=-0.5),
               waits=[("ssq2", p.val("ssq2"))], inc=("ssq2", 1))
        p.emit("vector", lambda e: e.engine_nop(),
               waits=[("ssq2", p.val("ssq2"))], inc=("sln", 1))
        if not is_ln2:
            p.emit("vector", lambda e: e.tensor_scalar(
                sp1[:, :], sr[:, :], 1.0, None, ALU.add),
                waits=[("sln", p.val("sln"))], inc=("sln", 1))
            p.emit("vector", lambda e: e.tensor_mul(smr[:, :], sm[:, :], sr[:, :]),
                   waits=[("sln", p.val("sln"))], inc=("sln", 1))
        stats_done = p.val("sln")
        tr_after = {}
        tr_sem = "strd" if dma_tr else "str"
        for i in range(NT):
            aw = [("sln", stats_done)]
            if i >= 2:
                aw.append((tr_sem, tr_after[i - 2]))
            p.emit("vector", lambda e, i=i: e.tensor_scalar(
                xlnsc[:, i % 2, :], x_s[:, i, :], sm[:, i:i + 1], sr[:, i:i + 1],
                ALU.subtract, ALU.mult), waits=aw, inc=("sln", 1))
            lnsc_done = p.val("sln")
            if not is_ln2:
                p.emit("vector", lambda e, i=i: e.tensor_scalar(
                    x_s[:, i, :], x_s[:, i, :], sp1[:, i:i + 1], smr[:, i:i + 1],
                    ALU.mult, ALU.subtract), inc=("sln", 1))
            if dma_tr:
                for c in range(ND):
                    qn = "scalar" if c % 2 else "sync"
                    p.emit(qn, lambda e, i=i, c=c: e.dma_start(
                        xlnT[:, c, 128 * i:128 * (i + 1)],
                        xlnsc[:, i % 2, 128 * c:128 * (c + 1)], transpose=True),
                        waits=[("sln", lnsc_done)] if c < 2 else (),
                        inc=("strd", 16))
                tr_after[i] = p.val("strd")
                rec = ("strd", p.val("strd"))
            else:
                nb = len(tr_banks)
                pu = tr_banks[i % nb]
                trw = [("sln", lnsc_done)]
                if i >= nb:
                    trw.append(("strev", p.val("strev") - (nb - 1)))
                for c in range(ND):
                    p.emit("tensor", lambda e, i=i, c=c, pu=pu: e.transpose(
                        pu[:, 128 * c:128 * (c + 1)],
                        xlnsc[:, i % 2, 128 * c:128 * (c + 1)],
                        identb[:, :]), waits=trw if c == 0 else (), inc=("str", 1))
                tr_after[i] = p.val("str")
                p.emit("vector", lambda e, i=i, pu=pu: e.tensor_copy(
                    xlnT[:, :, 128 * i:128 * (i + 1)],
                    pu[:, 0:1024].rearrange("p (c t) -> p c t", c=8)),
                    waits=[("str", p.val("str"))], inc=("strev", 1))
                rec = ("strev", p.val("strev"))
            tr_after.setdefault(i, None)
            if not dma_tr:
                tr_after[i] = p.val("str")
            yield_rec = rec
            tile_recs.append(yield_rec)

    tile_recs = []
    ps_ab = ps_a.bitcast(BF16)
    ps_bb = ps_b.bitcast(BF16)
    tile_recs.clear()
    emit_ln(False, lambda i: [(f"sx{i // 4}", 64), ("swB", swB_done)],
            tr_banks=[ps_ab, ps_bb])
    ln1_tr = {i: tile_recs[i][1] for i in range(NT)}
    ln1_strev = p.val("strev")

    vdone = {}

    def qkv_chunks(g, banks=None):
        banks = banks or [ps_f]
        nb = len(banks)
        qkT = qkTb[:, g % 2]
        vs = vsb[:, g % 2]

        def mk_qk(d, ts_):
            def emit():
                u = banks[p.val("suu") % nb]
                uw = [("sue", max(0, p.val("sue") - (nb - 1))),
                      wqkv_done[g],
                      ("strev", ln1_tr[4 * ts_ + 3])]
                r = 0 if d < 2 else 1
                sub = d % 2
                for c in range(ND):
                    lhsT = wqkv[:, c, :].rearrange(
                        "p (r e) -> p r e", r=3)[:, r, 128 * sub:128 * (sub + 1)]
                    rhs = xlnT[:, c, QS * ts_:QS * (ts_ + 1)]
                    p.emit("tensor", lambda e, l=lhsT, r_=rhs, c=c, u=u: e.matmul(
                        u[:, :], l, r_, start=(c == 0), stop=(c == ND - 1)),
                        waits=uw if c == 0 else (),
                        inc=("suu", 1) if c == ND - 1 else None)
                mmd = p.val("suu")
                if d < 2:
                    p.emit("vector", lambda e, d=d, t=ts_, u=u: e.tensor_scalar(
                        qkT[:, d, QS * t:QS * (t + 1)], u[:, :], 1.0 / 32.0,
                        bq32[:, 2 * g + d:2 * g + d + 1], ALU.mult, ALU.add),
                        waits=[("suu", mmd)], inc=("sue", 1))
                else:
                    p.emit("vector", lambda e, d=d, t=ts_, u=u: e.tensor_scalar(
                        qkT[:, d, QS * t:QS * (t + 1)], u[:, :], 1.0,
                        bqkvT[:, 8 + 2 * g + (d - 2):8 + 2 * g + (d - 2) + 1],
                        ALU.mult, ALU.add),
                        waits=[("suu", mmd)], inc=("sue", 1))
            return emit

        def mk_v(i):
            def emit():
                u = banks[p.val("suu") % nb]
                uw = [("sue", max(0, p.val("sue") - (nb - 1))),
                      wqkv_done[g],
                      ("strev", ln1_tr[i])]
                for c in range(ND):
                    lhsT = xlnT[:, c, 128 * i:128 * (i + 1)]
                    rhs = wqkv[:, c, :].rearrange("p (r e) -> p r e", r=3)[:, 2, :]
                    p.emit("tensor", lambda e, l=lhsT, r_=rhs, c=c, u=u: e.matmul(
                        u[:, 0:256], l, r_, start=(c == 0), stop=(c == ND - 1)),
                        waits=uw if c == 0 else (),
                        inc=("suu", 1) if c == ND - 1 else None)
                mmd = p.val("suu")
                p.emit("vector", lambda e, i=i, u=u: e.tensor_copy(
                    vs[:, i, :], u[:, 0:256]),
                    waits=[("suu", mmd)], inc=("sue", 1))
                vdone[(g, i)] = p.val("sue")
            return emit

        qk = [(lambda: True, mk_qk(d, ts_)) for d in range(4) for ts_ in range(NQ)]
        vv = [(lambda: True, mk_v(i)) for i in range(NT)]
        return qk, vv

    def bound_chunks(g, banks=None):
        banks = banks or [ps_f]
        nb = len(banks)
        qkT = qkTb[:, g % 2]
        chunks = []

        def mk_sq(ci):
            def emit():
                p.emit("gpsimd", lambda e, ci=ci: e.tensor_tensor(
                    sqb[:, :], qkT[:, ci, :], qkT[:, ci, :], ALU.mult),
                    waits=[("sue", p.val("sue")), ("sgq", p.val("sgq")),
                           ("sln", p.val("sln"))],
                    inc=("sgq", 1))
            return emit

        def mk_mm(ci, j):
            def emit():
                u = banks[p.val("suu") % nb]
                w = [("sgq", p.val("sgq")),
                     ("sue", max(0, p.val("sue") - (nb - 1))),
                     ("sbv", p.val("sbv"))]
                p.emit("tensor", lambda e, j=j, u=u: e.matmul(
                    u[0:2, :], sel2[:, :], sqb[:, 512 * j:512 * (j + 1)],
                    start=True, stop=True), waits=w, inc=("suu", 1))
                p.emit("vector", lambda e, ci=ci, j=j, u=u: e.reduce_max(
                    n16[0:2, 4 * ci + j:4 * ci + j + 1], u[0:2, :], axis=AX),
                    waits=[("suu", p.val("suu"))], inc=("sbv", 1))
                p.emit("vector", lambda e: e.engine_nop(),
                       waits=[("sbv", p.val("sbv"))], inc=("sue", 1))
            return emit

        def mk_fin():
            def emit():
                p.emit("vector", lambda e: e.reduce_max(
                    nqk[0:2, 0:1], n16[0:2, 0:8], axis=AX),
                    waits=[("sbv", p.val("sbv"))], inc=("sbv", 1))
                p.emit("vector", lambda e: e.reduce_max(
                    nqk[0:2, 1:2], n16[0:2, 8:16], axis=AX),
                    waits=[("sbv", p.val("sbv"))], inc=("sbv", 1))
                shmask = [1, 0] + list(range(2, 32))
                p.emit("vector", lambda e, m=shmask: e.stream_shuffle(
                    nqk[0:2, 2:4], nqk[0:2, 0:2], m),
                    waits=[("sbv", p.val("sbv"))], inc=("sbv", 1))
                p.emit("vector", lambda e: e.tensor_max(
                    nqk[0:1, 0:2], nqk[0:1, 0:2], nqk[0:1, 2:4]),
                    waits=[("sbv", p.val("sbv"))], inc=("sbv", 1))
                p.emit("vector", lambda e: e.tensor_mul(
                    bnd_s[:, :], nqk[0:1, 0:1], nqk[0:1, 1:2]),
                    waits=[("sbv", p.val("sbv"))], inc=("sbv", 1))
                p.emit("scalar", lambda e: e.activation(
                    bnd_s[:, :], bnd_s[:, :], ACTF.Ln),
                    waits=[("sbv", p.val("sbv"))], inc=("sbs", 1))
                p.emit("scalar", lambda e: e.activation(
                    bnd_s[:, :], bnd_s[:, :], ACTF.Exp, scale=0.5),
                    waits=[("sbs", p.val("sbs"))], inc=("sbs", 1))
                p.emit("vector", lambda e: e.tensor_scalar(
                    bnd_s[:, :], bnd_s[:, :], -1.0, BSHIFT, ALU.mult, ALU.add),
                    waits=[("sbs", p.val("sbs"))], inc=("sbv", 1))
                p.emit("vector", lambda e: e.tensor_copy(bndb_s[:, :], bnd_s[:, :]),
                       waits=[("sbv", p.val("sbv"))], inc=("sbv", 1))
                u = banks[p.val("suu") % nb]
                uw = [("sbv", p.val("sbv")),
                      ("sue", max(0, p.val("sue") - (nb - 1)))]
                p.emit("tensor", lambda e, u=u: e.matmul(
                    u[:, 0:1], onesb[0:1, :], bndb_s[:, :],
                    start=True, stop=True), waits=uw, inc=("suu", 1))
                p.emit("vector", lambda e, g=g, u=u: e.tensor_copy(
                    biasg[:, g:g + 1], u[:, 0:1]),
                    waits=[("suu", p.val("suu"))], inc=("sue", 1))
                p.emit("vector", lambda e: e.engine_nop(),
                       waits=[("sue", p.val("sue"))], inc=("sbnd", 1))
            return emit

        for ci in range(4):
            chunks.append((lambda: True, mk_sq(ci)))
            for j in range(4):
                chunks.append((lambda: True, mk_mm(ci, j)))
        chunks.append((lambda: True, mk_fin()))
        return chunks

    nrm_done = {}
    opadd = {}
    woT_done = {}

    def partial_chunks(g, with_bias, banks=None):
        banks = banks or [ps_f]
        nb = len(banks)
        use_e = any(b is ps_e for b in banks)
        chunks = []
        hslots = ((2 * g) % 4, (2 * g + 1) % 4)

        def mk(t, half):
            key = g * NQ + t // 4
            def ready():
                return key in nrm_done
            def emit():
                u = banks[p.val("suu") % nb]
                uw = [("sue", max(0, p.val("sue") - (nb - 1))),
                      woT_done[g],
                      ("snrm", nrm_done[key])]
                for j in range(2):
                    p.emit("tensor", lambda e, j=j, t=t, h=half, u=u: e.matmul(
                        u[:, :], heads[:, hslots[j], 128 * t:128 * (t + 1)],
                        woTs[:, g % 2, j, 512 * h:512 * (h + 1)],
                        start=(j == 0), stop=(j == 1 and not with_bias)),
                        waits=uw if j == 0 else (),
                        inc=("suu", 1) if (j == 1 and not with_bias) else None)
                if with_bias:
                    p.emit("tensor", lambda e, t=t, h=half, u=u: e.matmul(
                        u[:, :], onesb[0:1, :], bo2b[:, 512 * h:512 * (h + 1)],
                        start=False, stop=True),
                        waits=[("sbob", 1)], inc=("suu", 1))
                mmd = p.val("suu")
                p.emit("vector", lambda e, t=t, h=half, u=u: e.tensor_add(
                    x_s[:, t, 512 * h:512 * (h + 1)],
                    x_s[:, t, 512 * h:512 * (h + 1)], u[:, :]),
                    waits=[("suu", mmd)], inc=("sue", 1))
                opadd[(g, t, half)] = p.val("sue")
            return (ready, emit)

        def mk_bv():
            def emit():
                for half in range(2):
                    u = banks[p.val("suu") % nb]
                    uw = [("sue", max(0, p.val("sue") - (nb - 1))),
                          woT_done[g]]
                    if use_e:
                        uw.append(("snrm", nrm_done[g * NQ + NQ - 1]))
                    for j in range(2):
                        p.emit("tensor", lambda e, j=j, h=half, u=u: e.matmul(
                            u[0:1, :], bvbf[:, 2 * g + j:2 * g + j + 1],
                            woTs[:, g % 2, j, 512 * h:512 * (h + 1)],
                            start=(j == 0), stop=(j == 1)),
                            waits=uw if j == 0 else (),
                            inc=("suu", 1) if j == 1 else None)
                    p.emit("vector", lambda e, h=half, u=u: e.tensor_add(
                        bo_s[:, 512 * h:512 * (h + 1)],
                        bo_s[:, 512 * h:512 * (h + 1)], u[0:1, :]),
                        waits=[("suu", p.val("suu")), ("sbvv", p.val("sbvv")),
                               ("swB", swB_done)],
                        inc=("sue", 1))
                    p.emit("vector", lambda e: e.engine_nop(),
                           waits=[("sue", p.val("sue"))], inc=("sbvv", 1))
            return (lambda: True, emit)

        chunks.append(mk_bv())
        if with_bias:
            def cast_emit():
                p.emit("vector", lambda e: e.tensor_copy(bo2b[:, :], bo_s[:, :]),
                       waits=[("sbvv", p.val("sbvv"))], inc=("sbob", 1))
            chunks.append((lambda: True, cast_emit))
        for t in range(NT):
            for half in range(2):
                chunks.append(mk(t, half))
        return chunks

    exp_done = {}
    pv_done = {}

    def emit_attn_group(g, side):
        qkT = qkTb[:, g % 2]
        vs = vsb[:, g % 2]
        base_pti = g * NQ * NK * 2
        si = 0
        total_iters = NQ * NK

        def emit_scores_exp(q, k):
            for pr in range(2):
                pti = base_pti + (q * NK + k) * 2 + pr
                slot = pti % 2
                pslot = pti % 4
                w = [("sbnd", g + 1)]
                if pti >= 2:
                    w.append(("sexp", exp_done[pti - 2]))
                if pti == 0:
                    w.append(("ssq1", 16))
                p.emit("tensor", lambda e, s=slot, pr=pr, k=k, q=q, qkT=qkT: e.matmul(
                    sc_slot[s][:, 0:512],
                    qkT[0:64, 2 + pr, 128 * k:128 * (k + 1)],
                    qkT[0:64, pr, QS * q:QS * (q + 1)],
                    start=True, stop=True, tile_position=(0, 0)),
                    waits=w, inc=None)
                p.emit("tensor", lambda e, s=slot, pr=pr, k=k, q=q, qkT=qkT: e.matmul(
                    sc_slot[s][:, 512:1024],
                    qkT[64:128, 2 + pr, 128 * k:128 * (k + 1)],
                    qkT[64:128, pr, QS * q:QS * (q + 1)],
                    start=True, stop=True, tile_position=(64, 0)),
                    inc=("ssc", 1))
                scd = p.val("ssc")
                ew = [("ssc", scd)]
                if pti >= 4:
                    ew.append(("spv", pv_done[pti - 4]))
                p.emit("scalar", lambda e, s=slot, ps_=pslot, g=g: e.activation(
                    pt_s(ps_), sc_slot[s][:, :], ACTF.Exp,
                    bias=biasg[:, g:g + 1], scale=1.0),
                    waits=ew, inc=("sexp", 1))
                exp_done[pti] = p.val("sexp")

        def emit_v_dn(q, k):
            gq = g * NQ + q
            vw = [("sexp", exp_done[base_pti + (q * NK + k) * 2 + 1]),
                  ("sue", vdone[(g, k)])]
            if k == 0 and gq > 0:
                vw.append(("snrm", nrm_done[gq - 1]))
            if k == 0 and gq == 0:
                vw.append(("sue", head_sue))
            sA = (base_pti + (q * NK + k) * 2) % 4
            sB = (sA + 1) % 4
            for half, po in ((0, ps_c), (1, ps_d)):
                for a in range(4):
                    ps_slot = sA if a < 2 else sB
                    p.emit("tensor", lambda e, po=po, a=a, k=k, half=half, s=ps_slot, vs=vs: e.matmul(
                        po[32 * a:32 * (a + 1), :],
                        vs[:, k, 128 * half + 32 * a:128 * half + 32 * (a + 1)],
                        pt_s(s)[:, 512 * (a % 2):512 * (a % 2) + 512],
                        start=(k == 0), stop=(k == NK - 1),
                        tile_position=(0, 32 * a)),
                        waits=vw if (half == 0 and a == 0) else (), inc=None)
            for a in range(4):
                ps_slot = sA if a < 2 else sB
                p.emit("tensor", lambda e, a=a, k=k, s=ps_slot: e.matmul(
                    ps_e[32 * a:32 * (a + 1), :],
                    onesb[:, 0:32],
                    pt_s(s)[:, 512 * (a % 2):512 * (a % 2) + 512],
                    start=(k == 0), stop=(k == NK - 1),
                    tile_position=(0, 32 * a)),
                    inc=("spv", 1) if a == 3 else None)
            pv_done[base_pti + (q * NK + k) * 2] = p.val("spv")
            pv_done[base_pti + (q * NK + k) * 2 + 1] = p.val("spv")

        def emit_norm(q):
            gq = g * NQ + q
            lastpv = pv_done[base_pti + (q * NK + NK - 1) * 2]
            p.emit("vector", lambda e: e.reciprocal(rd[:, :], ps_e[:, :]),
                   waits=[("spv", lastpv)], inc=("snv", 1))
            p.emit("vector", lambda e, g=g, q=q: e.tensor_mul(
                heads[:, (2 * g) % 4, QS * q:QS * (q + 1)], ps_c[:, :], rd[:, :]),
                waits=[("snv", p.val("snv"))], inc=("snv", 1))
            p.emit("vector", lambda e, g=g, q=q: e.tensor_mul(
                heads[:, (2 * g + 1) % 4, QS * q:QS * (q + 1)], ps_d[:, :], rd[:, :]),
                inc=("snrm", 1))
            nrm_done[gq] = p.val("snrm")

        steps = [(q, k) for q in range(NQ) for k in range(NK)]
        for idx in range(total_iters + 1):
            if idx < total_iters:
                emit_scores_exp(*steps[idx])
            if idx >= 1:
                qp, kp = steps[idx - 1]
                emit_v_dn(qp, kp)
                if kp == NK - 1:
                    emit_norm(qp)
            target = ((idx + 1) * len(side) + total_iters) // (total_iters + 1)
            while si < min(len(side), target) and side[si][0]():
                side[si][1]()
                si += 1
        while si < len(side):
            assert side[si][0]()
            side[si][1]()
            si += 1

    HEAD2B = bool(int(os.environ.get("MK_HEAD2B", "1")))
    hqk, hv = qkv_chunks(0, banks=[ps_f, ps_e] if HEAD2B else [ps_f])
    for _, ch in hqk:
        ch()
    for _, ch in bound_chunks(0, banks=[ps_f, ps_e] if HEAD2B else [ps_f]):
        ch()
    for _, ch in hv:
        ch()
    head_sue = p.val("sue")

    for g in range(NG):
        side = []
        if g + 1 < NG:
            wqkv_done[g + 1] = emit_wqkv_dma(g + 1, extra=[("suu", p.val("suu"))])
        if g < NG - 1:
            woT_done[g] = emit_woT_dma(
                g, extra=[("sue", opadd.get((g - 2, NT - 1, 1), 0))])
        if g >= 1:
            side += partial_chunks(g - 1, with_bias=False)
        if g + 1 < NG:
            qk, vv = qkv_chunks(g + 1)
            side += qk
            side += bound_chunks(g + 1)
            side += vv
        if g == NG - 1:
            woT_done[3] = emit_woT_dma(
                3, extra=[("sue", opadd.get((1, NT - 1, 1), 0))])
            side += partial_chunks(3, with_bias=True)
        emit_attn_group(g, side)
    attn_done = p.val("snrm")

    p.emit("gpsimd", lambda e: e.dma_start(
        win_s[:, :, :], win_h.rearrange("(c p) m -> p c m", p=128)),
        waits=[("spv", p.val("spv"))], inc=("swA", 16))
    p.emit("gpsimd", lambda e: e.dma_start(w1blk_s[:, :], w1blk_h), inc=("swA", 16))
    p.emit("gpsimd", lambda e: e.dma_start(
        w2blk_s[:, :, :], w2blk_h.rearrange("(c p) m -> p c m", p=128)), inc=("swA", 16))
    p.emit("gpsimd", lambda e: e.dma_start(woutA_s[:, :], woutA_h), inc=("swA", 16))
    swA_ffn = p.val("swA")
    op_all = p.val("sue")

    def ln2_waits(i):
        return [("sue", opadd[(3, i, 1)])]

    tile_recs.clear()
    emit_ln(True, ln2_waits, tr_banks=[ps_eb, ps_fb])
    ln2_tr = {i: ("strev", tile_recs[i][1]) for i in range(NT)}

    p.emit("vector", lambda e: e.memset(gT_full[64:65, :], 1.0),
           waits=[("snrm", attn_done)], inc=("sgone", 1))

    for s in range(NQ):
        sl = slice(QS * s, QS * (s + 1))
        for c in range(ND):
            p.emit("tensor", lambda e, c=c, sl=sl: e.matmul(
                ps_c[0:64, :], win_s[:, c, :], xlnT[:, c, sl],
                start=(c == 0), stop=(c == ND - 1)),
                waits=[ln2_tr[4 * s + 3], ("swA", swA_ffn),
                       ("snv", p.val("snv"))] if c == 0 else (),
                inc=("sff", 1) if c == ND - 1 else None)
        p.emit("vector", lambda e, sl=sl: e.tensor_scalar(
            sT[:, sl], ps_c[0:64, :], 1.0, binc_s[:, :], ALU.mult, ALU.add),
            waits=[("sff", p.val("sff"))], inc=("sfe", 1))
        st_done = p.val("sfe")
        for c in range(ND):
            ph = [ps_a, ps_b][c % 2]
            phv = ph[:, 512 * ((c // 2) % 2):512 * ((c // 2) % 2) + 512]
            w = [("sfe", st_done)]
            if c >= 4:
                w.append(("sfe", p.val("sfe") - 3))
            p.emit("tensor", lambda e, c=c, sl=sl, phv=phv: e.matmul(
                phv, w1blk_s[:, 128 * c:128 * (c + 1)], sT[:, sl],
                start=True, stop=True), waits=w, inc=("sff", 1))
            p.emit("vector", lambda e, c=c, phv=phv: e.tensor_scalar(
                hT[:, c, :], phv, b1T_s[:, c:c + 1], 0.0, ALU.add, ALU.max),
                waits=[("sff", p.val("sff"))], inc=("sfe", 1))
        for c in range(ND):
            p.emit("tensor", lambda e, c=c: e.matmul(
                ps_d[0:64, :], w2blk_s[:, c, :], hT[:, c, :],
                start=(c == 0), stop=(c == ND - 1)),
                waits=[("sfe", p.val("sfe"))] if c == 0 else (),
                inc=("sff", 1) if c == ND - 1 else None)
        p.emit("vector", lambda e, sl=sl: e.tensor_scalar(
            gT_full[0:64, sl], ps_d[0:64, :], 1.0, b2c_s[:, :], ALU.mult, ALU.add),
            waits=[("sff", p.val("sff"))], inc=("sfe", 1))
        gt_done = p.val("sfe")
        wout_slots = [ps_a[:, 0:512], ps_a[:, 512:1024],
                      ps_b[:, 0:512], ps_b[:, 512:1024]]
        for tt in range(4):
            i = 4 * s + tt
            for half in range(2):
                n = 2 * i + half
                u = wout_slots[n % 4]
                w = [("sfe", gt_done), ("sgone", 1), ("sue", op_all),
                     ("ssq1", p.val("ssq1"))]
                if p.val("sfev2") >= 3:
                    w.append(("sfev2", p.val("sfev2") - 3))
                p.emit("tensor", lambda e, i=i, h=half, u=u: e.matmul(
                    u, gT_full[:, 128 * i:128 * (i + 1)],
                    woutA_s[:, 512 * h:512 * (h + 1)],
                    start=True, stop=True), waits=w, inc=("sff", 1))
                p.emit("vector", lambda e, i=i, h=half, u=u: e.tensor_add(
                    x_s[:, i, 512 * h:512 * (h + 1)],
                    x_s[:, i, 512 * h:512 * (h + 1)], u),
                    waits=[("sff", p.val("sff"))], inc=("sfev2", 1))
            p.emit("sync", lambda e, i=i: e.dma_start(out_tiles_h[i], x_s[:, i, :]),
                   waits=[("sfev2", p.val("sfev2"))], inc=("sout", 16))

    p.emit("sync", lambda e: None, waits=[("sout", 16 * NT)])

    semh = {}
    from contextlib import ExitStack
    ctx = ExitStack()
    for name in p.counts:
        semh[name] = ctx.enter_context(nc.semaphore(name))
    block = ctx.enter_context(nc.Block())
    with ctx:
        def runner(eng_obj, name):
            for fn, waits, inc in p.ops[name]:
                for s, v in waits:
                    if v > 0:
                        eng_obj.wait_ge(semh[s], v)
                ins = fn(eng_obj)
                if inc is not None:
                    s, amt = inc
                    ins.then_inc(semh[s], amt)

        @block.tensor
        def _(t):
            runner(t, "tensor")

        @block.scalar
        def _(s):
            runner(s, "scalar")

        @block.vector
        def _(v):
            runner(v, "vector")

        @block.sync
        def _(s):
            runner(s, "sync")

        @block.gpsimd
        def _(g):
            runner(g, "gpsimd")

    return nc


_NC_CACHE = {}


def _get_nc():
    key = "v1"
    if key not in _NC_CACHE:
        _NC_CACHE[key] = _build_nc()
    return _NC_CACHE[key]


def _prep_inputs(inputs):
    f32 = np.float32
    x = np.ascontiguousarray(np.asarray(inputs["L0_attn_x"], dtype=f32))
    w_qkv = np.asarray(inputs["L0_attn_w_qkv"], dtype=f32)
    b_qkv = np.asarray(inputs["L0_attn_b_qkv"], dtype=f32)
    w_o = np.asarray(inputs["L0_attn_w_o"], dtype=f32)
    b_o = np.asarray(inputs["L0_attn_b_o"], dtype=f32)
    Win = np.asarray(inputs["L0_ffn_Win"], dtype=f32)
    bin_ = np.asarray(inputs["L0_ffn_bin_"], dtype=f32)
    W1 = np.asarray(inputs["L0_ffn_W1"], dtype=f32)
    b1 = np.asarray(inputs["L0_ffn_b1"], dtype=f32)
    W2 = np.asarray(inputs["L0_ffn_W2"], dtype=f32)
    b2 = np.asarray(inputs["L0_ffn_b2"], dtype=f32)
    Wout = np.asarray(inputs["L0_ffn_Wout"], dtype=f32)
    bout = np.asarray(inputs["L0_ffn_bout"], dtype=f32)

    d_idx = np.arange(1024)
    h = d_idx // 64
    c = d_idx % 64
    newpos = (h // 4) * 256 + (c // 32) * 128 + (h % 4) * 32 + (c % 32)
    src = np.empty(1024, dtype=np.int64)
    src[newpos] = d_idx

    wqkvT = np.ascontiguousarray(w_qkv.T)
    wqkvT_p = wqkvT.copy()
    wqkvT_p[:, 2048:3072] = wqkvT[:, 2048 + src]
    bqkvT = np.ascontiguousarray(b_qkv[:2048].reshape(16, 128).T)
    bvTp = np.ascontiguousarray(b_qkv[2048:][src].reshape(8, 128).T)
    woTp = np.ascontiguousarray(w_o.T[src, :])
    bo = np.ascontiguousarray(b_o.reshape(1, 1024))

    w1blk = np.zeros((64, 1024), dtype=f32)
    mm = np.arange(64)
    for j in range(16):
        w1blk[mm, 16 * mm + j] = W1[:, j]
    b1T = np.ascontiguousarray(b1.reshape(1024).reshape(8, 128).T)
    w2blk = np.zeros((1024, 64), dtype=f32)
    for j in range(16):
        w2blk[16 * mm + j, mm] = W2[:, j]
    woutA = np.ascontiguousarray(np.vstack([Wout, bout.reshape(1, 1024)]))
    binc = np.ascontiguousarray(bin_.reshape(64, 1))
    b2c = np.ascontiguousarray(b2.reshape(64, 1))

    onesb = np.ones((128, 128), dtype=ml_dtypes.bfloat16)
    identb = np.eye(128, dtype=ml_dtypes.bfloat16)
    sel2 = np.zeros((128, 2), dtype=ml_dtypes.bfloat16)
    sel2[0:64, 0] = 1.0
    sel2[64:128, 1] = 1.0

    shared = dict(wqkvT=wqkvT_p, bqkvT=bqkvT, bvTp=bvTp, woTp=woTp, bo=bo,
                  win=np.ascontiguousarray(Win), binc=binc, w1blk=w1blk,
                  b1T=b1T, w2blk=w2blk, b2c=b2c, woutA=woutA,
                  onesb=onesb, identb=identb, sel2=sel2)
    in_maps = []
    for cidx in range(8):
        m = dict(shared)
        m["x"] = np.ascontiguousarray(x[cidx])
        in_maps.append(m)
    return in_maps


def kernel(**inputs):
    in_maps = _prep_inputs(inputs)
    nc = _get_nc()
    res = run_bass_kernel_spmd(nc, in_maps, core_ids=list(range(8)))
    out = np.stack([np.asarray(res.results[cidx]["out"]) for cidx in range(8)])
    return out.astype(np.float32)
